# version 02
# baseline (speedup 1.0000x reference)
import numpy as np
import ml_dtypes
import concourse.bass as bass
import concourse.mybir as mybir
from concourse.bass_utils import run_bass_kernel_spmd

F32 = mybir.dt.float32
BF16 = mybir.dt.bfloat16
I32 = mybir.dt.int32
AF = mybir.ActivationFunctionType
ALU = mybir.AluOpType
AX = mybir.AxisListType

S = 2048
D = 2048
TOK = 1024
BLK = 512
EPS = 1e-5
NCH = D // 128

OWN_BLOCKS = [[0, 3], [1, 2]]
SCHED = [[0, 2], [0, 1, 2, 3]]
MASKED_POS = [(0, 0), (0, 2), (1, 1), (1, 3)]

BF = ml_dtypes.bfloat16
ENGS = ("pe", "act", "dve", "pool", "sp")
AW = 55296


def _cf(x):
    return np.ascontiguousarray(x, dtype=np.float32)


def _cbf(x):
    return np.ascontiguousarray(np.asarray(x, dtype=np.float32).astype(BF))


def own_positions(parity):
    return np.concatenate([np.arange(b * BLK, (b + 1) * BLK) for b in OWN_BLOCKS[parity]])


def slot_blocks(parity):
    own, par = OWN_BLOCKS[parity], OWN_BLOCKS[1 - parity]
    return [own[0], own[1], par[0], par[1]]


def rope_tables(style, half, positions, d_full=None):
    p = positions.astype(np.float64)
    if style == "interleaved":
        inv = 10000.0 ** (-np.arange(half, dtype=np.float64) / half)
    elif style == "l1":
        inv = 10000.0 ** (-2.0 * np.arange(half, dtype=np.float64) / d_full)
    elif style == "l3":
        inv = 10000.0 ** (-np.arange(half, dtype=np.float64) / half)
    ang = inv[:, None] * p[None, :]
    return np.cos(ang), np.sin(ang)


def build_masks(parity):
    sb = slot_blocks(parity)
    own = OWN_BLOCKS[parity]
    tiles = []
    for qc, sl in MASKED_POS:
        qblk = own[qc]
        kblk = sb[sl]
        for sub in range(4):
            k_abs = kblk * BLK + sub * 128 + np.arange(128)
            q_abs = qblk * BLK + np.arange(BLK)
            tiles.append((k_abs[:, None] <= q_abs[None, :]).astype(np.float32))
    return np.stack(tiles)


class Em:
    def __init__(self):
        self.code = {e: [] for e in ENGS}
        self.cnt = {}
        self.step = {}
        self.last_wait = {}
        self.sem_objs = {}
        self.regs = {}

    def sem(self, name, step=1):
        self.cnt[name] = 0
        self.step[name] = step
        return name

    def op(self, eng, fn, sig=None, waits=()):
        for w in waits:
            if w is not None:
                self.wait(eng, w[0], w[1])
        if sig is not None:
            self.cnt[sig] += 1

        def closure(e, fn=fn, sig=sig):
            inst = fn(e)
            if sig is not None:
                inst.then_inc(self.sem_objs[sig], self.step[sig])
        self.code[eng].append(closure)
        if sig in ("dmaw", "dmas"):
            self.wait(eng, sig, self.cnt[sig])
        return self.tok(sig) if sig is not None else None

    def raw(self, eng, fn):
        self.code[eng].append(fn)

    def wait(self, eng, name, count):
        assert count <= self.cnt[name], (name, count, self.cnt[name])
        key = (eng, name)
        if self.last_wait.get(key, 0) >= count:
            return
        self.last_wait[key] = count
        step = self.step[name]

        def fn(e, name=name, count=count, step=step):
            e.wait_ge(self.sem_objs[name], count * step)
        self.code[eng].append(fn)

    def tok(self, name):
        return (name, self.cnt[name])


class Buf:

    def __init__(self, h, W, col0, cols, row0=0, rows=128):
        self.h, self.W, self.col0, self.cols = h, W, col0, cols
        self.row0, self.rows = row0, rows

    def __getitem__(self, key):
        r, c = key
        r0 = self.row0 + (r.start or 0)
        r1 = self.row0 + (r.stop if r.stop is not None else self.rows)
        c0 = self.col0 + (c.start or 0)
        c1 = self.col0 + (c.stop if c.stop is not None else self.cols)
        assert c1 <= self.col0 + self.cols and r1 <= self.row0 + self.rows
        return self.h[r0:r1, c0:c1]

    def at(self, coff, pattern):
        return bass.AP(self.h, self.row0 * 0 + self.col0 + coff, pattern)


class Ctx:
    def __init__(self, layers):
        self.layers = layers
        self.nc = bass.Bass("TRN2", target_bir_lowering=False)
        self.E = Em()
        self.sb = {}
        self.n_flags = 0
        self.inputs_ap = {}

    def din(self, name, shape, dtype):
        ap = self.nc.dram_tensor(name, list(shape), dtype, kind="ExternalInput").ap()
        self.inputs_ap[name] = ap
        return ap

    def dout(self, name, shape, dtype):
        return self.nc.dram_tensor(name, list(shape), dtype, kind="ExternalOutput").ap()

    def sbuf(self, name, shape, dtype):
        h = self.nc.alloc_sbuf_tensor(name, list(shape), dtype)
        self.sb[name] = h
        return h

    def region(self, col0, cols, row0=0, rows=128):
        assert col0 + cols <= AW
        return Buf(self.arena, AW, col0, cols, row0, rows)

    def flag(self):
        i = self.n_flags
        self.n_flags += 2
        return i - 0 if False else (self.n_flags - 2)

    def take_bank(self):
        i = self.bank_rr
        self.bank_rr = (self.bank_rr + 1) % 4
        return self.banks[i], self.bank_free[i], i

    def release_bank(self, i, tok):
        self.bank_free[i] = tok


def emit_ln_to_fm(C, xhat, waits=()):
    E = C.E
    x_tok = C.sb["x_tok"]
    stf = C.sb["stats"]
    xt = C.scr_tok
    ptr = C.ptr
    last_evac = None
    slot_act = [None, None]
    prev_pe = None
    t_res_done = list(waits) + [E.tok("dve")]
    xt_guard = E.tok("dve")
    for c in range(8):
        sb = (c % 2) * 32
        st = stf[:, sb:sb + 32]
        tile = x_tok[:, c * D:(c + 1) * D]
        w = list(t_res_done) + [slot_act[c % 2]]
        t1 = E.op("dve", lambda v, tile=tile, st=st: [
            v.bn_stats(st[:, 8 + 6 * j:14 + 6 * j], tile[:, j * 512:(j + 1) * 512])
            for j in range(4)][-1], sig="dve", waits=w)
        t2 = E.op("dve", lambda v, st=st: v.bn_aggr(st[:, 0:2], st[:, 8:32]),
                  sig="dve", waits=[t1])
        t3 = E.op("dve", lambda v, st=st: (
            v.tensor_scalar(st[:, 2:3], st[:, 0:1], -1.0, None, ALU.mult),
            v.tensor_scalar(st[:, 3:4], st[:, 1:2], EPS, None, ALU.add))[-1],
            sig="dve", waits=[t2])
        t_dve = E.op("dve", lambda v, st=st: v.reciprocal(st[:, 4:5], st[:, 3:4]),
                     sig="dve", waits=[t3])
        t5 = E.op("act", lambda a, st=st: a.activation(st[:, 5:6], st[:, 4:5], AF.Ln),
                  sig="act", waits=[t_dve, prev_pe])
        t6 = E.op("act", lambda a, st=st: a.activation(st[:, 5:6], st[:, 5:6], AF.Exp, scale=0.5),
                  sig="act", waits=[t5])
        t7 = E.op("act", lambda a, st=st: a.activation(st[:, 6:7], st[:, 2:3], AF.Identity, scale=st[:, 5:6]),
                  sig="act", waits=[t6])
        t_act = E.op("act", lambda a, tile=tile, st=st, xt=xt: a.activation(
            xt[:, 0:D], tile, AF.Identity, bias=st[:, 6:7], scale=st[:, 5:6]),
            sig="act", waits=[t7, xt_guard])
        slot_act[c % 2] = t_act
        for g in range(4):
            w2 = [t_act, C.ptr_free] + ([last_evac] if last_evac else [])
            t_pe = E.op("pe", lambda t, g=g, xt=xt, ptr=ptr: [
                t.transpose(ptr[:, j * 128:(j + 1) * 128],
                            xt[0:128, (g * 4 + j) * 128:(g * 4 + j + 1) * 128],
                            C.sb["ident"][:, :]) for j in range(4)][-1],
                sig="pe", waits=w2)
            last_evac = E.op("dve", lambda v, g=g, c=c, ptr=ptr, xhat=xhat: v.tensor_copy(
                xhat.at((g * 4) * TOK + c * 128, [[AW, 128], [TOK, 4], [1, 128]]),
                ptr[:, 0:512].rearrange("p (a b) -> p a b", a=4)),
                sig="dve", waits=[t_pe])
            C.ptr_free = last_evac
            prev_pe = t_pe
    C.lastln_pe = prev_pe
    return last_evac


def emit_ln2_tile(C, c, ysrc_ap, dep, slot_act=None):
    E = C.E
    x_tok = C.sb["x_tok"]
    sb = (c % 2) * 32
    st = C.sb["stats"][:, sb:sb + 32]
    w = [dep] + ([slot_act[c % 2]] if slot_act else [])
    t1 = E.op("dve", lambda v, ysrc=ysrc_ap, st=st: [
        v.bn_stats(st[:, 8 + 6 * j:14 + 6 * j], ysrc[:, j * 512:(j + 1) * 512])
        for j in range(4)][-1], sig="dve", waits=w)
    t2 = E.op("dve", lambda v, st=st: v.bn_aggr(st[:, 0:2], st[:, 8:32]),
              sig="dve", waits=[t1])
    t3 = E.op("dve", lambda v, st=st: (
        v.tensor_scalar(st[:, 2:3], st[:, 0:1], -1.0, None, ALU.mult),
        v.tensor_scalar(st[:, 3:4], st[:, 1:2], EPS, None, ALU.add))[-1],
        sig="dve", waits=[t2])
    t_dve = E.op("dve", lambda v, st=st: v.reciprocal(st[:, 4:5], st[:, 3:4]),
                 sig="dve", waits=[t3])
    t5 = E.op("act", lambda a, st=st: a.activation(st[:, 5:6], st[:, 4:5], AF.Ln),
              sig="act", waits=[t_dve])
    t6 = E.op("act", lambda a, st=st: a.activation(st[:, 5:6], st[:, 5:6], AF.Exp, scale=0.5),
              sig="act", waits=[t5])
    t_act = E.op("act", lambda a, st=st: a.activation(st[:, 6:7], st[:, 2:3], AF.Identity, scale=st[:, 5:6]),
                 sig="act", waits=[t6])
    tile = x_tok[:, c * D:(c + 1) * D]
    t_stt = E.op("dve", lambda v, ysrc=ysrc_ap, tile=tile, st=st: v.scalar_tensor_tensor(
        tile, ysrc, st[:, 5:6], tile, ALU.mult, ALU.add),
        sig="dve", waits=[t_act])
    t = E.op("act", lambda a, tile=tile, st=st: a.add(tile, tile, st[:, 6:7]),
             sig="act", waits=[t_stt])
    if slot_act is not None:
        slot_act[c % 2] = t
    return t


def load_slab(C, dram_ap, row0, rows, col0, cols, slot, extra=()):
    E = C.E
    nk = rows // 128
    ws, base, pitch = C.wslab_base[slot]

    def fn(e, dram_ap=dram_ap, row0=row0, rows=rows, col0=col0, cols=cols,
           base=base, nk=nk, ws=ws, pitch=pitch):
        src = dram_ap[row0:row0 + rows, col0:col0 + cols].rearrange("(k p) o -> p k o", p=128)
        dst = bass.AP(ws, base, [[pitch, 128], [cols, nk], [1, cols]])
        return e.dma_start(out=dst, in_=src)
    tok = E.op("pool", fn, sig="dmaw", waits=[C.wslab_busy[slot]] + list(extra))

    def get(k, c0, c1, ws=ws, base=base):
        return ws[:, base + k * cols + c0: base + k * cols + c1]
    return get, tok


def mm_evac(C, pairs, out_rows, ncols, evac_fn, waits, evac_eng="act", evac_sig=None):
    E = C.E
    bank, bfree, bi = C.take_bank()
    n = len(pairs)
    t_mm = E.op("pe", lambda t, pairs=pairs, bank=bank, out_rows=out_rows, ncols=ncols, n=n: [
        t.matmul(bank[0:out_rows, 0:ncols], l, r, start=(i == 0), stop=(i == n - 1), skip_group_check=True)
        for i, (l, r) in enumerate(pairs)][-1],
        sig="pe", waits=list(waits) + [bfree])
    t_ev = E.op(evac_eng, lambda e, bank=bank, out_rows=out_rows, ncols=ncols: evac_fn(
        e, bank[0:out_rows, 0:ncols]), sig=evac_sig or evac_eng, waits=[t_mm])
    C.release_bank(bi, t_ev)
    return t_ev


def emit_rope_half(C, buf, coff, tabname, half, nt, waits):
    E = C.E
    scr = C.scr_tok
    ca, cb = {"rt32": (2048, 3072), "rt64": (0, 1024)}[tabname]

    waits = list(waits) + [getattr(C, "lastln_pe", None)]

    def prods(v, buf=buf, coff=coff, half=half, nt=nt, ca=ca, cb=cb):
        tab = C.sb["ropes"]
        x1 = buf[0:half, coff:coff + nt]
        x2 = buf[half:2 * half, coff:coff + nt]
        cosA = tab[0:half, ca:ca + nt]
        sinA = tab[half:2 * half, ca:ca + nt]
        sinB = tab[0:half, cb:cb + nt]
        cosB = tab[half:2 * half, cb:cb + nt]
        v.tensor_tensor(scr[0:half, 0:nt], x1, cosA, ALU.mult)
        v.tensor_tensor(scr[0:half, nt:2 * nt], x2, sinA, ALU.mult)
        v.tensor_tensor(scr[half:2 * half, 0:nt], x2, cosB, ALU.mult)
        return v.tensor_tensor(scr[half:2 * half, nt:2 * nt], x1, sinB, ALU.mult)

    def combine(v, buf=buf, coff=coff, half=half, nt=nt):
        x1 = buf[0:half, coff:coff + nt]
        x2 = buf[half:2 * half, coff:coff + nt]
        v.tensor_tensor(x1, scr[0:half, 0:nt], scr[0:half, nt:2 * nt], ALU.subtract)
        return v.tensor_tensor(x2, scr[half:2 * half, 0:nt], scr[half:2 * half, nt:2 * nt], ALU.add)
    t1 = E.op("dve", prods, sig="dve", waits=waits)
    return E.op("dve", combine, sig="dve", waits=[t1])


def exchange_fm_buf(C, buf, nch, shreg, waits):
    def wr(e, buf=buf, nch=nch, shreg=shreg):
        regs = C.E.regs
        src = buf.at(0, [[AW, 128], [S, nch], [1, TOK]])
        d0 = shreg.ap()[0:nch * 128, :].rearrange("(k p) t -> p k t", p=128)
        with e.If_eq(regs["parity"], 0):
            e.dma_start(out=d0[:, :, 0:TOK], in_=src).then_inc(C.E.sem_objs["dmas"], 16)
        with e.Else():
            e.dma_start(out=d0[:, :, TOK:S], in_=src).then_inc(C.E.sem_objs["dmas"], 16)

    def rd(e, buf=buf, nch=nch, shreg=shreg):
        regs = C.E.regs
        dst = buf.at(TOK, [[AW, 128], [S, nch], [1, TOK]])
        s0 = shreg.ap()[0:nch * 128, :].rearrange("(k p) t -> p k t", p=128)
        with e.If_eq(regs["parity"], 0):
            e.dma_start(out=dst, in_=s0[:, :, TOK:S]).then_inc(C.E.sem_objs["dmas"], 16)
        with e.Else():
            e.dma_start(out=dst, in_=s0[:, :, 0:TOK]).then_inc(C.E.sem_objs["dmas"], 16)

    E = C.E
    fid = C.n_flags
    C.n_flags += 2
    for w in waits:
        if w is not None:
            E.wait("sp", w[0], w[1])
    E.raw("sp", wr)
    E.cnt["dmas"] += 1
    wr_cnt = E.cnt["dmas"]
    E.wait("sp", "dmas", wr_cnt)

    def send_flag(e, fid=fid):
        regs = C.E.regs
        with e.If_eq(regs["parity"], 0):
            e.dma_start(out=C.shf.ap()[0:1, fid:fid + 1], in_=C.sb["meta"][0:1, 1:2]).then_inc(
                C.E.sem_objs["dmas"], 16)
        with e.Else():
            e.dma_start(out=C.shf.ap()[0:1, fid + 1:fid + 2], in_=C.sb["meta"][0:1, 1:2]).then_inc(
                C.E.sem_objs["dmas"], 16)
    E.raw("sp", send_flag)
    E.cnt["dmas"] += 1
    E.last_wait[("sp", "dmas")] = E.cnt["dmas"]

    def poll(e, fid=fid):
        regs = C.E.regs
        fv, tgt, cc = regs["fv"], regs["tgt"], regs["cc"]
        e.reg_mov(cc, 1)
        with e.While(cc):
            with e.If_eq(regs["parity"], 0):
                e.dma_start(out=C.sb["flags"][0:1, 0:1],
                            in_=C.shf.ap()[0:1, fid + 1:fid + 2]).then_inc(C.E.sem_objs["poll"], 16)
            with e.Else():
                e.dma_start(out=C.sb["flags"][0:1, 0:1],
                            in_=C.shf.ap()[0:1, fid:fid + 1]).then_inc(C.E.sem_objs["poll"], 16)
            e.reg_add(tgt, tgt, 16)
            e.wait_ge(C.E.sem_objs["poll"], tgt)
            e.reg_load(fv, C.sb["flags"][0:1, 0:1])
            e.reg_alu(cc, fv, regs["nonce"], ALU.not_equal)
    E.raw("sp", poll)
    E.raw("sp", rd)
    E.cnt["dmas"] += 1
    return E.tok("dmas")



def emit_mla_attn(C, lx, q_lora, d_nope, d_rope, d_v, wqa_ap, wqb_ap, wkva_ap,
                  wkvb_ap, wo_ap):
    E = C.E
    H = 16
    d_qk = d_nope + d_rope
    nkch = (d_qk + 127) // 128
    kv_lora = 512
    nlat = kv_lora // 128
    scale = 1.0 / float(np.sqrt(d_qk))
    nql = q_lora // 128

    xhat = C.region(0, 16 * TOK)
    latb = C.region(16384, (nlat + 1) * S)
    qlat = C.region(16384 + (nlat + 1) * S, nql * TOK)
    obuf = C.region(38912, 16 * TOK)
    kh = C.region(0, nkch * S)
    vfm = C.region(2 * S, S)
    vtok = C.region(3 * S, S)
    qh = C.region(4 * S, 2 * TOK)
    pbuf = C.region(4 * S + 2 * TOK, 4 * 512)
    npb = 4
    zbs = C.region(4 * S + 2 * TOK + npb * 512, 512)

    t_xhat = emit_ln_to_fm(C, xhat, waits=[E.tok("act")])

    for oc in range(nql):
        slot = oc % 2
        wget, wtok = load_slab(C, wqa_ap, 0, D, oc * 128, 128, slot)
        last = None
        for q2 in range(2):
            last = mm_evac(
                C, [(wget(k, 0, 128), xhat[:, k * TOK + q2 * 512:k * TOK + (q2 + 1) * 512])
                    for k in range(NCH)], 128, 512,
                lambda a, bk, oc=oc, q2=q2: a.activation(
                    qlat[0:128, oc * TOK + q2 * 512:oc * TOK + (q2 + 1) * 512], bk, AF.Identity),
                [t_xhat, wtok])
        C.wslab_busy[slot] = last
    for oc in range(nlat + 1):
        rows = d_rope if oc == nlat else 128
        slot = oc % 2
        wget, wtok = load_slab(C, wkva_ap, 0, D, oc * 128, rows, slot)
        last = None
        for q2 in range(2):
            last = mm_evac(
                C, [(wget(k, 0, rows), xhat[:, k * TOK + q2 * 512:k * TOK + (q2 + 1) * 512])
                    for k in range(NCH)], rows, 512,
                lambda a, bk, oc=oc, q2=q2, rows=rows: a.activation(
                    latb[0:rows, oc * S + q2 * 512:oc * S + (q2 + 1) * 512], bk, AF.Identity),
                [t_xhat, wtok])
        C.wslab_busy[slot] = last
    t_rope_k = emit_rope_half(C, C.region(16384 + nlat * S, S), 0, "rt32",
                              d_rope // 2, TOK, [E.tok("act")])

    t_exch = exchange_fm_buf(C, latb, nlat + 1, C.shared[f"lat{lx}"],
                             [t_rope_k, E.tok("act")])
    dep_lat = t_exch

    kh_free = None
    vfm_free = None
    vtok_free = None
    qh_free = None
    pb_free = [None] * npb
    t_last_norm = None
    for h in range(H):
        slot = h % 2
        t_kcp = E.op("dve", lambda v: v.tensor_copy(
            kh[0:d_rope, 0:S], latb[0:d_rope, nlat * S:(nlat + 1) * S]),
            sig="dve", waits=[dep_lat, kh_free])
        wget, wtok = load_slab(C, wkvb_ap, 0, kv_lora,
                               h * (d_nope + d_v), d_nope + d_v, slot)
        t_kn = None
        for kg in range(4):
            def ev_kn(a, bk, kg=kg):
                r0 = 128 - d_rope
                inst = a.activation(kh.h[d_rope:128, kh.col0 + kg * 512:kh.col0 + (kg + 1) * 512],
                                    bk[0:r0, :], AF.Identity)
                if d_nope > r0:
                    inst = a.activation(
                        kh.h[0:d_nope - r0, kh.col0 + S + kg * 512:kh.col0 + S + (kg + 1) * 512],
                        bk[r0:d_nope, :], AF.Identity)
                return inst
            t_kn = mm_evac(
                C, [(wget(k, 0, d_nope), latb[:, k * S + kg * 512:k * S + (kg + 1) * 512])
                    for k in range(nlat)], d_nope, 512, ev_kn,
                [wtok, C.wslab_busy[slot], t_kcp, dep_lat])
        t_vf = None
        for kg in range(4):
            t_vf = mm_evac(
                C, [(wget(k, d_nope, d_nope + d_v), latb[:, k * S + kg * 512:k * S + (kg + 1) * 512])
                    for k in range(nlat)], d_v, 512,
                lambda a, bk, kg=kg: a.activation(
                    vfm[0:d_v, kg * 512:(kg + 1) * 512], bk, AF.Identity),
                [wtok, C.wslab_busy[slot], vfm_free, dep_lat])
        C.wslab_busy[slot] = t_vf
        t_vt = None
        for g in range(2):
            t_pe = E.op("pe", lambda t, g=g: [
                t.transpose(C.ptr[:, j * 128:(j + 1) * 128],
                            vfm[0:128, (g * 8 + j) * 128:(g * 8 + j + 1) * 128],
                            C.sb["ident"][:, :]) for j in range(8)][-1],
                sig="pe", waits=[t_vf, vtok_free, t_vt, C.ptr_free])
            t_vt = E.op("dve", lambda v, g=g: v.tensor_copy(
                vtok[0:128, g * 1024:(g + 1) * 1024], C.ptr[:, 0:1024]),
                sig="dve", waits=[t_pe])
            C.ptr_free = t_vt
        t_qh = None
        wgetq, wtokq = load_slab(C, wqb_ap, 0, q_lora, h * d_qk, d_qk, "a2")
        for q2 in range(2):
            def ev_q(a, bk, q2=q2):
                inst = a.activation(qh.h[0:min(128, d_qk),
                                         qh.col0 + q2 * 512:qh.col0 + (q2 + 1) * 512],
                                    bk[0:min(128, d_qk), :], AF.Identity)
                return inst
            t_qh = mm_evac(
                C, [(wgetq(k, 0, min(128, d_qk)),
                     qlat[:, k * TOK + q2 * 512:k * TOK + (q2 + 1) * 512])
                    for k in range(nql)], min(128, d_qk), 512, ev_q,
                [wtokq, qh_free, C.wslab_busy["a2"]])
            if d_qk > 128:
                t_qh = mm_evac(
                    C, [(wgetq(k, 128, d_qk),
                         qlat[:, k * TOK + q2 * 512:k * TOK + (q2 + 1) * 512])
                        for k in range(nql)], d_qk - 128, 512,
                    lambda a, bk, q2=q2: a.activation(
                        qh.h[0:d_qk - 128, qh.col0 + TOK + q2 * 512:qh.col0 + TOK + (q2 + 1) * 512],
                        bk, AF.Identity),
                    [wtokq, qh_free, C.wslab_busy["a2"]])
        C.wslab_busy["a2"] = t_qh
        t_qrope = emit_rope_half(C, qh, 0, "rt32", d_rope // 2, TOK, [t_qh])

        mpos = {}
        mi = 0
        for (q0, s0) in MASKED_POS:
            for sub in range(4):
                mpos[(q0, s0, sub)] = mi
                mi += 1
        for qc in range(2):
            tiles = [(sl, sub) for sl in SCHED[qc] for sub in range(4)]
            nt = len(tiles)
            pend = None
            for ti, (sl, sub) in enumerate(tiles):
                koff = sl * 512 + sub * 128
                sbank, sfree, sbi = C.take_bank()
                t_s = E.op("pe", lambda t, koff=koff, qc=qc, sbank=sbank: [
                    t.matmul(sbank[0:128, 0:512],
                             kh.h[0:(128 if c2 == 0 else d_qk - 128),
                                  kh.col0 + c2 * S + koff:kh.col0 + c2 * S + koff + 128],
                             qh.h[0:(128 if c2 == 0 else d_qk - 128),
                                  qh.col0 + c2 * TOK + qc * 512:qh.col0 + c2 * TOK + (qc + 1) * 512],
                             start=(c2 == 0), stop=(c2 == nkch - 1), skip_group_check=True)
                    for c2 in range(nkch)][-1],
                    sig="pe", waits=[sfree, t_kn, t_vt, t_qrope])
                ps = ti % npb
                t_exp = E.op("act", lambda a, sbank=sbank, ps=ps: a.activation(
                    pbuf[0:128, ps * 512:(ps + 1) * 512], sbank[0:128, 0:512],
                    AF.Exp, scale=scale), sig="act", waits=[t_s, pb_free[ps]])
                C.release_bank(sbi, t_exp)
                t_pready = t_exp
                if (qc, sl, sub) in mpos:
                    t_pready = E.op("dve", lambda v, ps=ps, m=mpos[(qc, sl, sub)]: v.tensor_tensor(
                        pbuf[0:128, ps * 512:(ps + 1) * 512],
                        pbuf[0:128, ps * 512:(ps + 1) * 512],
                        C.sb["masks"][:, m * 512:(m + 1) * 512], ALU.mult),
                        sig="dve", waits=[t_exp])
                if pend is not None:
                    pb_free[pend[0]] = emit_zav(C, pbuf, vtok, pend[0], pend[1], pend[2],
                                                pend[3], nt)
                pend = (ps, ti, koff, t_pready)
            pb_free[pend[0]] = emit_zav(C, pbuf, vtok, pend[0], pend[1], pend[2],
                                        pend[3], nt)
            t_last_norm = emit_znorm(C, obuf, zbs, h, qc)
        kh_free = E.tok("pe")
        vtok_free = E.tok("pe")
        qh_free = E.tok("pe")

    emit_out_ln2(C, obuf, wo_ap, [t_last_norm])


def emit_zav(C, pbuf, vtok, ps, ti, koff, t_pready, nt):
    E = C.E
    first = (ti == 0)
    last = (ti == nt - 1)
    kt = koff // 128
    waits = [t_pready]
    if first:
        waits += [C.obank_free, C.pz_free]
    t = E.op("pe", lambda t, ps=ps, first=first, last=last, kt=kt: (
        t.matmul(C.pz[0:1, 0:512], C.sb["ones"][:, 0:1],
                 pbuf[0:128, ps * 512:(ps + 1) * 512], start=first, stop=last,
                 skip_group_check=True),
        t.matmul(C.obank[0:128, 0:512], vtok[0:128, kt * 128:(kt + 1) * 128],
                 pbuf[0:128, ps * 512:(ps + 1) * 512], start=first, stop=last,
                 skip_group_check=True))[-1],
        sig="pe", waits=waits)
    return t


def emit_znorm(C, obuf, zbs, h, qc):
    E = C.E
    t_avz = E.tok("pe")
    def _recip(v):
        with C.nc.allow_low_precision(reason="softmax 1/Z to bf16; probs are bf16"):
            return v.reciprocal(C.sb["zrec"][0:1, 0:512], C.pz[0:1, 0:512])
    t_zev = E.op("dve", _recip, sig="dve", waits=[t_avz])
    C.pz_free = t_zev
    bank, bfree, bi = C.take_bank()
    t_bc = E.op("pe", lambda t, bank=bank: t.matmul(
        bank[0:128, 0:512], C.sb["ones"][0:1, 0:128], C.sb["zrec"][0:1, 0:512],
        start=True, stop=True, skip_group_check=True),
        sig="pe", waits=[t_zev, bfree])
    t_zb = E.op("dve", lambda v, bank=bank: v.tensor_copy(zbs[0:128, 0:512], bank[0:128, 0:512]),
                sig="dve", waits=[t_bc])
    C.release_bank(bi, t_zb)
    t_norm = E.op("dve", lambda v, h=h, qc=qc: v.tensor_tensor(
        obuf[0:128, h * TOK + qc * 512:h * TOK + (qc + 1) * 512],
        C.obank[0:128, 0:512], zbs[0:128, 0:512], ALU.mult),
        sig="dve", waits=[t_zb])
    C.obank_free = t_norm
    return t_norm


def emit_out_ln2(C, obuf, wo_ap, deps, obuf_col0=38912, ytok_col0=16384):
    E = C.E
    ytok = C.region(ytok_col0, 8 * D)
    tile_done = [None] * 8
    C.wslab_busy["big"] = (E.tok("pe"), E.tok("dve"))[0]
    E.wait("pool", "dve", E.cnt["dve"])
    for dg in range(4):
        wget, wtok = load_slab(C, wo_ap, 0, D, dg * 512, 512, "big")
        last = None
        for c in range(8):
            qc, c4 = divmod(c, 4)
            last = mm_evac(
                C, [(obuf[:, k * TOK + qc * 512 + c4 * 128:k * TOK + qc * 512 + c4 * 128 + 128],
                     wget(k, 0, 512)) for k in range(NCH)], 128, 512,
                lambda a, bk, c=c, dg=dg: a.activation(
                    ytok[0:128, c * D + dg * 512:c * D + dg * 512 + 512], bk, AF.Identity),
                deps + [wtok, C.wslab_busy["big"], tile_done[c]])
            tile_done[c] = last
        C.wslab_busy["big"] = last
    slot_act = [None, None]
    for c in range(8):
        t = emit_ln2_tile(C, c, ytok[0:128, c * D:(c + 1) * D], tile_done[c], slot_act)
        tile_done[c] = t
    return tile_done[-1]


def emit_swiglu_ffn(C, gu_ap, dn_ap, inter):
    E = C.E
    nic = inter // 128
    xhat = C.region(0, 16 * TOK)
    x2 = C.region(16384, nic * 512)
    yfm = C.region(8192, 16 * 512)

    C.ffn_slot_act = [None, None]
    for qc in range(2):
        scr_free = [E.tok("dve")] * 4
        t_xhat = emit_ln_to_fm(C, xhat, waits=[E.tok("act"), E.tok("pe")])
        x2_done = None
        qc_guard = E.tok("act")
        C.wslab_busy["a0"] = qc_guard
        C.wslab_busy["a1"] = qc_guard
        for hc in range(nic):
            slot = hc % 2
            wg, tg = load_slab(C, gu_ap, 0, D, hc * 128, 128, slot)
            wu, tu = load_slab(C, gu_ap, 0, D, inter + hc * 128, 128, ("a0", "a1")[slot])
            bankG, gfree, gi = C.take_bank()
            t_g = E.op("pe", lambda t, wg=wg, qc=qc, bankG=bankG: [
                t.matmul(bankG[0:128, 0:512], wg(k, 0, 128),
                         xhat[:, k * TOK + qc * 512:k * TOK + (qc + 1) * 512],
                         start=(k == 0), stop=(k == NCH - 1), skip_group_check=True)
                for k in range(NCH)][-1], sig="pe",
                waits=[t_xhat, tg, gfree])
            ssl = hc % 4
            t_sg = E.op("act", lambda a, bankG=bankG, ssl=ssl: a.activation(
                C.scr_tok[0:128, ssl * 512:(ssl + 1) * 512], bankG[0:128, 0:512], AF.Silu),
                sig="act", waits=[t_g, scr_free[ssl]])
            C.release_bank(gi, t_sg)
            bankU, ufree, ui = C.take_bank()
            t_u = E.op("pe", lambda t, wu=wu, qc=qc, bankU=bankU: [
                t.matmul(bankU[0:128, 0:512], wu(k, 0, 128),
                         xhat[:, k * TOK + qc * 512:k * TOK + (qc + 1) * 512],
                         start=(k == 0), stop=(k == NCH - 1), skip_group_check=True)
                for k in range(NCH)][-1], sig="pe",
                waits=[t_xhat, tu, ufree])
            x2_done = E.op("dve", lambda v, hc=hc, bankU=bankU, ssl=ssl: v.tensor_tensor(
                x2[0:128, hc * 512:(hc + 1) * 512], bankU[0:128, 0:512],
                C.scr_tok[0:128, ssl * 512:(ssl + 1) * 512], ALU.mult),
                sig="dve", waits=[t_u, t_sg])
            scr_free[ssl] = x2_done
            C.release_bank(ui, x2_done)
            C.wslab_busy[slot] = t_g
            C.wslab_busy[("a0", "a1")[slot]] = t_u
        yfm_done = [None] * 16
        C.wslab_busy["big"] = E.tok("pe")
        for oc in range(NCH):
            wget, wtok = load_slab(C, dn_ap, 0, inter, oc * 128, 128, "big")
            yfm_done[oc] = mm_evac(
                C, [(wget(k, 0, 128), x2[:, k * 512:(k + 1) * 512]) for k in range(nic)],
                128, 512,
                lambda a, bk, oc=oc: a.activation(
                    yfm[0:128, oc * 512:(oc + 1) * 512], bk, AF.Identity),
                [x2_done, wtok, C.wslab_busy["big"]])
            C.wslab_busy["big"] = yfm_done[oc]
        last_ev = None
        for c4 in range(4):
            c = qc * 4 + c4
            for g in range(4):
                t_pe = E.op("pe", lambda t, g=g, c4=c4: [
                    t.transpose(C.ptr[:, j * 128:(j + 1) * 128],
                                yfm[0:128, (g * 4 + j) * 512 + c4 * 128:(g * 4 + j) * 512 + c4 * 128 + 128],
                                C.sb["ident"][:, :]) for j in range(4)][-1],
                    sig="pe", waits=[yfm_done[g * 4 + 3], last_ev, C.ptr_free])
                last_ev = E.op("dve", lambda v, g=g: v.tensor_copy(
                    bass.AP(C.sb["ytok2"], g * 4 * 128, [[D, 128], [128, 4], [1, 128]]),
                    C.ptr[:, 0:512].rearrange("p (a b) -> p a b", a=4)),
                    sig="dve", waits=[t_pe])
                C.ptr_free = last_ev
            t = emit_ln2_tile(C, c, C.sb["ytok2"][0:128, 0:D], last_ev, C.ffn_slot_act)
            last_ev = t
    return E.tok("act")



def emit_l0_ffn(C):
    E = C.E
    inter = 8192
    nic = inter // 128
    xhat = C.region(0, 16 * TOK)
    x2 = C.region(16384, nic * 512)
    yfm = C.region(8192, 16 * 512)
    sup = C.region(49152, 2048)
    sdn = C.region(49152 + 2048, 2048)
    tsl = C.region(53248, 2048)
    wq = C.wio["L0_wupq"]
    wdq = C.wio["L0_wdnq"]
    ua = C.wio["L0_ua"]
    ub = C.wio["L0_ub"]
    da = C.wio["L0_da"]
    db = C.wio["L0_db"]

    ffn_guard = [(n, C.E.cnt[n]) for n in ("pe", "dve", "act")]
    t_sc = E.op("pool", lambda e: e.dma_start(
        out=sup.at(0, [[AW, 128], [32, 64], [1, 32]]),
        in_=C.inputs_ap["L0_wupsc"].rearrange("(c p) b -> p c b", p=128)), sig="dmaw",
        waits=[C.wslab_busy["a0"]] + ffn_guard)
    t_sc2 = E.op("pool", lambda e: e.dma_start(
        out=sdn.at(0, [[AW, 128], [128, 16], [1, 128]]),
        in_=C.inputs_ap["L0_wdnsc"].rearrange("(c p) b -> p c b", p=128)), sig="dmaw",
        waits=[C.wslab_busy["a1"]] + ffn_guard)
    E.op("pool", lambda e: e.dma_start(out=C.sb["vecs"][:, 0:64],
                                       in_=C.inputs_ap["L0_bup"][:, :]), sig="dmaw")
    t_bv = E.op("pool", lambda e: e.dma_start(out=C.sb["vecs"][:, 64:80],
                                              in_=C.inputs_ap["L0_bdn"][:, :]), sig="dmaw")

    C.ffn_slot_act = [None, None]
    for qc in range(2):
        t_xhat = emit_ln_to_fm(C, xhat, waits=[E.tok("act"), E.tok("pe")])
        wgu, tgu = load_slab(C, ua, 0, D, 0, 16, 0)
        u1 = C.scr_tok
        t_u1 = mm_evac(
            C, [(wgu(k, 0, 16), xhat[:, k * TOK + qc * 512:k * TOK + (qc + 1) * 512])
                for k in range(NCH)], 16, 512,
            lambda a, bk: a.activation(u1[0:16, 1536:2048], bk, AF.Identity),
            [t_xhat, tgu], evac_eng="act")
        C.wslab_busy[0] = t_u1
        x2_done = None
        t_tslfree = None
        lorb_tok = None
        for hc in range(nic):
            if hc % 16 == 0:
                lorb_tok = E.op("pool", lambda e, hc=hc: e.dma_start(
                    out=C.sb["wslab"][0:16, 2048:4096],
                    in_=ub[0:16, hc * 128:(hc + 16) * 128]), sig="dmaw",
                    waits=[C.wslab_busy[1]])
            wg0, tg0 = load_slab(C, wq, hc * 128, 128, 0, 2048, 0)
            t_dq = E.op("dve", lambda v, wg0=wg0, hc=hc: v.scalar_tensor_tensor(
                C.sb["wslab"][:, 0:2048].rearrange("p (b k) -> p b k", b=32),
                C.sb["wslab"][:, 0:2048].rearrange("p (b k) -> p b k", b=32),
                -7.5,
                sup[0:128, hc * 32:(hc + 1) * 32].unsqueeze(2).broadcast_to([128, 32, 64]),
                ALU.add, ALU.mult), sig="dve", waits=[tg0, t_sc])
            t_tr = None
            for g in range(4):
                t_pe = E.op("pe", lambda t, g=g: [
                    t.transpose(C.ptr[:, j * 128:(j + 1) * 128],
                                C.sb["wslab"][0:128, (g * 4 + j) * 128:(g * 4 + j + 1) * 128],
                                C.sb["ident"][:, :]) for j in range(4)][-1],
                    sig="pe", waits=[t_dq, t_tr, t_tslfree, C.ptr_free])
                t_tr = E.op("dve", lambda v, g=g: v.tensor_copy(
                    tsl.at(g * 512, [[AW, 128], [1, 512]]),
                    C.ptr[:, 0:512]), sig="dve", waits=[t_pe])
                C.ptr_free = t_tr
            bank, bfree, bi = C.take_bank()
            t_mm = E.op("pe", lambda t, qc=qc, bank=bank, hc=hc: ([
                t.matmul(bank[0:128, 0:512],
                         tsl[0:128, k * 128:(k + 1) * 128],
                         xhat[:, k * TOK + qc * 512:k * TOK + (qc + 1) * 512],
                         start=(k == 0), stop=False, skip_group_check=True)
                for k in range(NCH)],
                t.matmul(bank[0:128, 0:512],
                         C.sb["wslab"][0:16, 2048 + (hc % 16) * 128:2048 + (hc % 16 + 1) * 128],
                         u1[0:16, 1536:2048],
                         start=False, stop=True, skip_group_check=True))[-1],
                sig="pe", waits=[t_tr, t_u1, lorb_tok, bfree, C.wslab_busy[1]])
            t_tslfree = t_mm
            C.wslab_busy[1] = t_mm
            x2_done = E.op("act", lambda a, bank=bank, hc=hc: a.activation(
                x2[0:128, hc * 512:(hc + 1) * 512], bank[0:128, 0:512],
                AF.Relu, bias=C.sb["vecs"][0:128, hc:hc + 1]),
                sig="act", waits=[t_mm, t_bv])
            C.release_bank(bi, x2_done)
            C.wslab_busy[0] = t_mm
        wgd, tgd = load_slab(C, da, 0, inter, 0, 16, 1)
        t_u2 = mm_evac(
            C, [(wgd(k, 0, 16), x2[:, k * 512:(k + 1) * 512]) for k in range(nic)],
            16, 512,
            lambda a, bk: a.activation(u1[0:16, 1024:1536], bk, AF.Identity),
            [x2_done, tgd], evac_eng="act")
        C.wslab_busy[1] = t_u2
        t_lb2 = E.op("pool", lambda e: e.dma_start(
            out=C.sb["wslab"][0:16, 2048:4096], in_=db[0:16, 0:2048]), sig="dmaw",
            waits=[C.wslab_busy[1]])
        yfm_done = [None] * 16
        C.wslab_busy["big"] = E.tok("pe")
        for oc in range(NCH):
            wgb, tgb = load_slab(C, wdq, oc * 128, 128, 0, inter, "big")
            t_dq2 = E.op("dve", lambda v, oc=oc: v.scalar_tensor_tensor(
                C.arena[0:128, 0:8192].rearrange("p (b k) -> p b k", b=128),
                C.arena[0:128, 0:8192].rearrange("p (b k) -> p b k", b=128),
                -7.5,
                sdn[0:128, oc * 128:(oc + 1) * 128].unsqueeze(2).broadcast_to([128, 128, 64]),
                ALU.add, ALU.mult), sig="dve", waits=[tgb, t_sc2])
            bank, bfree, bi = C.take_bank()
            first = True
            t_tr2 = None
            for g in range(16):
                t_pe = E.op("pe", lambda t, g=g: [
                    t.transpose(C.ptr[:, j * 128:(j + 1) * 128],
                                C.arena[0:128, (g * 4 + j) * 128:(g * 4 + j + 1) * 128],
                                C.sb["ident"][:, :]) for j in range(4)][-1],
                    sig="pe", waits=[t_dq2, t_tr2, bfree, C.ptr_free])
                t_tr2 = E.op("dve", lambda v, g=g: v.tensor_copy(
                    tsl.at((g % 4) * 512, [[AW, 128], [1, 512]]),
                    C.ptr[:, 0:512]), sig="dve", waits=[t_pe])
                C.ptr_free = t_tr2
                t_mm2 = E.op("pe", lambda t, g=g, qc=qc, bank=bank, first=first: [
                    t.matmul(bank[0:128, 0:512],
                             tsl[0:128, (g % 4) * 512 + j * 128:(g % 4) * 512 + (j + 1) * 128],
                             x2[:, (g * 4 + j) * 512:(g * 4 + j + 1) * 512],
                             start=(first and j == 0), stop=False, skip_group_check=True)
                    for j in range(4)][-1], sig="pe", waits=[t_tr2])
                first = False
                t_tr2 = t_mm2
            t_mm3 = E.op("pe", lambda t, oc=oc, bank=bank: t.matmul(
                bank[0:128, 0:512],
                C.sb["wslab"][0:16, 2048 + oc * 128:2048 + (oc + 1) * 128],
                u1[0:16, 1024:1536],
                start=False, stop=True, skip_group_check=True),
                sig="pe", waits=[t_u2, t_lb2])
            yfm_done[oc] = E.op("act", lambda a, bank=bank, oc=oc: a.activation(
                yfm[0:128, oc * 512:(oc + 1) * 512], bank[0:128, 0:512],
                AF.Identity, bias=C.sb["vecs"][0:128, 64 + oc:64 + oc + 1]),
                sig="act", waits=[t_mm3, t_bv])
            C.release_bank(bi, yfm_done[oc])
            C.wslab_busy["big"] = E.tok("pe")
        C.wslab_busy[1] = E.tok("pe")
        last_ev = E.tok("pe")
        for c4 in range(4):
            c = qc * 4 + c4
            for g in range(4):
                t_pe = E.op("pe", lambda t, g=g, c4=c4: [
                    t.transpose(C.ptr[:, j * 128:(j + 1) * 128],
                                yfm[0:128, (g * 4 + j) * 512 + c4 * 128:(g * 4 + j) * 512 + c4 * 128 + 128],
                                C.sb["ident"][:, :]) for j in range(4)][-1],
                    sig="pe", waits=[yfm_done[g * 4 + 3], last_ev, C.ptr_free])
                last_ev = E.op("dve", lambda v, g=g: v.tensor_copy(
                    bass.AP(C.sb["ytok2"], g * 4 * 128, [[D, 128], [128, 4], [1, 128]]),
                    C.ptr[:, 0:512].rearrange("p (a b) -> p a b", a=4)),
                    sig="dve", waits=[t_pe])
                C.ptr_free = last_ev
            t = emit_ln2_tile(C, c, C.sb["ytok2"][0:128, 0:D], last_ev, C.ffn_slot_act)
            last_ev = t
    return E.tok("act")



def emit_l3_attn(C):
    E = C.E
    H = 16
    scale = 1.0 / float(np.sqrt(128.0))
    xhat = C.region(0, 16 * TOK)
    latb = C.region(16384, 4 * S)
    kstg = C.region(24576, 2 * TOK)
    qlat = C.region(26624, 8 * TOK)
    obuf = C.region(34816, 16 * TOK)
    qpe = C.region(43008, 8 * TOK)
    kh = C.region(0, S)
    vfm = C.region(2 * S, S)
    vtok = C.region(3 * S, S)
    qh = C.region(4 * S, TOK)
    qscr = C.region(4 * S + TOK, TOK)
    pbuf = C.region(4 * S + 2 * TOK, 4 * 512)
    zbs = C.region(4 * S + 2 * TOK + 4 * 512, 512)

    t_ropes = E.op("pool", lambda e: e.dma_start(
        out=C.sb["ropes"][:, 0:2 * TOK], in_=C.inputs_ap["ropes3_in"][:, :]),
        sig="dmaw", waits=[(n, E.cnt[n]) for n in ("dve",)])

    t_xhat = emit_ln_to_fm(C, xhat, waits=[E.tok("act"), E.tok("pe")])

    for nm, wap, nql_, dst, own_half in (
            ("qd", C.wio["L3_Wqd"], 8, qlat, False),
            ("kvd", C.wio["L3_Wkvd"], 4, latb, True)):
        for oc in range(nql_):
            slot = oc % 2
            wget, wtok = load_slab(C, wap, 0, D, oc * 128, 128, slot)
            last = None
            for q2 in range(2):
                W = S if own_half else TOK
                last = mm_evac(
                    C, [(wget(k, 0, 128), xhat[:, k * TOK + q2 * 512:k * TOK + (q2 + 1) * 512])
                        for k in range(NCH)], 128, 512,
                    lambda a, bk, oc=oc, q2=q2, dst=dst, W=W: a.activation(
                        dst[0:128, oc * W + q2 * 512:oc * W + (q2 + 1) * 512], bk, AF.Identity),
                    [t_xhat, wtok])
            C.wslab_busy[slot] = last
    kstg_free = [None, None]
    for oc in range(8):
        slot = oc % 2
        wget, wtok = load_slab(C, C.wio["L3_Wkr"], 0, D, oc * 128, 128, slot)
        last = None
        for q2 in range(2):
            last = mm_evac(
                C, [(wget(k, 0, 128), xhat[:, k * TOK + q2 * 512:k * TOK + (q2 + 1) * 512])
                    for k in range(NCH)], 128, 512,
                lambda a, bk, oc=oc, q2=q2: a.activation(
                    kstg[0:128, (oc % 2) * TOK + q2 * 512:(oc % 2) * TOK + (q2 + 1) * 512],
                    bk, AF.Identity),
                [t_xhat, wtok, kstg_free[oc % 2]])
        C.wslab_busy[slot] = last
        t_kr = emit_rope_half(C, C.region(24576 + (oc % 2) * TOK, TOK), 0,
                              "rt64", 64, TOK, [last, t_ropes])

        def kwr(e, oc=oc):
            sem = C.E.sem_objs["dmas"]
            regs = C.E.regs
            src_ap = kstg[0:128, (oc % 2) * TOK:(oc % 2 + 1) * TOK]
            sh = C.shared["lat3"].ap()
            with e.If_eq(regs["parity"], 0):
                e.dma_start(out=sh[(4 + oc) * 128:(5 + oc) * 128, 0:TOK],
                            in_=src_ap).then_inc(sem, 16)
            with e.Else():
                e.dma_start(out=sh[(4 + oc) * 128:(5 + oc) * 128, TOK:S],
                            in_=src_ap).then_inc(sem, 16)
        E.wait("sp", t_kr[0], t_kr[1])
        E.raw("sp", kwr)
        E.cnt["dmas"] += 1
        E.last_wait[("sp", "dmas")] = 0
        E.wait("sp", "dmas", E.cnt["dmas"])
        kstg_free[oc % 2] = ("dmas", E.cnt["dmas"])
    t_qlat_done = E.tok("act")
    for oc in range(8):
        slot = oc % 2
        wget, wtok = load_slab(C, C.wio["L3_Wqr"], 0, 1024, oc * 128, 128, slot)
        last = None
        for q2 in range(2):
            last = mm_evac(
                C, [(wget(k, 0, 128), qlat[:, k * TOK + q2 * 512:k * TOK + (q2 + 1) * 512])
                    for k in range(8)], 128, 512,
                lambda a, bk, oc=oc, q2=q2: a.activation(
                    qpe[0:128, oc * TOK + q2 * 512:oc * TOK + (q2 + 1) * 512], bk, AF.Identity),
                [t_qlat_done, wtok])
        C.wslab_busy[slot] = last
    t_qrope8 = [None] * 8
    for oc in range(8):
        t_qrope8[oc] = emit_rope_half(C, C.region(43008 + oc * TOK, TOK), 0,
                                      "rt64", 64, TOK, [E.tok("act"), t_ropes])

    t_exch = exchange_fm_buf(C, latb, 4, C.shared["lat3"], [E.tok("act")])
    dep_lat = t_exch

    kh_free = None
    vfm_free = None
    vtok_free = None
    qh_free = None
    pb_free = [None] * 4
    t_last_norm = None
    mpos = {}
    mi = 0
    for (q0, s0) in MASKED_POS:
        for sub in range(4):
            mpos[(q0, s0, sub)] = mi
            mi += 1
    for h in range(H):
        slot = h % 2
        ch = 4 + h // 2
        r1 = (h % 2) * 32
        def kpe_dma(e, ch=ch, r1=r1):
            sh = C.shared["lat3"].ap()
            src1 = sh[ch * 128 + r1:ch * 128 + r1 + 32, :]
            src2 = sh[ch * 128 + 64 + r1:ch * 128 + 64 + r1 + 32, :]
            sem = C.E.sem_objs["dmas"]
            regs = C.E.regs
            with e.If_eq(regs["parity"], 0):
                e.dma_start(out=kh[0:32, 0:TOK], in_=src1[:, 0:TOK]).then_inc(sem, 16)
                e.dma_start(out=kh[0:32, TOK:S], in_=src1[:, TOK:S]).then_inc(sem, 16)
                e.dma_start(out=kh[32:64, 0:TOK], in_=src2[:, 0:TOK]).then_inc(sem, 16)
                e.dma_start(out=kh[32:64, TOK:S], in_=src2[:, TOK:S]).then_inc(sem, 16)
            with e.Else():
                e.dma_start(out=kh[0:32, 0:TOK], in_=src1[:, TOK:S]).then_inc(sem, 16)
                e.dma_start(out=kh[0:32, TOK:S], in_=src1[:, 0:TOK]).then_inc(sem, 16)
                e.dma_start(out=kh[32:64, 0:TOK], in_=src2[:, TOK:S]).then_inc(sem, 16)
                e.dma_start(out=kh[32:64, TOK:S], in_=src2[:, 0:TOK]).then_inc(sem, 16)
        E.wait("sp", t_exch[0], t_exch[1])
        if kh_free is not None:
            E.wait("sp", kh_free[0], kh_free[1])
        E.raw("sp", kpe_dma)
        E.cnt["dmas"] += 4
        E.last_wait[("sp", "dmas")] = 0
        E.wait("sp", "dmas", E.cnt["dmas"])
        t_kpe = ("dmas", E.cnt["dmas"])
        wku, tku = load_slab(C, C.wio["L3_Wku"], 0, 512, h * 64, 64, slot)
        ws, base, pitch = C.wslab_base[slot]
        def ld_v(e, base=base, h=h, ws=ws, pitch=pitch):
            src_ap = C.wio["L3_Wvu"][0:512, h * 128:(h + 1) * 128].rearrange("(k p) o -> p k o", p=128)
            return e.dma_start(out=bass.AP(ws, base + 256, [[pitch, 128], [128, 4], [1, 128]]), in_=src_ap)
        tv = E.op("pool", ld_v, sig="dmaw")
        def ld_q(e, base=base, h=h, ws=ws, pitch=pitch):
            src_ap = C.wio["L3_Wqu"][0:1024, h * 64:(h + 1) * 64].rearrange("(k p) o -> p k o", p=128)
            return e.dma_start(out=bass.AP(ws, base + 768, [[pitch, 128], [64, 8], [1, 64]]), in_=src_ap)
        tq = E.op("pool", ld_q, sig="dmaw")
        t_kn = None
        for kg in range(4):
            t_kn = mm_evac(
                C, [(ws[:, base + k * 64:base + (k + 1) * 64],
                     latb[:, k * S + kg * 512:k * S + (kg + 1) * 512]) for k in range(4)],
                64, 512,
                lambda a, bk, kg=kg: a.activation(
                    kh.h[64:128, kh.col0 + kg * 512:kh.col0 + (kg + 1) * 512], bk, AF.Identity),
                [tku, dep_lat, kh_free])
        t_vf = None
        for kg in range(4):
            t_vf = mm_evac(
                C, [(ws[:, base + 256 + k * 128:base + 256 + (k + 1) * 128],
                     latb[:, k * S + kg * 512:k * S + (kg + 1) * 512]) for k in range(4)],
                128, 512,
                lambda a, bk, kg=kg: a.activation(
                    vfm[0:128, kg * 512:(kg + 1) * 512], bk, AF.Identity),
                [tv, dep_lat, vfm_free])
        t_vt = None
        for g in range(2):
            t_pe = E.op("pe", lambda t, g=g: [
                t.transpose(C.ptr[:, j * 128:(j + 1) * 128],
                            vfm[0:128, (g * 8 + j) * 128:(g * 8 + j + 1) * 128],
                            C.sb["ident"][:, :]) for j in range(8)][-1],
                sig="pe", waits=[t_vf, vtok_free, t_vt, C.ptr_free])
            t_vt = E.op("dve", lambda v, g=g: v.tensor_copy(
                vtok[0:128, g * 1024:(g + 1) * 1024], C.ptr[:, 0:1024]),
                sig="dve", waits=[t_pe])
            C.ptr_free = t_vt
        def qcp(v, h=h):
            ch2 = h // 2
            r2 = (h % 2) * 32
            v.tensor_copy(qh[0:32, 0:TOK], qpe[r2:r2 + 32, ch2 * TOK:ch2 * TOK + TOK])
            if h % 2 == 0:
                return v.tensor_copy(qh[32:64, 0:TOK], qpe[64:96, ch2 * TOK:ch2 * TOK + TOK])
            v.tensor_copy(qscr[0:64, 0:TOK], qpe[64:128, ch2 * TOK:ch2 * TOK + TOK])
            return v.tensor_copy(qh[32:64, 0:TOK], qscr[32:64, 0:TOK])
        t_qcp = E.op("dve", qcp, sig="dve", waits=[t_qrope8[h // 2], qh_free])
        t_qn = None
        for q2 in range(2):
            t_qn = mm_evac(
                C, [(ws[:, base + 768 + k * 64:base + 768 + (k + 1) * 64],
                     qlat[:, k * TOK + q2 * 512:k * TOK + (q2 + 1) * 512]) for k in range(8)],
                64, 512,
                lambda a, bk, q2=q2: a.activation(
                    qh.h[64:128, qh.col0 + q2 * 512:qh.col0 + (q2 + 1) * 512], bk, AF.Identity),
                [tq, qh_free, t_qcp])
        C.wslab_busy[slot] = t_qn
        for qc in range(2):
            tiles = [(sl, sub) for sl in SCHED[qc] for sub in range(4)]
            nt = len(tiles)
            pend = None
            for ti, (sl, sub) in enumerate(tiles):
                koff = sl * 512 + sub * 128
                sbank, sfree, sbi = C.take_bank()
                t_s = E.op("pe", lambda t, koff=koff, qc=qc, sbank=sbank: t.matmul(
                    sbank[0:128, 0:512], kh[0:128, koff:koff + 128],
                    qh[0:128, qc * 512:(qc + 1) * 512],
                    start=True, stop=True, skip_group_check=True),
                    sig="pe", waits=[sfree, t_kn, t_vt, t_qn, t_kpe])
                ps = ti % 4
                t_exp = E.op("act", lambda a, sbank=sbank, ps=ps: a.activation(
                    pbuf[0:128, ps * 512:(ps + 1) * 512], sbank[0:128, 0:512],
                    AF.Exp, scale=scale), sig="act", waits=[t_s, pb_free[ps]])
                C.release_bank(sbi, t_exp)
                t_pready = t_exp
                if (qc, sl, sub) in mpos:
                    t_pready = E.op("dve", lambda v, ps=ps, m=mpos[(qc, sl, sub)]: v.tensor_tensor(
                        pbuf[0:128, ps * 512:(ps + 1) * 512],
                        pbuf[0:128, ps * 512:(ps + 1) * 512],
                        C.sb["masks"][:, m * 512:(m + 1) * 512], ALU.mult),
                        sig="dve", waits=[t_exp])
                if pend is not None:
                    pb_free[pend[0]] = emit_zav(C, pbuf, vtok, pend[0], pend[1], pend[2],
                                                pend[3], nt)
                pend = (ps, ti, koff, t_pready)
            pb_free[pend[0]] = emit_zav(C, pbuf, vtok, pend[0], pend[1], pend[2],
                                        pend[3], nt)
            t_last_norm = emit_znorm(C, obuf, zbs, h, qc)
        kh_free = E.tok("pe")
        vtok_free = E.tok("pe")
        qh_free = E.tok("pe")
    emit_out_ln2(C, obuf, C.wio["L3_Wo"], [t_last_norm], obuf_col0=34816)


def emit_l3_ffn(C):
    E = C.E
    hid = 5632
    nic = hid // 128
    xhat = C.region(0, 16 * TOK)
    x2 = C.region(16384, nic * 512)
    yfm = C.region(8192, 16 * 512)
    wg_ap = C.wio["L3_wg"]
    wu_ap = C.wio["L3_wu"]
    wd_ap = C.wio["L3_wd"]
    E.op("pool", lambda e: e.dma_start(out=C.sb["vecs"][:, 80:124],
                                       in_=C.inputs_ap["L3_sg"][:, :]), sig="dmaw")
    E.op("pool", lambda e: e.dma_start(out=C.sb["vecs"][:, 124:168],
                                       in_=C.inputs_ap["L3_su"][:, :]), sig="dmaw")
    t_v = E.op("pool", lambda e: e.dma_start(out=C.sb["vecs"][:, 168:184],
                                             in_=C.inputs_ap["L3_sd"][:, :]), sig="dmaw")
    C.ffn_slot_act = [None, None]
    for qc in range(2):
        scr_free = [E.tok("dve")] * 4
        t_xhat = emit_ln_to_fm(C, xhat, waits=[E.tok("act"), E.tok("pe")])
        x2_done = None
        for hc in range(nic):
            wg, tg = load_slab(C, wg_ap, 0, D, hc * 128, 128, 0)
            wu, tu = load_slab(C, wu_ap, 0, D, hc * 128, 128, 1)
            t_s0 = E.op("dve", lambda v: v.tensor_scalar(
                C.sb["wslab"][:, 0:2048], C.sb["wslab"][:, 0:2048], -128.0, None, ALU.add),
                sig="dve", waits=[tg])
            t_s1 = E.op("dve", lambda v: v.tensor_scalar(
                C.sb["wslab"][:, 2048:4096], C.sb["wslab"][:, 2048:4096], -128.0, None, ALU.add),
                sig="dve", waits=[tu])
            bankG, gfree, gi = C.take_bank()
            t_g = E.op("pe", lambda t, wg=wg, qc=qc, bankG=bankG: [
                t.matmul(bankG[0:128, 0:512], wg(k, 0, 128),
                         xhat[:, k * TOK + qc * 512:k * TOK + (qc + 1) * 512],
                         start=(k == 0), stop=(k == NCH - 1), skip_group_check=True)
                for k in range(NCH)][-1], sig="pe", waits=[t_xhat, t_s0, gfree])
            ssl = (hc % 2) * 2
            t_sg = E.op("act", lambda a, bankG=bankG, ssl=ssl, hc=hc: a.activation(
                C.sb["ytok2"][0:128, ssl * 512:(ssl + 1) * 512], bankG[0:128, 0:512],
                AF.Silu, scale=C.sb["vecs"][0:128, 80 + hc:81 + hc]),
                sig="act", waits=[t_g, scr_free[ssl], t_v])
            C.release_bank(gi, t_sg)
            bankU, ufree, ui = C.take_bank()
            t_u = E.op("pe", lambda t, wu=wu, qc=qc, bankU=bankU: [
                t.matmul(bankU[0:128, 0:512], wu(k, 0, 128),
                         xhat[:, k * TOK + qc * 512:k * TOK + (qc + 1) * 512],
                         start=(k == 0), stop=(k == NCH - 1), skip_group_check=True)
                for k in range(NCH)][-1], sig="pe", waits=[t_xhat, t_s1, ufree])
            t_su = E.op("act", lambda a, bankU=bankU, ssl=ssl, hc=hc: a.activation(
                C.sb["ytok2"][0:128, (ssl + 1) * 512:(ssl + 2) * 512],
                bankU[0:128, 0:512], AF.Identity,
                scale=C.sb["vecs"][0:128, 124 + hc:125 + hc]),
                sig="act", waits=[t_u, scr_free[ssl + 1], t_v])
            x2_done = E.op("dve", lambda v, hc=hc, ssl=ssl: v.tensor_tensor(
                x2[0:128, hc * 512:(hc + 1) * 512],
                C.sb["ytok2"][0:128, ssl * 512:(ssl + 1) * 512],
                C.sb["ytok2"][0:128, (ssl + 1) * 512:(ssl + 2) * 512], ALU.mult),
                sig="dve", waits=[t_su, t_sg])
            C.release_bank(ui, x2_done)
            scr_free[ssl] = x2_done
            scr_free[ssl + 1] = x2_done
            C.wslab_busy[0] = t_g
            C.wslab_busy[1] = t_u
        yfm_done = [None] * 16
        C.wslab_busy["big"] = E.tok("pe")
        for oc in range(NCH):
            wgb, tgb = load_slab(C, wd_ap, 0, hid, oc * 128, 128, "big")
            t_sb = E.op("dve", lambda v: v.tensor_scalar(
                C.arena[0:128, 0:nic * 128], C.arena[0:128, 0:nic * 128], -128.0, None, ALU.add),
                sig="dve", waits=[tgb])
            yfm_done[oc] = mm_evac(
                C, [(C.arena[0:128, k * 128:(k + 1) * 128], x2[:, k * 512:(k + 1) * 512])
                    for k in range(nic)], 128, 512,
                lambda a, bk, oc=oc: a.activation(
                    yfm[0:128, oc * 512:(oc + 1) * 512], bk, AF.Identity,
                    scale=C.sb["vecs"][0:128, 168 + oc:169 + oc]),
                [x2_done, t_sb, t_v], evac_eng="act")
            C.wslab_busy["big"] = yfm_done[oc]
        last_ev = E.tok("pe")
        for c4 in range(4):
            c = qc * 4 + c4
            for g in range(4):
                t_pe = E.op("pe", lambda t, g=g, c4=c4: [
                    t.transpose(C.ptr[:, j * 128:(j + 1) * 128],
                                yfm[0:128, (g * 4 + j) * 512 + c4 * 128:(g * 4 + j) * 512 + c4 * 128 + 128],
                                C.sb["ident"][:, :]) for j in range(4)][-1],
                    sig="pe", waits=[yfm_done[g * 4 + 3], last_ev, C.ptr_free])
                last_ev = E.op("dve", lambda v, g=g: v.tensor_copy(
                    bass.AP(C.sb["ytok2"], g * 4 * 128, [[D, 128], [128, 4], [1, 128]]),
                    C.ptr[:, 0:512].rearrange("p (a b) -> p a b", a=4)),
                    sig="dve", waits=[t_pe])
                C.ptr_free = last_ev
            t = emit_ln2_tile(C, c, C.sb["ytok2"][0:128, 0:D], last_ev, C.ffn_slot_act)
            last_ev = t
    return E.tok("act")



def emit_l1_attn(C):
    E = C.E
    H = 16
    xhat = C.region(0, 16 * TOK)
    obuf = C.region(16384, 16 * TOK)
    mbuf = C.region(32768, 8192)
    hs = 40960
    qhb = C.region(hs, TOK)
    khb = C.region(hs + TOK, TOK)
    ktil = C.region(hs + 2 * TOK, TOK)
    ktok = C.region(hs + 3 * TOK, TOK)
    vfmb = C.region(hs + 4 * TOK, TOK)
    vtok = C.region(hs + 5 * TOK, TOK)
    sgb = C.region(hs + 6 * TOK, TOK)
    gam = C.region(hs + 7 * TOK, TOK)
    pbuf = C.region(49152, 4 * 512)
    sq2 = C.region(49152 + 2048, 512)
    rstdb = C.region(49152 + 2048 + 512, 512)
    mcomb = C.region(49152 + 2048 + 1024, 128)

    t_ropes = E.op("pool", lambda e: e.dma_start(
        out=C.sb["ropes"][:, :], in_=C.inputs_ap["ropes_in"][:, :]),
        sig="dmaw", waits=[(n, E.cnt[n]) for n in ("dve",)])
    t_cv = E.op("pool", lambda e: e.dma_start(
        out=C.sb["cdec"][:, :], in_=C.inputs_ap["cdec_in"][:, :]), sig="dmaw")
    t_om = E.op("pool", lambda e: e.dma_start(
        out=C.sb["omg"][:, :], in_=C.inputs_ap["omg_in"][:, :]), sig="dmaw")
    t_gn = E.op("pool", lambda e: e.dma_start(
        out=C.sb["gnw"][0:1, 0:128], in_=C.inputs_ap["gnw_in"][:, :]), sig="dmaw")

    t_xhat = emit_ln_to_fm(C, xhat, waits=[E.tok("act"), E.tok("pe")])

    wq = C.wio["L1_Wq"]
    wk = C.wio["L1_Wk"]
    wv = C.wio["L1_Wv"]
    wg = C.wio["L1_Wg"]
    scale = float(1.0 / np.sqrt(128.0))

    kh_free = None
    kh_free_d = None
    t_last = None
    pb_free = [None] * 4
    ob_free = None
    C.wslab_busy["a3"] = E.tok("pe")
    for h in range(H):
        slot = h % 2
        t_gam = E.op("pool", lambda e, h=h: e.dma_start(
            out=gam[0:128, 0:TOK], in_=C.inputs_ap["gam_in"][:, h * TOK:(h + 1) * TOK]),
            sig="dmaw", waits=[kh_free, kh_free_d])
        wgk, tk = load_slab(C, wk, 0, D, h * 128, 128, slot)
        t_k = None
        for q2 in range(2):
            t_k = mm_evac(
                C, [(wgk(k, 0, 128), xhat[:, k * TOK + q2 * 512:k * TOK + (q2 + 1) * 512])
                    for k in range(NCH)], 128, 512,
                lambda a, bk, q2=q2: a.activation(
                    khb[0:128, q2 * 512:(q2 + 1) * 512], bk, AF.Identity),
                [t_xhat, tk, kh_free, kh_free_d])
        C.wslab_busy[slot] = t_k
        t_kr = emit_rope_half(C, khb, 0, "rt64", 64, TOK, [t_k, t_ropes])
        t_kt = E.op("dve", lambda v: (
            v.tensor_tensor(ktil[0:128, 0:512], khb[0:128, 0:512], gam[0:128, 512:1024], ALU.mult),
            v.tensor_tensor(ktil[0:128, 512:1024], khb[0:128, 512:1024], gam[0:128, 512:1024], ALU.mult))[-1],
            sig="dve", waits=[t_kr, t_gam, kh_free, kh_free_d])
        t_tr = None
        for g in range(2):
            t_pe = E.op("pe", lambda t, g=g: [
                t.transpose(C.ptr[:, j * 128:(j + 1) * 128],
                            ktil[0:128, (g * 4 + j) * 128:(g * 4 + j + 1) * 128],
                            C.sb["ident"][:, :]) for j in range(4)][-1],
                sig="pe", waits=[t_kt, t_tr, kh_free, C.ptr_free])
            t_tr = E.op("dve", lambda v, g=g: v.tensor_copy(
                ktok[0:128, g * 512:(g + 1) * 512], C.ptr[:, 0:512]),
                sig="dve", waits=[t_pe])
            C.ptr_free = t_tr
        wgv, tv = load_slab(C, wv, 0, D, h * 128, 128, "a3")
        t_v = None
        for q2 in range(2):
            t_v = mm_evac(
                C, [(wgv(k, 0, 128), xhat[:, k * TOK + q2 * 512:k * TOK + (q2 + 1) * 512])
                    for k in range(NCH)], 128, 512,
                lambda a, bk, q2=q2: a.activation(
                    vfmb[0:128, q2 * 512:(q2 + 1) * 512], bk, AF.Identity),
                [t_xhat, tv, kh_free, kh_free_d])
        C.wslab_busy["a3"] = t_v
        t_trv = None
        for g in range(2):
            t_pe = E.op("pe", lambda t, g=g: [
                t.transpose(C.ptr[:, j * 128:(j + 1) * 128],
                            vfmb[0:128, (g * 4 + j) * 128:(g * 4 + j + 1) * 128],
                            C.sb["ident"][:, :]) for j in range(4)][-1],
                sig="pe", waits=[t_v, t_trv, kh_free, C.ptr_free])
            t_trv = E.op("dve", lambda v, g=g: v.tensor_copy(
                vtok[0:128, g * 512:(g + 1) * 512], C.ptr[:, 0:512]),
                sig="dve", waits=[t_pe])
            C.ptr_free = t_trv
        for i in range(2):
            bank, bfree, bi = C.take_bank()
            t_m = E.op("pe", lambda t, i=i, bank=bank: [
                t.matmul(bank[0:128, 0:128],
                         ktok[0:128, i * 512 + j * 128:i * 512 + (j + 1) * 128],
                         vtok[0:128, i * 512 + j * 128:i * 512 + (j + 1) * 128],
                         start=(j == 0), stop=(j == 3), skip_group_check=True)
                for j in range(4)][-1], sig="pe", waits=[t_tr, t_trv, bfree])
            t_mev = E.op("dve", lambda v, i=i, h=h, bank=bank: v.tensor_copy(
                mbuf[0:128, (h * 2 + i) * 128:(h * 2 + i + 1) * 128], bank[0:128, 0:128]),
                sig="dve", waits=[t_m])
            C.release_bank(bi, t_mev)
        fid = C.n_flags
        C.n_flags += 2
        E.wait("sp", "dve", E.cnt["dve"])

        def mwr(e, h=h):
            sem = C.E.sem_objs["dmas"]
            regs = C.E.regs
            src_ap = mbuf[0:128, h * 256:(h + 1) * 256]
            with e.If_eq(regs["parity"], 0):
                e.dma_start(out=C.shared["lat1"].ap()[0:128, h * 256:(h + 1) * 256],
                            in_=src_ap).then_inc(sem, 16)
            with e.Else():
                e.dma_start(out=C.shared["lat1"].ap()[0:128, 4096 + h * 256:4096 + (h + 1) * 256],
                            in_=src_ap).then_inc(sem, 16)
        E.raw("sp", mwr)
        E.cnt["dmas"] += 1
        E.last_wait[("sp", "dmas")] = 0
        E.wait("sp", "dmas", E.cnt["dmas"])

        def mflag(e, fid=fid):
            sem = C.E.sem_objs["dmas"]
            regs = C.E.regs
            with e.If_eq(regs["parity"], 0):
                e.dma_start(out=C.shf.ap()[0:1, fid:fid + 1], in_=C.sb["meta"][0:1, 1:2]).then_inc(sem, 16)
            with e.Else():
                e.dma_start(out=C.shf.ap()[0:1, fid + 1:fid + 2], in_=C.sb["meta"][0:1, 1:2]).then_inc(sem, 16)
        E.raw("sp", mflag)
        E.cnt["dmas"] += 1
        E.last_wait[("sp", "dmas")] = 0
        E.wait("sp", "dmas", E.cnt["dmas"])

        wgq, tq = load_slab(C, wq, 0, D, h * 128, 128, slot)
        t_q = None
        for q2 in range(2):
            t_q = mm_evac(
                C, [(wgq(k, 0, 128), xhat[:, k * TOK + q2 * 512:k * TOK + (q2 + 1) * 512])
                    for k in range(NCH)], 128, 512,
                lambda a, bk, q2=q2: a.activation(
                    qhb[0:128, q2 * 512:(q2 + 1) * 512], bk, AF.Identity),
                [t_xhat, tq, kh_free, kh_free_d])
        t_qr = emit_rope_half(C, qhb, 0, "rt64", 64, TOK, [t_q, t_ropes])
        t_qg = E.op("dve", lambda v: (
            v.tensor_tensor(qhb[0:128, 0:512], qhb[0:128, 0:512], gam[0:128, 0:512], ALU.mult),
            v.tensor_tensor(qhb[0:128, 512:1024], qhb[0:128, 512:1024], gam[0:128, 0:512], ALU.mult))[-1],
            sig="dve", waits=[t_qr, t_gam])
        wgg, tg = load_slab(C, wg, 0, D, h * 128, 128, "a3")
        t_g = None
        for q2 in range(2):
            t_g = mm_evac(
                C, [(wgg(k, 0, 128), xhat[:, k * TOK + q2 * 512:k * TOK + (q2 + 1) * 512])
                    for k in range(NCH)], 128, 512,
                lambda a, bk, q2=q2: a.activation(
                    sgb[0:128, q2 * 512:(q2 + 1) * 512], bk, AF.Silu),
                [t_xhat, tg, kh_free, kh_free_d])
        C.wslab_busy[slot] = t_q
        C.wslab_busy["a3"] = t_g

        def mpoll(e, fid=fid):
            regs = C.E.regs
            fv, tgt, cc = regs["fv"], regs["tgt"], regs["cc"]
            e.reg_mov(cc, 1)
            with e.While(cc):
                with e.If_eq(regs["parity"], 0):
                    e.dma_start(out=C.sb["flags"][0:1, 0:1],
                                in_=C.shf.ap()[0:1, fid + 1:fid + 2]).then_inc(C.E.sem_objs["poll"], 16)
                with e.Else():
                    e.dma_start(out=C.sb["flags"][0:1, 0:1],
                                in_=C.shf.ap()[0:1, fid:fid + 1]).then_inc(C.E.sem_objs["poll"], 16)
                e.reg_add(tgt, tgt, 16)
                e.wait_ge(C.E.sem_objs["poll"], tgt)
                e.reg_load(fv, C.sb["flags"][0:1, 0:1])
                e.reg_alu(cc, fv, regs["nonce"], ALU.not_equal)
        E.raw("sp", mpoll)

        def mrd(e, h=h):
            sem = C.E.sem_objs["dmas"]
            regs = C.E.regs
            dst = mbuf[0:128, 4096 + h * 256:4096 + (h + 1) * 256]
            with e.If_eq(regs["parity"], 0):
                e.dma_start(out=dst, in_=C.shared["lat1"].ap()[0:128, 4096 + h * 256:4096 + (h + 1) * 256]).then_inc(sem, 16)
            with e.Else():
                e.dma_start(out=dst, in_=C.shared["lat1"].ap()[0:128, h * 256:(h + 1) * 256]).then_inc(sem, 16)
        E.raw("sp", mrd)
        E.cnt["dmas"] += 1
        E.last_wait[("sp", "dmas")] = 0
        E.wait("sp", "dmas", E.cnt["dmas"])
        t_mprt = ("dmas", E.cnt["dmas"])

        for qc in range(2):
            t_mc = E.op("dve", lambda v, h=h, qc=qc: (
                v.tensor_scalar(mcomb[0:128, 0:128],
                                mbuf[0:128, 4096 + h * 256:4096 + h * 256 + 128],
                                C.sb["omg"][0:128, h * 8 + qc * 4 + 2:h * 8 + qc * 4 + 3],
                                None, ALU.mult),
                v.scalar_tensor_tensor(mcomb[0:128, 0:128],
                                       mbuf[0:128, 4096 + h * 256 + 128:4096 + (h + 1) * 256],
                                       C.sb["omg"][0:128, h * 8 + qc * 4 + 3:h * 8 + qc * 4 + 4],
                                       mcomb[0:128, 0:128], ALU.mult, ALU.add))[-1],
                sig="dve", waits=[t_mprt, t_om, ob_free])
            tiles = ([(0, j) for j in range(4)] if qc == 0
                     else [(0, j) for j in range(4)] + [(1, j) for j in range(4)])
            nt = len(tiles)
            pend = None
            for ti, (sl, sub) in enumerate(tiles):
                koff = sl * 512 + sub * 128
                cidx = h * 12 + (qc * 4 if qc == 0 else 4) + ti
                sbank, sfree, sbi = C.take_bank()
                t_s = E.op("pe", lambda t, koff=koff, qc=qc, sbank=sbank: t.matmul(
                    sbank[0:128, 0:512], khb[0:128, koff:koff + 128],
                    qhb[0:128, qc * 512:(qc + 1) * 512],
                    start=True, stop=True, skip_group_check=True),
                    sig="pe", waits=[sfree, t_qg, t_kt])
                ps = ti % 4
                t_exp = E.op("act", lambda a, sbank=sbank, ps=ps, cidx=cidx: a.activation(
                    pbuf[0:128, ps * 512:(ps + 1) * 512], sbank[0:128, 0:512],
                    AF.Identity, scale=C.sb["cdec"][0:128, cidx:cidx + 1]),
                    sig="act", waits=[t_s, pb_free[ps], t_cv])
                C.release_bank(sbi, t_exp)
                t_pready = t_exp
                if (qc == 0 and sl == 0) or (qc == 1 and sl == 1):
                    m = sub if qc == 0 else 8 + sub
                    t_pready = E.op("dve", lambda v, ps=ps, m=m: v.tensor_tensor(
                        pbuf[0:128, ps * 512:(ps + 1) * 512],
                        pbuf[0:128, ps * 512:(ps + 1) * 512],
                        C.sb["masks"][:, m * 512:(m + 1) * 512], ALU.mult),
                        sig="dve", waits=[t_exp])
                if pend is not None:
                    pb_free[pend[0]] = emit_av_l1(C, pbuf, vtok, pend[0], pend[2], pend[3],
                                                  first=(pend[1] == 0), last=False)
                pend = (ps, ti, koff, t_pready)
            pb_free[pend[0]] = emit_av_l1(C, pbuf, vtok, pend[0], pend[2], pend[3],
                                          first=(pend[1] == 0), last=False)
            t_cross = E.op("pe", lambda t, qc=qc: t.matmul(
                C.obank[0:128, 0:512], mcomb[0:128, 0:128],
                qhb[0:128, qc * 512:(qc + 1) * 512],
                start=False, stop=True, skip_group_check=True),
                sig="pe", waits=[t_mc, t_qg])
            t_sq = E.op("act", lambda a: a.activation(
                sq2[0:128, 0:512], C.obank[0:128, 0:512], AF.Square),
                sig="act", waits=[t_cross])
            t_sum = E.op("pe", lambda t: t.matmul(
                C.pz[0:1, 0:512], C.sb["ones"][:, 0:1], sq2[0:128, 0:512],
                start=True, stop=True, skip_group_check=True),
                sig="pe", waits=[t_sq, C.pz_free])
            t_ms = E.op("dve", lambda v: v.tensor_scalar(
                C.sb["zrec"][0:1, 0:512], C.pz[0:1, 0:512], 1.0 / 128.0, EPS, ALU.mult, ALU.add),
                sig="dve", waits=[t_sum])
            C.pz_free = t_ms

            def _rs(v):
                with C.nc.allow_low_precision(reason="rms denom bf16; probs bf16"):
                    return v.reciprocal(C.sb["zrec"][0:1, 0:512], C.sb["zrec"][0:1, 0:512])
            t_rc = E.op("dve", _rs, sig="dve", waits=[t_ms])
            t_ln = E.op("act", lambda a: (
                a.activation(C.sb["zrec"][0:1, 0:512], C.sb["zrec"][0:1, 0:512], AF.Ln),
                a.activation(C.sb["zrec"][0:1, 0:512], C.sb["zrec"][0:1, 0:512], AF.Exp, scale=0.5))[-1],
                sig="act", waits=[t_rc])
            bank, bfree, bi = C.take_bank()
            t_bc = E.op("pe", lambda t, bank=bank: t.matmul(
                bank[0:128, 0:512], C.sb["gnw"][0:1, 0:128], C.sb["zrec"][0:1, 0:512],
                start=True, stop=True, skip_group_check=True),
                sig="pe", waits=[t_ln, bfree, t_gn])
            t_rb = E.op("dve", lambda v, bank=bank: v.tensor_copy(
                rstdb[0:128, 0:512], bank[0:128, 0:512]), sig="dve", waits=[t_bc])
            C.release_bank(bi, t_rb)
            t_t1 = E.op("dve", lambda v: v.tensor_tensor(
                sq2[0:128, 0:512], C.obank[0:128, 0:512], rstdb[0:128, 0:512], ALU.mult),
                sig="dve", waits=[t_rb])
            ob_free = t_t1
            t_fin = E.op("dve", lambda v, h=h, qc=qc: v.tensor_tensor(
                obuf[0:128, h * TOK + qc * 512:h * TOK + (qc + 1) * 512],
                sq2[0:128, 0:512], sgb[0:128, qc * 512:(qc + 1) * 512], ALU.mult),
                sig="dve", waits=[t_t1, t_g])
            C.obank_free = t_t1
            t_last = t_fin
        kh_free = E.tok("pe")
        kh_free_d = E.tok("dve")
    emit_out_ln2(C, obuf, C.wio["L1_Wo"], [t_last], obuf_col0=16384, ytok_col0=32768)


def emit_av_l1(C, pbuf, vtok, ps, koff, t_pready, first, last):
    E = C.E
    waits = [t_pready]
    if first:
        waits.append(C.obank_free)
    kt = koff // 128
    t = E.op("pe", lambda t, ps=ps, first=first, kt=kt: t.matmul(
        C.obank[0:128, 0:512], vtok[0:128, kt * 128:(kt + 1) * 128],
        pbuf[0:128, ps * 512:(ps + 1) * 512],
        start=first, stop=False, skip_group_check=True),
        sig="pe", waits=waits)
    if first:
        C.obank_free = None
    return t


def emit_l1_ffn(C):
    E = C.E
    xhat = C.region(0, 16 * TOK)
    ytok = C.region(16384, 8 * D)
    xab = C.region(24576, 512)
    wtl = C.region(25088, 512)
    wmT = C.region(25600, 512)
    l1b = C.region(26112, D)
    lgt = C.region(28160, 512)

    t_gb = E.op("pool", lambda e: e.dma_start(
        out=C.sb["vecs"][0:8, 184:185], in_=C.inputs_ap["L1_gb"][:, :]), sig="dmaw")
    t_rep = E.op("pool", lambda e: e.dma_start(
        out=C.sb["gnw"][0:8, 128:256], in_=C.inputs_ap["rep_in"][:, :]), sig="dmaw")

    C.ffn_slot_act = [None, None]
    C.part_guards = [E.tok("pe"), E.tok("dve"), E.tok("act"), E.tok("dmas")]
    for qc in range(2):
        t_xhat = emit_ln_to_fm(C, xhat, waits=[E.tok("act"), E.tok("pe")])
        wgt, tgt_ = load_slab(C, C.wio["L1_gw"], 0, D, 0, 8, 0)
        t_lg = mm_evac(
            C, [(wgt(k, 0, 8), xhat[:, k * TOK + qc * 512:k * TOK + (qc + 1) * 512])
                for k in range(NCH)], 8, 512,
            lambda a, bk: a.activation(lgt[0:8, 0:512], bk, AF.Identity,
                                       bias=C.sb["vecs"][0:8, 184:185]),
            [t_xhat, tgt_, t_gb], evac_eng="act")
        C.wslab_busy[0] = t_lg
        wga, ta = load_slab(C, C.wio["L1_A"], 0, D, 0, 128, 1)
        t_xa = mm_evac(
            C, [(wga(k, 0, 128), xhat[:, k * TOK + qc * 512:k * TOK + (qc + 1) * 512])
                for k in range(NCH)], 128, 512,
            lambda a, bk: a.activation(xab[0:128, 0:512], bk, AF.Identity),
            [t_xhat, ta], evac_eng="act")
        C.wslab_busy[1] = t_xa
        st = C.sb["stats"]
        y2 = C.sb["ytok2"]
        t_wm = None
        for c4 in range(4):
            t_tp = E.op("pe", lambda t, c4=c4: t.transpose(
                C.ptr[:, 0:8], lgt[0:8, c4 * 128:(c4 + 1) * 128],
                C.sb["ident"][0:8, 0:8]),
                sig="pe", waits=[t_lg, t_wm, C.ptr_free])
            ta_ = E.op("dve", lambda v, st=st: v.reduce_max(
                st[:, 0:1], C.ptr[0:128, 0:8], axis=AX.X), sig="dve", waits=[t_tp])
            tb_ = E.op("dve", lambda v, st=st: v.tensor_scalar(
                st[:, 8:16], C.ptr[0:128, 0:8], st[:, 0:1], None, ALU.is_equal),
                sig="dve", waits=[ta_])
            tc_ = E.op("dve", lambda v, st=st: v.scalar_tensor_tensor(
                st[:, 16:24], st[:, 8:16], -1e30, C.ptr[0:128, 0:8],
                ALU.mult, ALU.add), sig="dve", waits=[tb_])
            td_ = E.op("dve", lambda v, st=st: v.reduce_max(
                st[:, 1:2], st[:, 16:24], axis=AX.X), sig="dve", waits=[tc_])
            te_ = E.op("dve", lambda v, st=st: v.tensor_tensor(
                st[:, 2:3], st[:, 0:1], st[:, 1:2], ALU.subtract),
                sig="dve", waits=[td_])
            C.ptr_free = tc_
            t_sg2 = E.op("act", lambda a, st=st: a.activation(
                st[:, 3:4], st[:, 2:3], AF.Sigmoid), sig="act", waits=[te_])
            tf_ = E.op("dve", lambda v, st=st: v.tensor_scalar(
                st[:, 4:5], st[:, 3:4], -1.0, 1.0, ALU.mult, ALU.add),
                sig="dve", waits=[t_sg2, te_])
            tg2_ = E.op("dve", lambda v, st=st: v.tensor_scalar(
                st[:, 24:32], st[:, 16:24], st[:, 1:2], None, ALU.is_equal),
                sig="dve", waits=[tf_])
            th_ = E.op("dve", lambda v, st=st: v.tensor_scalar(
                st[:, 8:16], st[:, 8:16], st[:, 3:4], None, ALU.mult),
                sig="dve", waits=[tg2_])
            ti_ = E.op("dve", lambda v, st=st: v.tensor_scalar(
                st[:, 24:32], st[:, 24:32], st[:, 4:5], None, ALU.mult),
                sig="dve", waits=[th_])
            t_d2 = E.op("dve", lambda v, st=st, y2=y2: v.tensor_tensor(
                y2[0:128, 0:8], st[:, 8:16], st[:, 24:32], ALU.add),
                sig="dve", waits=[ti_, t_wm])
            t_tb = E.op("pe", lambda t, y2=y2: t.transpose(
                C.ptr[0:8, 8:136], y2[0:128, 0:8], C.sb["ident"][:, :]),
                sig="pe", waits=[t_d2, C.ptr_free])
            t_wm = E.op("dve", lambda v, c4=c4: v.tensor_copy(
                wmT[0:8, c4 * 128:(c4 + 1) * 128], C.ptr[0:8, 8:136]),
                sig="dve", waits=[t_tb])
            C.ptr_free = t_wm
        bank, bfree, bi = C.take_bank()
        t_wt = E.op("pe", lambda t, bank=bank: t.matmul(
            bank[0:128, 0:512], C.sb["gnw"][0:8, 128:256], wmT[0:8, 0:512],
            start=True, stop=True, skip_group_check=True),
            sig="pe", waits=[t_wm, t_rep, bfree])
        t_wtl = E.op("dve", lambda v, bank=bank: v.tensor_copy(
            wtl[0:128, 0:512], bank[0:128, 0:512]), sig="dve", waits=[t_wt])
        C.release_bank(bi, t_wtl)
        t_u = E.op("dve", lambda v: v.tensor_tensor(
            xab[0:128, 0:512], xab[0:128, 0:512], wtl[0:128, 0:512], ALU.mult),
            sig="dve", waits=[t_wtl, t_xa])
        wgb, tb = load_slab(C, C.wio["L1_B"], 0, 128, 0, 2048, "l1b2",
                            extra=C.part_guards if qc == 0 else ())
        t_bb = E.op("pool", lambda e: e.dma_start(
            out=l1b[0:1, 0:D], in_=C.inputs_ap["L1_b"][:, :]), sig="dmaw")
        C.wslab_busy["l1b2"] = tb
        tile_done = [None] * 4
        for dg in range(4):
            wgw, tw = load_slab(C, C.wio["L1_W"], 0, D, dg * 512, 512, "l1w",
                                extra=C.part_guards if qc == 0 and dg == 0 else ())
            for c4 in range(4):
                pairs = [(xhat[:, k * TOK + qc * 512 + c4 * 128:k * TOK + qc * 512 + c4 * 128 + 128],
                          wgw(k, 0, 512)) for k in range(NCH)]
                pairs.append((xab[0:128, c4 * 128:(c4 + 1) * 128], wgb(0, dg * 512, (dg + 1) * 512)))
                pairs.append((C.sb["ones"][0:1, 0:128], l1b[0:1, dg * 512:(dg + 1) * 512]))
                tile_done[c4] = mm_evac(
                    C, pairs, 128, 512,
                    lambda a, bk, c4=c4, dg=dg: a.activation(
                        ytok[0:128, c4 * D + dg * 512:c4 * D + dg * 512 + 512], bk, AF.Identity),
                    [t_u, tw, tb, t_bb, tile_done[c4]], evac_eng="act")
            C.wslab_busy["l1w"] = tile_done[3]
        for c4 in range(4):
            c = qc * 4 + c4
            t = emit_ln2_tile(C, c, ytok[0:128, c4 * D:(c4 + 1) * D], tile_done[c4], C.ffn_slot_act)
            tile_done[c4] = t
    return E.tok("act")


WSLAB_SMALL = [0, 2048]
WSLAB_W = 4096


def build_program(layers, parts=("attn", "ffn"), dumps=()):
    C = Ctx(layers)
    C.parts = parts
    C.dbg_dumps = list(dumps)
    nc, E = C.nc, C.E

    for s in ("pe", "act", "dve", "pool"):
        E.sem(s, 1)
    E.sem("dmas", 16)
    E.sem("dmaw", 16)
    E.sem("poll", 16)

    x_in = C.din("x_in", [TOK, D], F32)
    C.din("meta_in", [1, 4], I32)
    C.din("ident_in", [128, 128], BF16)
    C.din("masks_in", [16 * 128, 512], mybir.dt.uint8)
    C.din("ropes_in", [128, 4 * TOK], BF16)
    y_out = C.dout("y", [TOK, D], F32)
    C.wio = {}
    if 0 in layers:
        C.wio["L0_Wqa"] = C.din("L0_Wqa", [D, 512], F32)
        C.wio["L0_Wqb"] = C.din("L0_Wqb", [512, 3072], F32)
        C.wio["L0_Wkva"] = C.din("L0_Wkva", [D, 576], F32)
        C.wio["L0_Wkvb"] = C.din("L0_Wkvb", [512, 4096], F32)
        C.wio["L0_Wo"] = C.din("L0_Wo", [D, D], F32)
        C.wio["L0_wupq"] = C.din("L0_wupq", [8192, D], I32)
        C.wio["L0_wdnq"] = C.din("L0_wdnq", [D, 8192], I32)
        C.din("L0_wupsc", [8192, 32], F32)
        C.din("L0_wdnsc", [D, 128], F32)
        C.din("L0_bup", [128, 64], F32)
        C.din("L0_bdn", [128, 16], F32)
        C.wio["L0_ua"] = C.din("L0_ua", [D, 16], F32)
        C.wio["L0_ub"] = C.din("L0_ub", [16, 8192], F32)
        C.wio["L0_da"] = C.din("L0_da", [8192, 16], F32)
        C.wio["L0_db"] = C.din("L0_db", [16, D], F32)
    if 1 in layers:
        for nm, shp in (("L1_Wq", [D, D]), ("L1_Wk", [D, D]), ("L1_Wv", [D, D]),
                        ("L1_Wg", [D, D]), ("L1_Wo", [D, D]), ("L1_W", [D, D]),
                        ("L1_A", [D, 128]), ("L1_B", [128, D]), ("L1_gw", [D, 8])):
            C.wio[nm] = C.din(nm, shp, F32)
        C.din("L1_b", [1, D], F32)
        C.din("L1_gb", [8, 1], F32)
        C.din("gnw_in", [1, 128], BF16)
        C.din("rep_in", [8, 128], BF16)
        C.din("gam_in", [128, 16 * TOK], BF16)
        C.din("cdec_in", [128, 192], F32)
        C.din("omg_in", [128, 128], F32)
    if 3 in layers:
        C.wio["L3_Wqd"] = C.din("L3_Wqd", [D, 1024], F32)
        C.wio["L3_Wqu"] = C.din("L3_Wqu", [1024, 1024], F32)
        C.wio["L3_Wqr"] = C.din("L3_Wqr", [1024, 1024], F32)
        C.wio["L3_Wkvd"] = C.din("L3_Wkvd", [D, 512], F32)
        C.wio["L3_Wku"] = C.din("L3_Wku", [512, 1024], F32)
        C.wio["L3_Wkr"] = C.din("L3_Wkr", [D, 1024], F32)
        C.wio["L3_Wvu"] = C.din("L3_Wvu", [512, 2048], F32)
        C.wio["L3_Wo"] = C.din("L3_Wo", [D, D], F32)
        C.wio["L3_wg"] = C.din("L3_wg", [D, 5632], I32)
        C.wio["L3_wu2"] = None
        C.wio["L3_wu"] = C.din("L3_wu", [D, 5632], I32)
        C.wio["L3_wd"] = C.din("L3_wd", [5632, D], I32)
        C.din("L3_sg", [128, 44], F32)
        C.din("L3_su", [128, 44], F32)
        C.din("L3_sd", [128, 16], F32)
        C.din("ropes3_in", [128, 2 * TOK], BF16)
    if 2 in layers:
        C.wio["L2_Wqa"] = C.din("L2_Wqa", [D, 1536], F32)
        C.wio["L2_Wqb"] = C.din("L2_Wqb", [1536, D], F32)
        C.wio["L2_Wkva"] = C.din("L2_Wkva", [D, 576], F32)
        C.wio["L2_Wkvb"] = C.din("L2_Wkvb", [512, 3072], F32)
        C.wio["L2_Wo"] = C.din("L2_Wo", [D, D], F32)
        C.wio["L2_gu"] = C.din("L2_gu", [D, 16384], F32)
        C.wio["L2_dn"] = C.din("L2_dn", [8192, D], F32)
    C.shared = {}
    for lx in layers:
        if lx in (0, 2):
            C.shared[f"lat{lx}"] = nc.dram_tensor(f"sh_lat{lx}", [5 * 128, S], BF16,
                                                  addr_space="Shared")
        elif lx == 1:
            C.shared["lat1"] = nc.dram_tensor("sh_lat1", [128, 8192], BF16,
                                              addr_space="Shared")
        elif lx == 3:
            C.shared["lat3"] = nc.dram_tensor("sh_lat3", [12 * 128, S], BF16,
                                              addr_space="Shared")
    C.shf = nc.dram_tensor("sh_flags", [1, 64], I32, addr_space="Shared")

    C.sbuf("x_tok", [128, 8 * D], F32)
    C.arena = C.sbuf("arena", [128, AW], BF16)
    C.sbuf("wslab", [128, WSLAB_W], BF16)
    C.sbuf("masks", [128, 16 * 512], mybir.dt.uint8)
    C.sbuf("ropes", [128, 4 * TOK], BF16)
    C.scr_tok = C.sbuf("ytok2", [128, 2 * TOK], BF16)
    C.sbuf("stats", [128, 64], F32)
    C.sbuf("zrec", [1, 512], BF16)
    C.sbuf("ones", [128, 128], BF16)
    C.sbuf("ident", [128, 128], BF16)
    C.sbuf("vecs", [128, 192], F32)
    C.sbuf("flags", [1, 2], I32)
    if 1 in layers:
        C.sbuf("gnw", [8, 256], BF16)
        C.sbuf("cdec", [128, 192], F32)
        C.sbuf("omg", [128, 128], F32)

    C.sbuf("meta", [1, 4], I32)

    C.banks = [nc.alloc_psum_tensor(f"pb{i}", [128, 512], F32) for i in range(4)]
    C.bank_free = [None] * 4
    C.bank_rr = 0
    C.obank = nc.alloc_psum_tensor("obank", [128, 512], F32)
    C.pz = nc.alloc_psum_tensor("pz", [128, 512], F32)
    C.ptr = nc.alloc_psum_tensor("ptr", [128, 1024], BF16)
    C.obank_free = None
    C.pz_free = None
    C.ptr_free = None

    ws = C.sb["wslab"]
    C.wslab_base = {0: (ws, WSLAB_SMALL[0], WSLAB_W), 1: (ws, WSLAB_SMALL[1], WSLAB_W),
                    "a2": (C.arena, 14336, AW), "big": (C.arena, 0, AW),
                    "a0": (C.arena, 49152, AW), "a1": (C.arena, 49152 + 2048, AW),
                    "a3": (C.arena, 52352, AW),
                    "l1w": (C.arena, 28672, AW), "l1b2": (C.arena, 36864, AW)}
    C.wslab_busy = {0: None, 1: None, "a2": None, "big": None, "a0": None,
                    "a1": None, "a3": None, "l1w": None, "l1b2": None}

    def alloc_regs(e):
        for r in ("parity", "nonce", "fv", "tgt", "cc"):
            E.regs[r] = e.alloc_register(r)
    E.raw("sp", alloc_regs)
    E.op("sp", lambda e: e.dma_start(out=C.sb["meta"][0:1, 0:4],
                                     in_=C.inputs_ap["meta_in"][0:1, 0:4]), sig="dmas")
    t_meta = E.tok("dmas")

    def load_regs(e):
        e.reg_load(E.regs["parity"], C.sb["meta"][0:1, 0:1])
        e.reg_load(E.regs["nonce"], C.sb["meta"][0:1, 1:2])
        e.reg_mov(E.regs["tgt"], 0)
    E.wait("sp", "dmas", t_meta[1])
    E.raw("sp", load_regs)
    t_x = E.op("sp", lambda e: e.dma_start(
        out=C.sb["x_tok"][:, :], in_=x_in.rearrange("(c p) f -> p c f", p=128)),
        sig="dmas")
    E.op("pool", lambda e: e.dma_start(
        out=C.sb["ident"][:, :], in_=C.inputs_ap["ident_in"][:, :]), sig="dmaw")
    E.op("pool", lambda e: e.dma_start(
        out=C.sb["ropes"][:, :], in_=C.inputs_ap["ropes_in"][:, :]), sig="dmaw")
    E.op("pool", lambda e: e.dma_start(
        out=C.sb["masks"][:, :].rearrange("p (t q) -> p t q", t=16),
        in_=C.inputs_ap["masks_in"].rearrange("(t k) q -> k t q", k=128)), sig="dmaw")
    t_ldconst = E.tok("dmaw")
    t_ones = E.op("dve", lambda v: v.memset(C.sb["ones"][:, :], 1.0), sig="dve")
    C.boot = [t_meta, t_x, t_ldconst, t_ones, E.tok("dmas")]

    E.wait("dve", "dmas", t_x[1])
    E.wait("dve", "dmaw", t_ldconst[1])
    E.wait("pe", "dmaw", t_ldconst[1])
    E.wait("act", "dmas", t_x[1])

    if "lnonly" in C.parts:
        xhat = C.region(0, 16 * TOK)
        emit_ln_to_fm(C, xhat, waits=[E.tok("act")])
    for lx in layers:
        if lx == 2:
            if "attn" in C.parts:
                emit_mla_attn(C, 2, q_lora=1536, d_nope=64, d_rope=64, d_v=128,
                              wqa_ap=C.wio["L2_Wqa"], wqb_ap=C.wio["L2_Wqb"],
                              wkva_ap=C.wio["L2_Wkva"], wkvb_ap=C.wio["L2_Wkvb"],
                              wo_ap=C.wio["L2_Wo"])
            if "ffn" in C.parts:
                emit_swiglu_ffn(C, C.wio["L2_gu"], C.wio["L2_dn"], inter=8192)
        elif lx == 0:
            if "attn" in C.parts:
                emit_mla_attn(C, 0, q_lora=512, d_nope=128, d_rope=64, d_v=128,
                              wqa_ap=C.wio["L0_Wqa"], wqb_ap=C.wio["L0_Wqb"],
                              wkva_ap=C.wio["L0_Wkva"], wkvb_ap=C.wio["L0_Wkvb"],
                              wo_ap=C.wio["L0_Wo"])
            if "ffn" in C.parts:
                emit_l0_ffn(C)
        elif lx == 1:
            if "attn" in C.parts:
                emit_l1_attn(C)
            if "ffn" in C.parts:
                emit_l1_ffn(C)
        elif lx == 3:
            if "attn" in C.parts:
                emit_l3_attn(C)
            if "ffn" in C.parts:
                emit_l3_ffn(C)
        else:
            raise NotImplementedError(lx)

    E.wait("sp", "act", E.cnt["act"])
    E.wait("sp", "dve", E.cnt["dve"])
    E.wait("sp", "pe", E.cnt["pe"])
    for (dnm, col0, rows, cols) in C.dbg_dumps:
        dap = C.dout("dbg_" + dnm, [rows, cols], BF16)
        E.op("sp", lambda e, dap=dap, col0=col0, rows=rows, cols=cols: e.dma_start(
            out=dap, in_=C.arena[0:rows, col0:col0 + cols]), sig="dmas")
    t_st = E.op("sp", lambda e: e.dma_start(
        out=y_out.rearrange("(c p) f -> p c f", p=128), in_=C.sb["x_tok"][:, :]),
        sig="dmas")
    E.wait("sp", "dmas", t_st[1])

    sem_ctxs = []
    for name in list(E.cnt.keys()):
        h = nc.semaphore(name)
        obj = h.__enter__()
        sem_ctxs.append(h)
        E.sem_objs[name] = obj
    C.sem_ctxs = sem_ctxs
    with nc.Block() as block:
        block.sync(lambda e: [fn(e) for fn in E.code["sp"]])
        block.tensor(lambda e: [fn(e) for fn in E.code["pe"]])
        block.scalar(lambda e: [fn(e) for fn in E.code["act"]])
        block.vector(lambda e: [fn(e) for fn in E.code["dve"]])
        block.gpsimd(lambda e: [fn(e) for fn in E.code["pool"]])
    return C


def _interleave_to_halfsplit_cols(w, col0, d):
    perm = np.concatenate([np.arange(0, d, 2), np.arange(1, d, 2)])
    w = w.copy()
    w[:, col0:col0 + d] = w[:, col0 + perm]
    return w


def _perm_qhead_cols(wqb, H, dh, d_nope, d_rope):
    pe = np.concatenate([d_nope + np.arange(0, d_rope, 2), d_nope + np.arange(1, d_rope, 2)])
    perm1 = np.concatenate([pe, np.arange(d_nope)])
    out = np.empty_like(wqb)
    for h in range(H):
        out[:, h * dh:(h + 1) * dh] = wqb[:, h * dh + perm1]
    return out


def host_prepare(inputs, layers, seed_extra=0):
    rng = np.random.default_rng()
    nonce = int(rng.integers(1 << 20, 1 << 30))
    x0 = np.asarray(inputs["L0_attn_x"], dtype=np.float32)
    per_core_common = {}
    if 0 in layers:
        wqb = _perm_qhead_cols(_cf(inputs["L0_attn_Wqb"]), 16, 192, 128, 64)
        wkva = _cf(inputs["L0_attn_Wkva"])
        pe_perm = np.concatenate([512 + np.arange(0, 64, 2), 512 + np.arange(1, 64, 2)])
        wkva = np.concatenate([wkva[:, :512], wkva[:, pe_perm]], axis=1)
        per_core_common.update({
            "L0_Wqa": _cf(inputs["L0_attn_Wqa"]),
            "L0_Wqb": _cf(wqb),
            "L0_Wkva": _cf(wkva),
            "L0_Wkvb": _cf(inputs["L0_attn_Wkvb"]),
            "L0_Wo": _cf(inputs["L0_attn_Wo"]),
            "L0_wupq": np.ascontiguousarray(np.asarray(inputs["L0_ffn_w_up_q"], dtype=np.int32)),
            "L0_wdnq": np.ascontiguousarray(np.asarray(inputs["L0_ffn_w_down_q"], dtype=np.int32)),
            "L0_wupsc": _cf(inputs["L0_ffn_w_up_scale"]),
            "L0_wdnsc": _cf(inputs["L0_ffn_w_down_scale"]),
            "L0_bup": _cf(np.asarray(inputs["L0_ffn_b_up"]).reshape(64, 128).T),
            "L0_bdn": _cf(np.asarray(inputs["L0_ffn_b_down"]).reshape(16, 128).T),
            "L0_ua": _cf(inputs["L0_ffn_w_up_lora_a"]),
            "L0_ub": _cf(inputs["L0_ffn_w_up_lora_b"]),
            "L0_da": _cf(inputs["L0_ffn_w_down_lora_a"]),
            "L0_db": _cf(inputs["L0_ffn_w_down_lora_b"]),
        })
    if 1 in layers:
        H16 = 16
        la = np.asarray(inputs["L1_ffn_lora_A"], dtype=np.float32)
        lb = np.asarray(inputs["L1_ffn_lora_B"], dtype=np.float32)
        Ahat = np.ascontiguousarray(la.reshape(128, D).T)
        Bhat = np.ascontiguousarray(np.transpose(lb, (0, 2, 1)).reshape(128, D))
        rep = np.zeros((8, 128), np.float32)
        for e8 in range(8):
            rep[e8, e8 * 16:(e8 + 1) * 16] = 1.0
        gammas = 1.0 - 2.0 ** (-5.0 - np.arange(H16, dtype=np.float64))
        gam = np.zeros((128, 16 * TOK), np.float64)
        for h in range(H16):
            gam[:, h * TOK:h * TOK + 512] = gammas[h] ** np.arange(512)[None, :]
            gam[:, h * TOK + 512:(h + 1) * TOK] = gammas[h] ** (511 - np.arange(512))[None, :]
        sc128 = 1.0 / np.sqrt(128.0)
        per_core_common.update({
            "L1_Wq": _cf(inputs["L1_attn_Wq"]), "L1_Wk": _cf(inputs["L1_attn_Wk"]),
            "L1_Wv": _cf(inputs["L1_attn_Wv"]), "L1_Wg": _cf(inputs["L1_attn_Wg"]),
            "L1_Wo": _cf(inputs["L1_attn_Wo"]),
            "L1_W": _cf(np.asarray(inputs["L1_ffn_W"]).T),
            "L1_A": _cf(Ahat), "L1_B": _cf(Bhat),
            "L1_gw": _cf(np.asarray(inputs["L1_ffn_gate_w"]).T),
            "L1_b": _cf(np.asarray(inputs["L1_ffn_b"]).reshape(1, D)),
            "L1_gb": _cf(np.asarray(inputs["L1_ffn_gate_b"]).reshape(8, 1)),
            "gnw_in": _cbf(np.asarray(inputs["L1_attn_g_norm_w"]).reshape(1, 128)),
            "rep_in": _cbf(rep),
            "gam_in": _cbf(gam),
        })
    if 3 in layers:
        pperm = np.concatenate([np.concatenate([
            2 * k * 64 + np.arange(32), (2 * k + 1) * 64 + np.arange(32),
            2 * k * 64 + 32 + np.arange(32), (2 * k + 1) * 64 + 32 + np.arange(32)])
            for k in range(8)])
        per_core_common.update({
            "L3_Wqd": _cf(inputs["L3_attn_Wq_down"]),
            "L3_Wqu": _cf(inputs["L3_attn_Wq_up"]),
            "L3_Wqr": _cf(np.asarray(inputs["L3_attn_Wq_rope"])[:, pperm]),
            "L3_Wkvd": _cf(inputs["L3_attn_Wkv_down"]),
            "L3_Wku": _cf(inputs["L3_attn_Wk_up"]),
            "L3_Wkr": _cf(np.asarray(inputs["L3_attn_Wk_rope"])[:, pperm]),
            "L3_Wvu": _cf(inputs["L3_attn_Wv_up"]),
            "L3_Wo": _cf(inputs["L3_attn_Wo"]),
            "L3_wg": np.ascontiguousarray(np.asarray(inputs["L3_ffn_wq_gate"], dtype=np.int32).T),
            "L3_wu": np.ascontiguousarray(np.asarray(inputs["L3_ffn_wq_up"], dtype=np.int32).T),
            "L3_wd": np.ascontiguousarray(np.asarray(inputs["L3_ffn_wq_down"], dtype=np.int32).T),
            "L3_sg": _cf(np.asarray(inputs["L3_ffn_s_gate"]).reshape(44, 128).T),
            "L3_su": _cf(np.asarray(inputs["L3_ffn_s_up"]).reshape(44, 128).T),
            "L3_sd": _cf(np.asarray(inputs["L3_ffn_s_down"]).reshape(16, 128).T),
        })
    if 2 in layers:
        wqb = _perm_qhead_cols(_cf(inputs["L2_attn_Wqb"]), 16, 128, 64, 64)
        wkva = _cf(inputs["L2_attn_Wkva"])
        pe_perm = np.concatenate([512 + np.arange(0, 64, 2), 512 + np.arange(1, 64, 2)])
        wkva = np.concatenate([wkva[:, :512], wkva[:, pe_perm]], axis=1)
        per_core_common.update({
            "L2_Wqa": _cf(inputs["L2_attn_Wqa"]),
            "L2_Wqb": _cf(wqb),
            "L2_Wkva": _cf(wkva),
            "L2_Wkvb": _cf(inputs["L2_attn_Wkvb"]),
            "L2_Wo": _cf(inputs["L2_attn_Wo"]),
            "L2_gu": _cf(np.ascontiguousarray(np.asarray(inputs["L2_ffn_w_gate_up"]).T)),
            "L2_dn": _cf(np.ascontiguousarray(np.asarray(inputs["L2_ffn_w_down"]).T)),
        })
    ident = np.eye(128, dtype=np.float32).astype(BF)
    in_maps = []
    for core in range(8):
        p, e = divmod(core, 2)
        pos = own_positions(e)
        m = dict(per_core_common)
        m["x_in"] = _cf(x0[p][pos])
        m["meta_in"] = np.array([[e, nonce, 0, 0]], dtype=np.int32)
        m["ident_in"] = ident
        m["masks_in"] = build_masks(e).astype(np.uint8).reshape(16 * 128, 512)
        c32, s32 = rope_tables("interleaved", 32, pos)
        c64, s64 = rope_tables("l1", 64, pos, d_full=128)
        ropes = np.zeros((128, 4 * TOK), np.float32)
        ropes[:, 0:TOK] = np.concatenate([c64, s64], axis=0)
        ropes[:, TOK:2 * TOK] = np.concatenate([s64, c64], axis=0)
        ropes[0:64, 2 * TOK:3 * TOK] = np.concatenate([c32, s32], axis=0)
        ropes[0:64, 3 * TOK:4 * TOK] = np.concatenate([s32, c32], axis=0)
        m["ropes_in"] = ropes.astype(BF)
        if 1 in layers:
            gammas = 1.0 - 2.0 ** (-5.0 - np.arange(16, dtype=np.float64))
            sc = 1.0 / np.sqrt(128.0)
            own = OWN_BLOCKS[e]
            partner = OWN_BLOCKS[1 - e]
            cdec = np.zeros((128, 192), np.float64)
            for h in range(16):
                g = gammas[h]
                pidx = np.arange(128)
                for ti in range(12):
                    if ti < 4:
                        E_ = -(128.0 * ti + pidx)
                    elif ti < 8:
                        E_ = 512.0 * (own[1] - own[0]) - (128.0 * (ti - 4) + pidx)
                    else:
                        E_ = -(128.0 * (ti - 8) + pidx)
                    cdec[:, h * 12 + ti] = (g ** E_) * sc
            m["cdec_in"] = _cf(cdec)
            omg = np.zeros((128, 128), np.float64)
            blocks = [own[0], own[1], partner[0], partner[1]]
            for h in range(16):
                g = gammas[h]
                for qc in range(2):
                    qb = own[qc]
                    for i in (2, 3):
                        ib = blocks[i]
                        if ib < qb:
                            omg[:, h * 8 + qc * 4 + i] = (g ** (512.0 * (qb - ib) - 511.0)) * sc
            m["omg_in"] = _cf(omg)
        if 3 in layers:
            c3, s3 = rope_tables("l3", 32, pos)
            r3 = np.zeros((128, 2 * TOK), np.float32)
            r3[:, 0:TOK] = np.concatenate([c3, c3, s3, s3], axis=0)
            r3[:, TOK:2 * TOK] = np.concatenate([s3, s3, c3, c3], axis=0)
            m["ropes3_in"] = r3.astype(BF)
        in_maps.append(m)
    return in_maps


def run_layers(inputs, layers, parts=("attn", "ffn"), dumps=(), ret_raw=False):
    in_maps = host_prepare(inputs, layers)
    C = build_program(layers, parts=parts, dumps=dumps)
    res = run_bass_kernel_spmd(C.nc, in_maps, core_ids=list(range(8)))
    out = np.zeros((4, S, D), dtype=np.float32)
    for core in range(8):
        p, e = divmod(core, 2)
        pos = own_positions(e)
        out[p][pos] = res.results[core]["y"]
    if ret_raw:
        return out, res
    return out


def kernel(**inputs):
    return run_layers(inputs, [0, 1, 2, 3])


# version 03 (final)
# speedup vs baseline: 1.0040x; 1.0040x over previous
import numpy as np
import ml_dtypes
import concourse.bass as bass
import concourse.mybir as mybir
from concourse.bass_utils import run_bass_kernel_spmd

F32 = mybir.dt.float32
BF16 = mybir.dt.bfloat16
I32 = mybir.dt.int32
I8 = mybir.dt.int8
U8 = mybir.dt.uint8
AF = mybir.ActivationFunctionType
ALU = mybir.AluOpType
AX = mybir.AxisListType

S = 2048
D = 2048
TOK = 1024
BLK = 512
EPS = 1e-5
NCH = D // 128

OWN_BLOCKS = [[0, 3], [1, 2]]
SCHED = [[0, 2], [0, 1, 2, 3]]
MASKED_POS = [(0, 0), (0, 2), (1, 1), (1, 3)]

BF = ml_dtypes.bfloat16
ENGS = ("pe", "act", "dve", "pool", "sp")
AW = 55296


def _cf(x):
    return np.ascontiguousarray(x, dtype=np.float32)


def _cbf(x):
    return np.ascontiguousarray(np.asarray(x, dtype=np.float32).astype(BF))


def own_positions(parity):
    return np.concatenate([np.arange(b * BLK, (b + 1) * BLK) for b in OWN_BLOCKS[parity]])


def slot_blocks(parity):
    own, par = OWN_BLOCKS[parity], OWN_BLOCKS[1 - parity]
    return [own[0], own[1], par[0], par[1]]


def rope_tables(style, half, positions, d_full=None):
    p = positions.astype(np.float64)
    if style == "interleaved":
        inv = 10000.0 ** (-np.arange(half, dtype=np.float64) / half)
    elif style == "l1":
        inv = 10000.0 ** (-2.0 * np.arange(half, dtype=np.float64) / d_full)
    elif style == "l3":
        inv = 10000.0 ** (-np.arange(half, dtype=np.float64) / half)
    ang = inv[:, None] * p[None, :]
    return np.cos(ang), np.sin(ang)


def build_masks(parity):
    sb = slot_blocks(parity)
    own = OWN_BLOCKS[parity]
    tiles = []
    for qc, sl in MASKED_POS:
        qblk = own[qc]
        kblk = sb[sl]
        for sub in range(4):
            k_abs = kblk * BLK + sub * 128 + np.arange(128)
            q_abs = qblk * BLK + np.arange(BLK)
            tiles.append((k_abs[:, None] <= q_abs[None, :]).astype(np.float32))
    return np.stack(tiles)


class Em:
    def __init__(self):
        self.code = {e: [] for e in ENGS}
        self.cnt = {}
        self.step = {}
        self.last_wait = {}
        self.sem_objs = {}
        self.regs = {}
        self.dmaw_rr = 0

    def sem(self, name, step=1):
        self.cnt[name] = 0
        self.step[name] = step
        return name

    def op(self, eng, fn, sig=None, waits=()):
        if sig == "dmaw":
            sig = "dmaw%d" % (self.dmaw_rr % 4)
            self.dmaw_rr += 1
            if self.cnt[sig]:
                self.wait(eng, sig, self.cnt[sig])
        for w in waits:
            if w is not None:
                self.wait(eng, w[0], w[1])
        if sig is not None:
            self.cnt[sig] += 1

        def closure(e, fn=fn, sig=sig):
            inst = fn(e)
            if sig is not None:
                inst.then_inc(self.sem_objs[sig], self.step[sig])
        self.code[eng].append(closure)
        if sig == "dmas":
            self.wait(eng, sig, self.cnt[sig])
        return self.tok(sig) if sig is not None else None

    def raw(self, eng, fn):
        self.code[eng].append(fn)

    def wait(self, eng, name, count):
        assert count <= self.cnt[name], (name, count, self.cnt[name])
        key = (eng, name)
        if self.last_wait.get(key, 0) >= count:
            return
        self.last_wait[key] = count
        step = self.step[name]

        def fn(e, name=name, count=count, step=step):
            e.wait_ge(self.sem_objs[name], count * step)
        self.code[eng].append(fn)

    def tok(self, name):
        return (name, self.cnt[name])


class Buf:

    def __init__(self, h, W, col0, cols, row0=0, rows=128):
        self.h, self.W, self.col0, self.cols = h, W, col0, cols
        self.row0, self.rows = row0, rows

    def __getitem__(self, key):
        r, c = key
        r0 = self.row0 + (r.start or 0)
        r1 = self.row0 + (r.stop if r.stop is not None else self.rows)
        c0 = self.col0 + (c.start or 0)
        c1 = self.col0 + (c.stop if c.stop is not None else self.cols)
        assert c1 <= self.col0 + self.cols and r1 <= self.row0 + self.rows
        return self.h[r0:r1, c0:c1]

    def at(self, coff, pattern):
        return bass.AP(self.h, self.row0 * 0 + self.col0 + coff, pattern)


class Ctx:
    def __init__(self, layers):
        self.layers = layers
        self.nc = bass.Bass("TRN2", target_bir_lowering=False)
        self.E = Em()
        self.sb = {}
        self.n_flags = 0
        self.inputs_ap = {}

    def din(self, name, shape, dtype):
        ap = self.nc.dram_tensor(name, list(shape), dtype, kind="ExternalInput").ap()
        self.inputs_ap[name] = ap
        return ap

    def dout(self, name, shape, dtype):
        return self.nc.dram_tensor(name, list(shape), dtype, kind="ExternalOutput").ap()

    def sbuf(self, name, shape, dtype):
        h = self.nc.alloc_sbuf_tensor(name, list(shape), dtype)
        self.sb[name] = h
        return h

    def region(self, col0, cols, row0=0, rows=128):
        assert col0 + cols <= AW
        return Buf(self.arena, AW, col0, cols, row0, rows)

    def flag(self):
        i = self.n_flags
        self.n_flags += 2
        return i - 0 if False else (self.n_flags - 2)

    def take_bank(self):
        i = self.bank_rr
        self.bank_rr = (self.bank_rr + 1) % 4
        return self.banks[i], self.bank_free[i], i

    def release_bank(self, i, tok):
        self.bank_free[i] = tok


def emit_ln_to_fm(C, xhat, waits=()):
    E = C.E
    x_tok = C.sb["x_tok"]
    stf = C.sb["stats"]
    xt = C.scr_tok
    ptr = C.ptr
    last_evac = None
    slot_act = [None, None]
    prev_pe = None
    t_res_done = list(waits) + [E.tok("dve")]
    xt_guard = E.tok("dve")
    for c in range(8):
        sb = (c % 2) * 32
        st = stf[:, sb:sb + 32]
        tile = x_tok[:, c * D:(c + 1) * D]
        w = list(t_res_done) + [slot_act[c % 2]]
        t1 = E.op("dve", lambda v, tile=tile, st=st: [
            v.bn_stats(st[:, 8 + 6 * j:14 + 6 * j], tile[:, j * 512:(j + 1) * 512])
            for j in range(4)][-1], sig="dve", waits=w)
        t2 = E.op("dve", lambda v, st=st: v.bn_aggr(st[:, 0:2], st[:, 8:32]),
                  sig="dve", waits=[t1])
        t3 = E.op("dve", lambda v, st=st: (
            v.tensor_scalar(st[:, 2:3], st[:, 0:1], -1.0, None, ALU.mult),
            v.tensor_scalar(st[:, 3:4], st[:, 1:2], EPS, None, ALU.add))[-1],
            sig="dve", waits=[t2])
        t_dve = E.op("dve", lambda v, st=st: v.reciprocal(st[:, 4:5], st[:, 3:4]),
                     sig="dve", waits=[t3])
        t5 = E.op("act", lambda a, st=st: a.activation(st[:, 5:6], st[:, 4:5], AF.Ln),
                  sig="act", waits=[t_dve, prev_pe])
        t6 = E.op("act", lambda a, st=st: a.activation(st[:, 5:6], st[:, 5:6], AF.Exp, scale=0.5),
                  sig="act", waits=[t5])
        t7 = E.op("act", lambda a, st=st: a.activation(st[:, 6:7], st[:, 2:3], AF.Identity, scale=st[:, 5:6]),
                  sig="act", waits=[t6])
        t_act = E.op("act", lambda a, tile=tile, st=st, xt=xt: a.activation(
            xt[:, 0:D], tile, AF.Identity, bias=st[:, 6:7], scale=st[:, 5:6]),
            sig="act", waits=[t7, xt_guard])
        slot_act[c % 2] = t_act
        for g in range(4):
            w2 = [t_act, C.ptr_free] + ([last_evac] if last_evac else [])
            t_pe = E.op("pe", lambda t, g=g, xt=xt, ptr=ptr: [
                t.transpose(ptr[:, j * 128:(j + 1) * 128],
                            xt[0:128, (g * 4 + j) * 128:(g * 4 + j + 1) * 128],
                            C.sb["ident"][:, :]) for j in range(4)][-1],
                sig="pe", waits=w2)
            last_evac = E.op("dve", lambda v, g=g, c=c, ptr=ptr, xhat=xhat: v.tensor_copy(
                xhat.at((g * 4) * TOK + c * 128, [[AW, 128], [TOK, 4], [1, 128]]),
                ptr[:, 0:512].rearrange("p (a b) -> p a b", a=4)),
                sig="dve", waits=[t_pe])
            C.ptr_free = last_evac
            prev_pe = t_pe
    C.lastln_pe = prev_pe
    return last_evac


def emit_ln2_tile(C, c, ysrc_ap, dep, slot_act=None):
    E = C.E
    x_tok = C.sb["x_tok"]
    sb = (c % 2) * 32
    st = C.sb["stats"][:, sb:sb + 32]
    w = [dep] + ([slot_act[c % 2]] if slot_act else [])
    t1 = E.op("dve", lambda v, ysrc=ysrc_ap, st=st: [
        v.bn_stats(st[:, 8 + 6 * j:14 + 6 * j], ysrc[:, j * 512:(j + 1) * 512])
        for j in range(4)][-1], sig="dve", waits=w)
    t2 = E.op("dve", lambda v, st=st: v.bn_aggr(st[:, 0:2], st[:, 8:32]),
              sig="dve", waits=[t1])
    t3 = E.op("dve", lambda v, st=st: (
        v.tensor_scalar(st[:, 2:3], st[:, 0:1], -1.0, None, ALU.mult),
        v.tensor_scalar(st[:, 3:4], st[:, 1:2], EPS, None, ALU.add))[-1],
        sig="dve", waits=[t2])
    t_dve = E.op("dve", lambda v, st=st: v.reciprocal(st[:, 4:5], st[:, 3:4]),
                 sig="dve", waits=[t3])
    t5 = E.op("act", lambda a, st=st: a.activation(st[:, 5:6], st[:, 4:5], AF.Ln),
              sig="act", waits=[t_dve])
    t6 = E.op("act", lambda a, st=st: a.activation(st[:, 5:6], st[:, 5:6], AF.Exp, scale=0.5),
              sig="act", waits=[t5])
    t_act = E.op("act", lambda a, st=st: a.activation(st[:, 6:7], st[:, 2:3], AF.Identity, scale=st[:, 5:6]),
                 sig="act", waits=[t6])
    tile = x_tok[:, c * D:(c + 1) * D]
    t_stt = E.op("dve", lambda v, ysrc=ysrc_ap, tile=tile, st=st: v.scalar_tensor_tensor(
        tile, ysrc, st[:, 5:6], tile, ALU.mult, ALU.add),
        sig="dve", waits=[t_act])
    t = E.op("act", lambda a, tile=tile, st=st: a.add(tile, tile, st[:, 6:7]),
             sig="act", waits=[t_stt])
    if slot_act is not None:
        slot_act[c % 2] = t
    return t


def load_slab(C, dram_ap, row0, rows, col0, cols, slot, extra=()):
    E = C.E
    nk = rows // 128
    ws, base, pitch = C.wslab_base[slot]

    def fn(e, dram_ap=dram_ap, row0=row0, rows=rows, col0=col0, cols=cols,
           base=base, nk=nk, ws=ws, pitch=pitch):
        src = dram_ap[row0:row0 + rows, col0:col0 + cols].rearrange("(k p) o -> p k o", p=128)
        dst = bass.AP(ws, base, [[pitch, 128], [cols, nk], [1, cols]])
        return e.dma_start(out=dst, in_=src)
    tok = E.op("pool", fn, sig="dmaw", waits=[C.wslab_busy[slot]] + list(extra))

    def get(k, c0, c1, ws=ws, base=base):
        return ws[:, base + k * cols + c0: base + k * cols + c1]
    return get, tok


def mm_evac(C, pairs, out_rows, ncols, evac_fn, waits, evac_eng="act", evac_sig=None):
    E = C.E
    bank, bfree, bi = C.take_bank()
    n = len(pairs)
    t_mm = E.op("pe", lambda t, pairs=pairs, bank=bank, out_rows=out_rows, ncols=ncols, n=n: [
        t.matmul(bank[0:out_rows, 0:ncols], l, r, start=(i == 0), stop=(i == n - 1), skip_group_check=True)
        for i, (l, r) in enumerate(pairs)][-1],
        sig="pe", waits=list(waits) + [bfree])
    t_ev = E.op(evac_eng, lambda e, bank=bank, out_rows=out_rows, ncols=ncols: evac_fn(
        e, bank[0:out_rows, 0:ncols]), sig=evac_sig or evac_eng, waits=[t_mm])
    C.release_bank(bi, t_ev)
    return t_ev


def emit_rope_half(C, buf, coff, tabname, half, nt, waits):
    E = C.E
    scr = C.scr_tok
    ca, cb = {"rt32": (2048, 3072), "rt64": (0, 1024)}[tabname]

    waits = list(waits) + [getattr(C, "lastln_pe", None)]

    def prods(v, buf=buf, coff=coff, half=half, nt=nt, ca=ca, cb=cb):
        tab = C.sb["ropes"]
        x1 = buf[0:half, coff:coff + nt]
        x2 = buf[half:2 * half, coff:coff + nt]
        cosA = tab[0:half, ca:ca + nt]
        sinA = tab[half:2 * half, ca:ca + nt]
        sinB = tab[0:half, cb:cb + nt]
        cosB = tab[half:2 * half, cb:cb + nt]
        v.tensor_tensor(scr[0:half, 0:nt], x1, cosA, ALU.mult)
        v.tensor_tensor(scr[0:half, nt:2 * nt], x2, sinA, ALU.mult)
        v.tensor_tensor(scr[half:2 * half, 0:nt], x2, cosB, ALU.mult)
        return v.tensor_tensor(scr[half:2 * half, nt:2 * nt], x1, sinB, ALU.mult)

    def combine(v, buf=buf, coff=coff, half=half, nt=nt):
        x1 = buf[0:half, coff:coff + nt]
        x2 = buf[half:2 * half, coff:coff + nt]
        v.tensor_tensor(x1, scr[0:half, 0:nt], scr[0:half, nt:2 * nt], ALU.subtract)
        return v.tensor_tensor(x2, scr[half:2 * half, 0:nt], scr[half:2 * half, nt:2 * nt], ALU.add)
    t1 = E.op("dve", prods, sig="dve", waits=waits)
    return E.op("dve", combine, sig="dve", waits=[t1])


def exchange_fm_buf(C, buf, nch, shreg, waits):
    def wr(e, buf=buf, nch=nch, shreg=shreg):
        regs = C.E.regs
        src = buf.at(0, [[AW, 128], [S, nch], [1, TOK]])
        d0 = shreg.ap()[0:nch * 128, :].rearrange("(k p) t -> p k t", p=128)
        with e.If_eq(regs["parity"], 0):
            e.dma_start(out=d0[:, :, 0:TOK], in_=src).then_inc(C.E.sem_objs["dmas"], 16)
        with e.Else():
            e.dma_start(out=d0[:, :, TOK:S], in_=src).then_inc(C.E.sem_objs["dmas"], 16)

    def rd(e, buf=buf, nch=nch, shreg=shreg):
        regs = C.E.regs
        dst = buf.at(TOK, [[AW, 128], [S, nch], [1, TOK]])
        s0 = shreg.ap()[0:nch * 128, :].rearrange("(k p) t -> p k t", p=128)
        with e.If_eq(regs["parity"], 0):
            e.dma_start(out=dst, in_=s0[:, :, TOK:S]).then_inc(C.E.sem_objs["dmas"], 16)
        with e.Else():
            e.dma_start(out=dst, in_=s0[:, :, 0:TOK]).then_inc(C.E.sem_objs["dmas"], 16)

    E = C.E
    fid = C.n_flags
    C.n_flags += 2
    for w in waits:
        if w is not None:
            E.wait("sp", w[0], w[1])
    E.raw("sp", wr)
    E.cnt["dmas"] += 1
    wr_cnt = E.cnt["dmas"]
    E.wait("sp", "dmas", wr_cnt)

    def send_flag(e, fid=fid):
        regs = C.E.regs
        with e.If_eq(regs["parity"], 0):
            e.dma_start(out=C.shf.ap()[0:1, fid:fid + 1], in_=C.sb["meta"][0:1, 1:2]).then_inc(
                C.E.sem_objs["dmas"], 16)
        with e.Else():
            e.dma_start(out=C.shf.ap()[0:1, fid + 1:fid + 2], in_=C.sb["meta"][0:1, 1:2]).then_inc(
                C.E.sem_objs["dmas"], 16)
    E.raw("sp", send_flag)
    E.cnt["dmas"] += 1
    E.last_wait[("sp", "dmas")] = E.cnt["dmas"]

    def poll(e, fid=fid):
        regs = C.E.regs
        fv, tgt, cc = regs["fv"], regs["tgt"], regs["cc"]
        e.reg_mov(cc, 1)
        with e.While(cc):
            with e.If_eq(regs["parity"], 0):
                e.dma_start(out=C.sb["flags"][0:1, 0:1],
                            in_=C.shf.ap()[0:1, fid + 1:fid + 2]).then_inc(C.E.sem_objs["poll"], 16)
            with e.Else():
                e.dma_start(out=C.sb["flags"][0:1, 0:1],
                            in_=C.shf.ap()[0:1, fid:fid + 1]).then_inc(C.E.sem_objs["poll"], 16)
            e.reg_add(tgt, tgt, 16)
            e.wait_ge(C.E.sem_objs["poll"], tgt)
            e.reg_load(fv, C.sb["flags"][0:1, 0:1])
            e.reg_alu(cc, fv, regs["nonce"], ALU.not_equal)
    E.raw("sp", poll)
    E.raw("sp", rd)
    E.cnt["dmas"] += 1
    return E.tok("dmas")



def emit_mla_attn(C, lx, q_lora, d_nope, d_rope, d_v, wqa_ap, wqb_ap, wkva_ap,
                  wkvb_ap, wo_ap):
    E = C.E
    H = 16
    d_qk = d_nope + d_rope
    nkch = (d_qk + 127) // 128
    kv_lora = 512
    nlat = kv_lora // 128
    scale = 1.0 / float(np.sqrt(d_qk))
    nql = q_lora // 128

    xhat = C.region(0, 16 * TOK)
    latb = C.region(16384, (nlat + 1) * S)
    qlat = C.region(16384 + (nlat + 1) * S, nql * TOK)
    obuf = C.region(38912, 16 * TOK)
    kh = C.region(0, nkch * S)
    vfm = C.region(2 * S, S)
    vtok = C.region(3 * S, S)
    qh = C.region(4 * S, 2 * TOK)
    pbuf = C.region(4 * S + 2 * TOK, 4 * 512)
    npb = 4
    zbs = C.region(4 * S + 2 * TOK + npb * 512, 512)

    t_xhat = emit_ln_to_fm(C, xhat, waits=[E.tok("act")])

    for oc in range(nql):
        slot = oc % 2
        wget, wtok = load_slab(C, wqa_ap, 0, D, oc * 128, 128, slot)
        last = None
        for q2 in range(2):
            last = mm_evac(
                C, [(wget(k, 0, 128), xhat[:, k * TOK + q2 * 512:k * TOK + (q2 + 1) * 512])
                    for k in range(NCH)], 128, 512,
                lambda a, bk, oc=oc, q2=q2: a.activation(
                    qlat[0:128, oc * TOK + q2 * 512:oc * TOK + (q2 + 1) * 512], bk, AF.Identity),
                [t_xhat, wtok])
        C.wslab_busy[slot] = last
    for oc in range(nlat + 1):
        rows = d_rope if oc == nlat else 128
        slot = oc % 2
        wget, wtok = load_slab(C, wkva_ap, 0, D, oc * 128, rows, slot)
        last = None
        for q2 in range(2):
            last = mm_evac(
                C, [(wget(k, 0, rows), xhat[:, k * TOK + q2 * 512:k * TOK + (q2 + 1) * 512])
                    for k in range(NCH)], rows, 512,
                lambda a, bk, oc=oc, q2=q2, rows=rows: a.activation(
                    latb[0:rows, oc * S + q2 * 512:oc * S + (q2 + 1) * 512], bk, AF.Identity),
                [t_xhat, wtok])
        C.wslab_busy[slot] = last
    t_rope_k = emit_rope_half(C, C.region(16384 + nlat * S, S), 0, "rt32",
                              d_rope // 2, TOK, [E.tok("act")])

    t_exch = exchange_fm_buf(C, latb, nlat + 1, C.shared[f"lat{lx}"],
                             [t_rope_k, E.tok("act")])
    dep_lat = t_exch

    kh_free = None
    vfm_free = None
    vtok_free = None
    qh_free = None
    pb_free = [None] * npb
    t_last_norm = None
    for h in range(H):
        slot = h % 2
        t_kcp = E.op("dve", lambda v: v.tensor_copy(
            kh[0:d_rope, 0:S], latb[0:d_rope, nlat * S:(nlat + 1) * S]),
            sig="dve", waits=[dep_lat, kh_free])
        wget, wtok = load_slab(C, wkvb_ap, 0, kv_lora,
                               h * (d_nope + d_v), d_nope + d_v, slot)
        t_kn = None
        for kg in range(4):
            def ev_kn(a, bk, kg=kg):
                r0 = 128 - d_rope
                inst = a.activation(kh.h[d_rope:128, kh.col0 + kg * 512:kh.col0 + (kg + 1) * 512],
                                    bk[0:r0, :], AF.Identity)
                if d_nope > r0:
                    inst = a.activation(
                        kh.h[0:d_nope - r0, kh.col0 + S + kg * 512:kh.col0 + S + (kg + 1) * 512],
                        bk[r0:d_nope, :], AF.Identity)
                return inst
            t_kn = mm_evac(
                C, [(wget(k, 0, d_nope), latb[:, k * S + kg * 512:k * S + (kg + 1) * 512])
                    for k in range(nlat)], d_nope, 512, ev_kn,
                [wtok, C.wslab_busy[slot], t_kcp, dep_lat])
        t_vf = None
        for kg in range(4):
            t_vf = mm_evac(
                C, [(wget(k, d_nope, d_nope + d_v), latb[:, k * S + kg * 512:k * S + (kg + 1) * 512])
                    for k in range(nlat)], d_v, 512,
                lambda a, bk, kg=kg: a.activation(
                    vfm[0:d_v, kg * 512:(kg + 1) * 512], bk, AF.Identity),
                [wtok, C.wslab_busy[slot], vfm_free, dep_lat])
        C.wslab_busy[slot] = t_vf
        t_vt = None
        for g in range(2):
            t_pe = E.op("pe", lambda t, g=g: [
                t.transpose(C.ptr[:, j * 128:(j + 1) * 128],
                            vfm[0:128, (g * 8 + j) * 128:(g * 8 + j + 1) * 128],
                            C.sb["ident"][:, :]) for j in range(8)][-1],
                sig="pe", waits=[t_vf, vtok_free, t_vt, C.ptr_free])
            t_vt = E.op("dve", lambda v, g=g: v.tensor_copy(
                vtok[0:128, g * 1024:(g + 1) * 1024], C.ptr[:, 0:1024]),
                sig="dve", waits=[t_pe])
            C.ptr_free = t_vt
        t_qh = None
        wgetq, wtokq = load_slab(C, wqb_ap, 0, q_lora, h * d_qk, d_qk, "a2")
        for q2 in range(2):
            def ev_q(a, bk, q2=q2):
                inst = a.activation(qh.h[0:min(128, d_qk),
                                         qh.col0 + q2 * 512:qh.col0 + (q2 + 1) * 512],
                                    bk[0:min(128, d_qk), :], AF.Identity)
                return inst
            t_qh = mm_evac(
                C, [(wgetq(k, 0, min(128, d_qk)),
                     qlat[:, k * TOK + q2 * 512:k * TOK + (q2 + 1) * 512])
                    for k in range(nql)], min(128, d_qk), 512, ev_q,
                [wtokq, qh_free, C.wslab_busy["a2"]])
            if d_qk > 128:
                t_qh = mm_evac(
                    C, [(wgetq(k, 128, d_qk),
                         qlat[:, k * TOK + q2 * 512:k * TOK + (q2 + 1) * 512])
                        for k in range(nql)], d_qk - 128, 512,
                    lambda a, bk, q2=q2: a.activation(
                        qh.h[0:d_qk - 128, qh.col0 + TOK + q2 * 512:qh.col0 + TOK + (q2 + 1) * 512],
                        bk, AF.Identity),
                    [wtokq, qh_free, C.wslab_busy["a2"]])
        C.wslab_busy["a2"] = t_qh
        t_qrope = emit_rope_half(C, qh, 0, "rt32", d_rope // 2, TOK, [t_qh])

        mpos = {}
        mi = 0
        for (q0, s0) in MASKED_POS:
            for sub in range(4):
                mpos[(q0, s0, sub)] = mi
                mi += 1
        for qc in range(2):
            tiles = [(sl, sub) for sl in SCHED[qc] for sub in range(4)]
            nt = len(tiles)
            pend = None
            for ti, (sl, sub) in enumerate(tiles):
                koff = sl * 512 + sub * 128
                sbank, sfree, sbi = C.take_bank()
                t_s = E.op("pe", lambda t, koff=koff, qc=qc, sbank=sbank: [
                    t.matmul(sbank[0:128, 0:512],
                             kh.h[0:(128 if c2 == 0 else d_qk - 128),
                                  kh.col0 + c2 * S + koff:kh.col0 + c2 * S + koff + 128],
                             qh.h[0:(128 if c2 == 0 else d_qk - 128),
                                  qh.col0 + c2 * TOK + qc * 512:qh.col0 + c2 * TOK + (qc + 1) * 512],
                             start=(c2 == 0), stop=(c2 == nkch - 1), skip_group_check=True)
                    for c2 in range(nkch)][-1],
                    sig="pe", waits=[sfree, t_kn, t_vt, t_qrope])
                ps = ti % npb
                t_exp = E.op("act", lambda a, sbank=sbank, ps=ps: a.activation(
                    pbuf[0:128, ps * 512:(ps + 1) * 512], sbank[0:128, 0:512],
                    AF.Exp, scale=scale), sig="act", waits=[t_s, pb_free[ps]])
                C.release_bank(sbi, t_exp)
                t_pready = t_exp
                if (qc, sl, sub) in mpos:
                    t_pready = E.op("dve", lambda v, ps=ps, m=mpos[(qc, sl, sub)]: v.tensor_tensor(
                        pbuf[0:128, ps * 512:(ps + 1) * 512],
                        pbuf[0:128, ps * 512:(ps + 1) * 512],
                        C.sb["masks"][:, m * 512:(m + 1) * 512], ALU.mult),
                        sig="dve", waits=[t_exp])
                if pend is not None:
                    pb_free[pend[0]] = emit_zav(C, pbuf, vtok, pend[0], pend[1], pend[2],
                                                pend[3], nt)
                pend = (ps, ti, koff, t_pready)
            pb_free[pend[0]] = emit_zav(C, pbuf, vtok, pend[0], pend[1], pend[2],
                                        pend[3], nt)
            t_last_norm = emit_znorm(C, obuf, zbs, h, qc)
        kh_free = E.tok("pe")
        vtok_free = E.tok("pe")
        qh_free = E.tok("pe")

    emit_out_ln2(C, obuf, wo_ap, [t_last_norm])


def emit_zav(C, pbuf, vtok, ps, ti, koff, t_pready, nt):
    E = C.E
    first = (ti == 0)
    last = (ti == nt - 1)
    kt = koff // 128
    waits = [t_pready]
    if first:
        waits += [C.obank_free, C.pz_free]
    t = E.op("pe", lambda t, ps=ps, first=first, last=last, kt=kt: (
        t.matmul(C.pz[0:1, 0:512], C.sb["ones"][:, 0:1],
                 pbuf[0:128, ps * 512:(ps + 1) * 512], start=first, stop=last,
                 skip_group_check=True),
        t.matmul(C.obank[0:128, 0:512], vtok[0:128, kt * 128:(kt + 1) * 128],
                 pbuf[0:128, ps * 512:(ps + 1) * 512], start=first, stop=last,
                 skip_group_check=True))[-1],
        sig="pe", waits=waits)
    return t


def emit_znorm(C, obuf, zbs, h, qc):
    E = C.E
    t_avz = E.tok("pe")
    def _recip(v):
        with C.nc.allow_low_precision(reason="softmax 1/Z to bf16; probs are bf16"):
            return v.reciprocal(C.sb["zrec"][0:1, 0:512], C.pz[0:1, 0:512])
    t_zev = E.op("dve", _recip, sig="dve", waits=[t_avz])
    C.pz_free = t_zev
    bank, bfree, bi = C.take_bank()
    t_bc = E.op("pe", lambda t, bank=bank: t.matmul(
        bank[0:128, 0:512], C.sb["ones"][0:1, 0:128], C.sb["zrec"][0:1, 0:512],
        start=True, stop=True, skip_group_check=True),
        sig="pe", waits=[t_zev, bfree])
    t_zb = E.op("dve", lambda v, bank=bank: v.tensor_copy(zbs[0:128, 0:512], bank[0:128, 0:512]),
                sig="dve", waits=[t_bc])
    C.release_bank(bi, t_zb)
    t_norm = E.op("dve", lambda v, h=h, qc=qc: v.tensor_tensor(
        obuf[0:128, h * TOK + qc * 512:h * TOK + (qc + 1) * 512],
        C.obank[0:128, 0:512], zbs[0:128, 0:512], ALU.mult),
        sig="dve", waits=[t_zb])
    C.obank_free = t_norm
    return t_norm


def emit_out_ln2(C, obuf, wo_ap, deps, obuf_col0=38912, ytok_col0=16384):
    E = C.E
    ytok = C.region(ytok_col0, 8 * D)
    tile_done = [None] * 8
    C.wslab_busy["big"] = (E.tok("pe"), E.tok("dve"))[0]
    E.wait("pool", "dve", E.cnt["dve"])
    for dg in range(4):
        wget, wtok = load_slab(C, wo_ap, 0, D, dg * 512, 512, "big")
        last = None
        for c in range(8):
            qc, c4 = divmod(c, 4)
            last = mm_evac(
                C, [(obuf[:, k * TOK + qc * 512 + c4 * 128:k * TOK + qc * 512 + c4 * 128 + 128],
                     wget(k, 0, 512)) for k in range(NCH)], 128, 512,
                lambda a, bk, c=c, dg=dg: a.activation(
                    ytok[0:128, c * D + dg * 512:c * D + dg * 512 + 512], bk, AF.Identity),
                deps + [wtok, C.wslab_busy["big"], tile_done[c]])
            tile_done[c] = last
        C.wslab_busy["big"] = last
    slot_act = [None, None]
    for c in range(8):
        t = emit_ln2_tile(C, c, ytok[0:128, c * D:(c + 1) * D], tile_done[c], slot_act)
        tile_done[c] = t
    return tile_done[-1]


def emit_swiglu_ffn(C, gu_ap, dn_ap, inter):
    E = C.E
    nic = inter // 128
    xhat = C.region(0, 16 * TOK)
    x2 = C.region(16384, nic * 512)
    yfm = C.region(8192, 16 * 512)

    C.ffn_slot_act = [None, None]
    for qc in range(2):
        scr_free = [E.tok("dve")] * 4
        t_xhat = emit_ln_to_fm(C, xhat, waits=[E.tok("act"), E.tok("pe")])
        x2_done = None
        qc_guard = E.tok("act")
        C.wslab_busy["a0"] = qc_guard
        C.wslab_busy["a1"] = qc_guard
        for hc in range(nic):
            slot = hc % 2
            wg, tg = load_slab(C, gu_ap, 0, D, hc * 128, 128, slot)
            wu, tu = load_slab(C, gu_ap, 0, D, inter + hc * 128, 128, ("a0", "a1")[slot])
            bankG, gfree, gi = C.take_bank()
            t_g = E.op("pe", lambda t, wg=wg, qc=qc, bankG=bankG: [
                t.matmul(bankG[0:128, 0:512], wg(k, 0, 128),
                         xhat[:, k * TOK + qc * 512:k * TOK + (qc + 1) * 512],
                         start=(k == 0), stop=(k == NCH - 1), skip_group_check=True)
                for k in range(NCH)][-1], sig="pe",
                waits=[t_xhat, tg, gfree])
            ssl = hc % 4
            t_sg = E.op("act", lambda a, bankG=bankG, ssl=ssl: a.activation(
                C.scr_tok[0:128, ssl * 512:(ssl + 1) * 512], bankG[0:128, 0:512], AF.Silu),
                sig="act", waits=[t_g, scr_free[ssl]])
            C.release_bank(gi, t_sg)
            bankU, ufree, ui = C.take_bank()
            t_u = E.op("pe", lambda t, wu=wu, qc=qc, bankU=bankU: [
                t.matmul(bankU[0:128, 0:512], wu(k, 0, 128),
                         xhat[:, k * TOK + qc * 512:k * TOK + (qc + 1) * 512],
                         start=(k == 0), stop=(k == NCH - 1), skip_group_check=True)
                for k in range(NCH)][-1], sig="pe",
                waits=[t_xhat, tu, ufree])
            x2_done = E.op("dve", lambda v, hc=hc, bankU=bankU, ssl=ssl: v.tensor_tensor(
                x2[0:128, hc * 512:(hc + 1) * 512], bankU[0:128, 0:512],
                C.scr_tok[0:128, ssl * 512:(ssl + 1) * 512], ALU.mult),
                sig="dve", waits=[t_u, t_sg])
            scr_free[ssl] = x2_done
            C.release_bank(ui, x2_done)
            C.wslab_busy[slot] = t_g
            C.wslab_busy[("a0", "a1")[slot]] = t_u
        yfm_done = [None] * 16
        C.wslab_busy["big"] = E.tok("pe")
        for oc in range(NCH):
            wget, wtok = load_slab(C, dn_ap, 0, inter, oc * 128, 128, "big")
            yfm_done[oc] = mm_evac(
                C, [(wget(k, 0, 128), x2[:, k * 512:(k + 1) * 512]) for k in range(nic)],
                128, 512,
                lambda a, bk, oc=oc: a.activation(
                    yfm[0:128, oc * 512:(oc + 1) * 512], bk, AF.Identity),
                [x2_done, wtok, C.wslab_busy["big"]])
            C.wslab_busy["big"] = yfm_done[oc]
        last_ev = None
        for c4 in range(4):
            c = qc * 4 + c4
            for g in range(4):
                t_pe = E.op("pe", lambda t, g=g, c4=c4: [
                    t.transpose(C.ptr[:, j * 128:(j + 1) * 128],
                                yfm[0:128, (g * 4 + j) * 512 + c4 * 128:(g * 4 + j) * 512 + c4 * 128 + 128],
                                C.sb["ident"][:, :]) for j in range(4)][-1],
                    sig="pe", waits=[yfm_done[g * 4 + 3], last_ev, C.ptr_free])
                last_ev = E.op("dve", lambda v, g=g: v.tensor_copy(
                    bass.AP(C.sb["ytok2"], g * 4 * 128, [[D, 128], [128, 4], [1, 128]]),
                    C.ptr[:, 0:512].rearrange("p (a b) -> p a b", a=4)),
                    sig="dve", waits=[t_pe])
                C.ptr_free = last_ev
            t = emit_ln2_tile(C, c, C.sb["ytok2"][0:128, 0:D], last_ev, C.ffn_slot_act)
            last_ev = t
    return E.tok("act")



def emit_l0_ffn(C):
    E = C.E
    inter = 8192
    nic = inter // 128
    xhat = C.region(0, 16 * TOK)
    x2 = C.region(16384, nic * 512)
    yfm = C.region(8192, 16 * 512)
    sup = C.region(49152, 2048)
    sdn = C.region(49152 + 2048, 2048)
    tsl = C.region(53248, 2048)
    wq = C.wio["L0_wupq"]
    wdq = C.wio["L0_wdnq"]
    ua = C.wio["L0_ua"]
    ub = C.wio["L0_ub"]
    da = C.wio["L0_da"]
    db = C.wio["L0_db"]

    ffn_guard = [(n, C.E.cnt[n]) for n in ("pe", "dve", "act")]
    t_sc = E.op("pool", lambda e: e.dma_start(
        out=sup.at(0, [[AW, 128], [32, 64], [1, 32]]),
        in_=C.inputs_ap["L0_wupsc"].rearrange("(c p) b -> p c b", p=128)), sig="dmaw",
        waits=[C.wslab_busy["a0"]] + ffn_guard)
    t_sc2 = E.op("pool", lambda e: e.dma_start(
        out=sdn.at(0, [[AW, 128], [128, 16], [1, 128]]),
        in_=C.inputs_ap["L0_wdnsc"].rearrange("(c p) b -> p c b", p=128)), sig="dmaw",
        waits=[C.wslab_busy["a1"]] + ffn_guard)
    E.op("pool", lambda e: e.dma_start(out=C.sb["vecs"][:, 0:64],
                                       in_=C.inputs_ap["L0_bup"][:, :]), sig="dmaw")
    t_bv = E.op("pool", lambda e: e.dma_start(out=C.sb["vecs"][:, 64:80],
                                              in_=C.inputs_ap["L0_bdn"][:, :]), sig="dmaw")

    C.ffn_slot_act = [None, None]
    for qc in range(2):
        t_xhat = emit_ln_to_fm(C, xhat, waits=[E.tok("act"), E.tok("pe")])
        wgu, tgu = load_slab(C, ua, 0, D, 0, 16, 0)
        u1 = C.scr_tok
        t_u1 = mm_evac(
            C, [(wgu(k, 0, 16), xhat[:, k * TOK + qc * 512:k * TOK + (qc + 1) * 512])
                for k in range(NCH)], 16, 512,
            lambda a, bk: a.activation(u1[0:16, 1536:2048], bk, AF.Identity),
            [t_xhat, tgu], evac_eng="act")
        C.wslab_busy[0] = t_u1
        x2_done = None
        t_tslfree = None
        lorb_tok = None
        for hc in range(nic):
            if hc % 16 == 0:
                lorb_tok = E.op("pool", lambda e, hc=hc: e.dma_start(
                    out=C.sb["wslab"][0:16, 2048:4096],
                    in_=ub[0:16, hc * 128:(hc + 16) * 128]), sig="dmaw",
                    waits=[C.wslab_busy[1]])
            wg0, tg0 = load_slab(C, wq, hc * 128, 128, 0, 2048, 0)
            t_dq = E.op("dve", lambda v, wg0=wg0, hc=hc: v.scalar_tensor_tensor(
                C.sb["wslab"][:, 0:2048].rearrange("p (b k) -> p b k", b=32),
                C.sb["wslab"][:, 0:2048].rearrange("p (b k) -> p b k", b=32),
                -7.5,
                sup[0:128, hc * 32:(hc + 1) * 32].unsqueeze(2).broadcast_to([128, 32, 64]),
                ALU.add, ALU.mult), sig="dve", waits=[tg0, t_sc])
            t_tr = None
            for g in range(4):
                t_pe = E.op("pe", lambda t, g=g: [
                    t.transpose(C.ptr[:, j * 128:(j + 1) * 128],
                                C.sb["wslab"][0:128, (g * 4 + j) * 128:(g * 4 + j + 1) * 128],
                                C.sb["ident"][:, :]) for j in range(4)][-1],
                    sig="pe", waits=[t_dq, t_tr, t_tslfree, C.ptr_free])
                t_tr = E.op("dve", lambda v, g=g: v.tensor_copy(
                    tsl.at(g * 512, [[AW, 128], [1, 512]]),
                    C.ptr[:, 0:512]), sig="dve", waits=[t_pe])
                C.ptr_free = t_tr
            bank, bfree, bi = C.take_bank()
            t_mm = E.op("pe", lambda t, qc=qc, bank=bank, hc=hc: ([
                t.matmul(bank[0:128, 0:512],
                         tsl[0:128, k * 128:(k + 1) * 128],
                         xhat[:, k * TOK + qc * 512:k * TOK + (qc + 1) * 512],
                         start=(k == 0), stop=False, skip_group_check=True)
                for k in range(NCH)],
                t.matmul(bank[0:128, 0:512],
                         C.sb["wslab"][0:16, 2048 + (hc % 16) * 128:2048 + (hc % 16 + 1) * 128],
                         u1[0:16, 1536:2048],
                         start=False, stop=True, skip_group_check=True))[-1],
                sig="pe", waits=[t_tr, t_u1, lorb_tok, bfree, C.wslab_busy[1]])
            t_tslfree = t_mm
            C.wslab_busy[1] = t_mm
            x2_done = E.op("act", lambda a, bank=bank, hc=hc: a.activation(
                x2[0:128, hc * 512:(hc + 1) * 512], bank[0:128, 0:512],
                AF.Relu, bias=C.sb["vecs"][0:128, hc:hc + 1]),
                sig="act", waits=[t_mm, t_bv])
            C.release_bank(bi, x2_done)
            C.wslab_busy[0] = t_mm
        wgd, tgd = load_slab(C, da, 0, inter, 0, 16, 1)
        t_u2 = mm_evac(
            C, [(wgd(k, 0, 16), x2[:, k * 512:(k + 1) * 512]) for k in range(nic)],
            16, 512,
            lambda a, bk: a.activation(u1[0:16, 1024:1536], bk, AF.Identity),
            [x2_done, tgd], evac_eng="act")
        C.wslab_busy[1] = t_u2
        t_lb2 = E.op("pool", lambda e: e.dma_start(
            out=C.sb["wslab"][0:16, 2048:4096], in_=db[0:16, 0:2048]), sig="dmaw",
            waits=[C.wslab_busy[1]])
        yfm_done = [None] * 16
        C.wslab_busy["big"] = E.tok("pe")
        for oc in range(NCH):
            wgb, tgb = load_slab(C, wdq, oc * 128, 128, 0, inter, "big")
            t_dq2 = E.op("dve", lambda v, oc=oc: v.scalar_tensor_tensor(
                C.arena[0:128, 0:8192].rearrange("p (b k) -> p b k", b=128),
                C.arena[0:128, 0:8192].rearrange("p (b k) -> p b k", b=128),
                -7.5,
                sdn[0:128, oc * 128:(oc + 1) * 128].unsqueeze(2).broadcast_to([128, 128, 64]),
                ALU.add, ALU.mult), sig="dve", waits=[tgb, t_sc2])
            bank, bfree, bi = C.take_bank()
            first = True
            t_tr2 = None
            for g in range(16):
                t_pe = E.op("pe", lambda t, g=g: [
                    t.transpose(C.ptr[:, j * 128:(j + 1) * 128],
                                C.arena[0:128, (g * 4 + j) * 128:(g * 4 + j + 1) * 128],
                                C.sb["ident"][:, :]) for j in range(4)][-1],
                    sig="pe", waits=[t_dq2, t_tr2, bfree, C.ptr_free])
                t_tr2 = E.op("dve", lambda v, g=g: v.tensor_copy(
                    tsl.at((g % 4) * 512, [[AW, 128], [1, 512]]),
                    C.ptr[:, 0:512]), sig="dve", waits=[t_pe])
                C.ptr_free = t_tr2
                t_mm2 = E.op("pe", lambda t, g=g, qc=qc, bank=bank, first=first: [
                    t.matmul(bank[0:128, 0:512],
                             tsl[0:128, (g % 4) * 512 + j * 128:(g % 4) * 512 + (j + 1) * 128],
                             x2[:, (g * 4 + j) * 512:(g * 4 + j + 1) * 512],
                             start=(first and j == 0), stop=False, skip_group_check=True)
                    for j in range(4)][-1], sig="pe", waits=[t_tr2])
                first = False
                t_tr2 = t_mm2
            t_mm3 = E.op("pe", lambda t, oc=oc, bank=bank: t.matmul(
                bank[0:128, 0:512],
                C.sb["wslab"][0:16, 2048 + oc * 128:2048 + (oc + 1) * 128],
                u1[0:16, 1024:1536],
                start=False, stop=True, skip_group_check=True),
                sig="pe", waits=[t_u2, t_lb2])
            yfm_done[oc] = E.op("act", lambda a, bank=bank, oc=oc: a.activation(
                yfm[0:128, oc * 512:(oc + 1) * 512], bank[0:128, 0:512],
                AF.Identity, bias=C.sb["vecs"][0:128, 64 + oc:64 + oc + 1]),
                sig="act", waits=[t_mm3, t_bv])
            C.release_bank(bi, yfm_done[oc])
            C.wslab_busy["big"] = E.tok("pe")
        C.wslab_busy[1] = E.tok("pe")
        last_ev = E.tok("pe")
        for c4 in range(4):
            c = qc * 4 + c4
            for g in range(4):
                t_pe = E.op("pe", lambda t, g=g, c4=c4: [
                    t.transpose(C.ptr[:, j * 128:(j + 1) * 128],
                                yfm[0:128, (g * 4 + j) * 512 + c4 * 128:(g * 4 + j) * 512 + c4 * 128 + 128],
                                C.sb["ident"][:, :]) for j in range(4)][-1],
                    sig="pe", waits=[yfm_done[g * 4 + 3], last_ev, C.ptr_free])
                last_ev = E.op("dve", lambda v, g=g: v.tensor_copy(
                    bass.AP(C.sb["ytok2"], g * 4 * 128, [[D, 128], [128, 4], [1, 128]]),
                    C.ptr[:, 0:512].rearrange("p (a b) -> p a b", a=4)),
                    sig="dve", waits=[t_pe])
                C.ptr_free = last_ev
            t = emit_ln2_tile(C, c, C.sb["ytok2"][0:128, 0:D], last_ev, C.ffn_slot_act)
            last_ev = t
    return E.tok("act")



def emit_l3_attn(C):
    E = C.E
    H = 16
    scale = 1.0 / float(np.sqrt(128.0))
    xhat = C.region(0, 16 * TOK)
    latb = C.region(16384, 4 * S)
    kstg = C.region(24576, 2 * TOK)
    qlat = C.region(26624, 8 * TOK)
    obuf = C.region(34816, 16 * TOK)
    qpe = C.region(43008, 8 * TOK)
    kh = C.region(0, S)
    vfm = C.region(2 * S, S)
    vtok = C.region(3 * S, S)
    qh = C.region(4 * S, TOK)
    qscr = C.region(4 * S + TOK, TOK)
    pbuf = C.region(4 * S + 2 * TOK, 4 * 512)
    zbs = C.region(4 * S + 2 * TOK + 4 * 512, 512)

    t_ropes = E.op("pool", lambda e: e.dma_start(
        out=C.sb["ropes"][:, 0:2 * TOK], in_=C.inputs_ap["ropes3_in"][:, :]),
        sig="dmaw", waits=[(n, E.cnt[n]) for n in ("dve",)])

    t_xhat = emit_ln_to_fm(C, xhat, waits=[E.tok("act"), E.tok("pe")])

    for nm, wap, nql_, dst, own_half in (
            ("qd", C.wio["L3_Wqd"], 8, qlat, False),
            ("kvd", C.wio["L3_Wkvd"], 4, latb, True)):
        for oc in range(nql_):
            slot = oc % 2
            wget, wtok = load_slab(C, wap, 0, D, oc * 128, 128, slot)
            last = None
            for q2 in range(2):
                W = S if own_half else TOK
                last = mm_evac(
                    C, [(wget(k, 0, 128), xhat[:, k * TOK + q2 * 512:k * TOK + (q2 + 1) * 512])
                        for k in range(NCH)], 128, 512,
                    lambda a, bk, oc=oc, q2=q2, dst=dst, W=W: a.activation(
                        dst[0:128, oc * W + q2 * 512:oc * W + (q2 + 1) * 512], bk, AF.Identity),
                    [t_xhat, wtok])
            C.wslab_busy[slot] = last
    kstg_free = [None, None]
    for oc in range(8):
        slot = oc % 2
        wget, wtok = load_slab(C, C.wio["L3_Wkr"], 0, D, oc * 128, 128, slot)
        last = None
        for q2 in range(2):
            last = mm_evac(
                C, [(wget(k, 0, 128), xhat[:, k * TOK + q2 * 512:k * TOK + (q2 + 1) * 512])
                    for k in range(NCH)], 128, 512,
                lambda a, bk, oc=oc, q2=q2: a.activation(
                    kstg[0:128, (oc % 2) * TOK + q2 * 512:(oc % 2) * TOK + (q2 + 1) * 512],
                    bk, AF.Identity),
                [t_xhat, wtok, kstg_free[oc % 2]])
        C.wslab_busy[slot] = last
        t_kr = emit_rope_half(C, C.region(24576 + (oc % 2) * TOK, TOK), 0,
                              "rt64", 64, TOK, [last, t_ropes])

        def kwr(e, oc=oc):
            sem = C.E.sem_objs["dmas"]
            regs = C.E.regs
            src_ap = kstg[0:128, (oc % 2) * TOK:(oc % 2 + 1) * TOK]
            sh = C.shared["lat3"].ap()
            with e.If_eq(regs["parity"], 0):
                e.dma_start(out=sh[(4 + oc) * 128:(5 + oc) * 128, 0:TOK],
                            in_=src_ap).then_inc(sem, 16)
            with e.Else():
                e.dma_start(out=sh[(4 + oc) * 128:(5 + oc) * 128, TOK:S],
                            in_=src_ap).then_inc(sem, 16)
        E.wait("sp", t_kr[0], t_kr[1])
        E.raw("sp", kwr)
        E.cnt["dmas"] += 1
        E.last_wait[("sp", "dmas")] = 0
        E.wait("sp", "dmas", E.cnt["dmas"])
        kstg_free[oc % 2] = ("dmas", E.cnt["dmas"])
    t_qlat_done = E.tok("act")
    for oc in range(8):
        slot = oc % 2
        wget, wtok = load_slab(C, C.wio["L3_Wqr"], 0, 1024, oc * 128, 128, slot)
        last = None
        for q2 in range(2):
            last = mm_evac(
                C, [(wget(k, 0, 128), qlat[:, k * TOK + q2 * 512:k * TOK + (q2 + 1) * 512])
                    for k in range(8)], 128, 512,
                lambda a, bk, oc=oc, q2=q2: a.activation(
                    qpe[0:128, oc * TOK + q2 * 512:oc * TOK + (q2 + 1) * 512], bk, AF.Identity),
                [t_qlat_done, wtok])
        C.wslab_busy[slot] = last
    t_qrope8 = [None] * 8
    for oc in range(8):
        t_qrope8[oc] = emit_rope_half(C, C.region(43008 + oc * TOK, TOK), 0,
                                      "rt64", 64, TOK, [E.tok("act"), t_ropes])

    t_exch = exchange_fm_buf(C, latb, 4, C.shared["lat3"], [E.tok("act")])
    dep_lat = t_exch

    kh_free = None
    vfm_free = None
    vtok_free = None
    qh_free = None
    pb_free = [None] * 4
    t_last_norm = None
    mpos = {}
    mi = 0
    for (q0, s0) in MASKED_POS:
        for sub in range(4):
            mpos[(q0, s0, sub)] = mi
            mi += 1
    for h in range(H):
        slot = h % 2
        ch = 4 + h // 2
        r1 = (h % 2) * 32
        def kpe_dma(e, ch=ch, r1=r1):
            sh = C.shared["lat3"].ap()
            src1 = sh[ch * 128 + r1:ch * 128 + r1 + 32, :]
            src2 = sh[ch * 128 + 64 + r1:ch * 128 + 64 + r1 + 32, :]
            sem = C.E.sem_objs["dmas"]
            regs = C.E.regs
            with e.If_eq(regs["parity"], 0):
                e.dma_start(out=kh[0:32, 0:TOK], in_=src1[:, 0:TOK]).then_inc(sem, 16)
                e.dma_start(out=kh[0:32, TOK:S], in_=src1[:, TOK:S]).then_inc(sem, 16)
                e.dma_start(out=kh[32:64, 0:TOK], in_=src2[:, 0:TOK]).then_inc(sem, 16)
                e.dma_start(out=kh[32:64, TOK:S], in_=src2[:, TOK:S]).then_inc(sem, 16)
            with e.Else():
                e.dma_start(out=kh[0:32, 0:TOK], in_=src1[:, TOK:S]).then_inc(sem, 16)
                e.dma_start(out=kh[0:32, TOK:S], in_=src1[:, 0:TOK]).then_inc(sem, 16)
                e.dma_start(out=kh[32:64, 0:TOK], in_=src2[:, TOK:S]).then_inc(sem, 16)
                e.dma_start(out=kh[32:64, TOK:S], in_=src2[:, 0:TOK]).then_inc(sem, 16)
        E.wait("sp", t_exch[0], t_exch[1])
        if kh_free is not None:
            E.wait("sp", kh_free[0], kh_free[1])
        E.raw("sp", kpe_dma)
        E.cnt["dmas"] += 4
        E.last_wait[("sp", "dmas")] = 0
        E.wait("sp", "dmas", E.cnt["dmas"])
        t_kpe = ("dmas", E.cnt["dmas"])
        wku, tku = load_slab(C, C.wio["L3_Wku"], 0, 512, h * 64, 64, slot)
        ws, base, pitch = C.wslab_base[slot]
        def ld_v(e, base=base, h=h, ws=ws, pitch=pitch):
            src_ap = C.wio["L3_Wvu"][0:512, h * 128:(h + 1) * 128].rearrange("(k p) o -> p k o", p=128)
            return e.dma_start(out=bass.AP(ws, base + 256, [[pitch, 128], [128, 4], [1, 128]]), in_=src_ap)
        tv = E.op("pool", ld_v, sig="dmaw")
        def ld_q(e, base=base, h=h, ws=ws, pitch=pitch):
            src_ap = C.wio["L3_Wqu"][0:1024, h * 64:(h + 1) * 64].rearrange("(k p) o -> p k o", p=128)
            return e.dma_start(out=bass.AP(ws, base + 768, [[pitch, 128], [64, 8], [1, 64]]), in_=src_ap)
        tq = E.op("pool", ld_q, sig="dmaw")
        t_kn = None
        for kg in range(4):
            t_kn = mm_evac(
                C, [(ws[:, base + k * 64:base + (k + 1) * 64],
                     latb[:, k * S + kg * 512:k * S + (kg + 1) * 512]) for k in range(4)],
                64, 512,
                lambda a, bk, kg=kg: a.activation(
                    kh.h[64:128, kh.col0 + kg * 512:kh.col0 + (kg + 1) * 512], bk, AF.Identity),
                [tku, dep_lat, kh_free])
        t_vf = None
        for kg in range(4):
            t_vf = mm_evac(
                C, [(ws[:, base + 256 + k * 128:base + 256 + (k + 1) * 128],
                     latb[:, k * S + kg * 512:k * S + (kg + 1) * 512]) for k in range(4)],
                128, 512,
                lambda a, bk, kg=kg: a.activation(
                    vfm[0:128, kg * 512:(kg + 1) * 512], bk, AF.Identity),
                [tv, dep_lat, vfm_free])
        t_vt = None
        for g in range(2):
            t_pe = E.op("pe", lambda t, g=g: [
                t.transpose(C.ptr[:, j * 128:(j + 1) * 128],
                            vfm[0:128, (g * 8 + j) * 128:(g * 8 + j + 1) * 128],
                            C.sb["ident"][:, :]) for j in range(8)][-1],
                sig="pe", waits=[t_vf, vtok_free, t_vt, C.ptr_free])
            t_vt = E.op("dve", lambda v, g=g: v.tensor_copy(
                vtok[0:128, g * 1024:(g + 1) * 1024], C.ptr[:, 0:1024]),
                sig="dve", waits=[t_pe])
            C.ptr_free = t_vt
        def qcp(v, h=h):
            ch2 = h // 2
            r2 = (h % 2) * 32
            v.tensor_copy(qh[0:32, 0:TOK], qpe[r2:r2 + 32, ch2 * TOK:ch2 * TOK + TOK])
            if h % 2 == 0:
                return v.tensor_copy(qh[32:64, 0:TOK], qpe[64:96, ch2 * TOK:ch2 * TOK + TOK])
            v.tensor_copy(qscr[0:64, 0:TOK], qpe[64:128, ch2 * TOK:ch2 * TOK + TOK])
            return v.tensor_copy(qh[32:64, 0:TOK], qscr[32:64, 0:TOK])
        t_qcp = E.op("dve", qcp, sig="dve", waits=[t_qrope8[h // 2], qh_free])
        t_qn = None
        for q2 in range(2):
            t_qn = mm_evac(
                C, [(ws[:, base + 768 + k * 64:base + 768 + (k + 1) * 64],
                     qlat[:, k * TOK + q2 * 512:k * TOK + (q2 + 1) * 512]) for k in range(8)],
                64, 512,
                lambda a, bk, q2=q2: a.activation(
                    qh.h[64:128, qh.col0 + q2 * 512:qh.col0 + (q2 + 1) * 512], bk, AF.Identity),
                [tq, qh_free, t_qcp])
        C.wslab_busy[slot] = t_qn
        for qc in range(2):
            tiles = [(sl, sub) for sl in SCHED[qc] for sub in range(4)]
            nt = len(tiles)
            pend = None
            for ti, (sl, sub) in enumerate(tiles):
                koff = sl * 512 + sub * 128
                sbank, sfree, sbi = C.take_bank()
                t_s = E.op("pe", lambda t, koff=koff, qc=qc, sbank=sbank: t.matmul(
                    sbank[0:128, 0:512], kh[0:128, koff:koff + 128],
                    qh[0:128, qc * 512:(qc + 1) * 512],
                    start=True, stop=True, skip_group_check=True),
                    sig="pe", waits=[sfree, t_kn, t_vt, t_qn, t_kpe])
                ps = ti % 4
                t_exp = E.op("act", lambda a, sbank=sbank, ps=ps: a.activation(
                    pbuf[0:128, ps * 512:(ps + 1) * 512], sbank[0:128, 0:512],
                    AF.Exp, scale=scale), sig="act", waits=[t_s, pb_free[ps]])
                C.release_bank(sbi, t_exp)
                t_pready = t_exp
                if (qc, sl, sub) in mpos:
                    t_pready = E.op("dve", lambda v, ps=ps, m=mpos[(qc, sl, sub)]: v.tensor_tensor(
                        pbuf[0:128, ps * 512:(ps + 1) * 512],
                        pbuf[0:128, ps * 512:(ps + 1) * 512],
                        C.sb["masks"][:, m * 512:(m + 1) * 512], ALU.mult),
                        sig="dve", waits=[t_exp])
                if pend is not None:
                    pb_free[pend[0]] = emit_zav(C, pbuf, vtok, pend[0], pend[1], pend[2],
                                                pend[3], nt)
                pend = (ps, ti, koff, t_pready)
            pb_free[pend[0]] = emit_zav(C, pbuf, vtok, pend[0], pend[1], pend[2],
                                        pend[3], nt)
            t_last_norm = emit_znorm(C, obuf, zbs, h, qc)
        kh_free = E.tok("pe")
        vtok_free = E.tok("pe")
        qh_free = E.tok("pe")
    emit_out_ln2(C, obuf, C.wio["L3_Wo"], [t_last_norm], obuf_col0=34816)


def emit_l3_ffn(C):
    E = C.E
    hid = 5632
    nic = hid // 128
    xhat = C.region(0, 16 * TOK)
    x2 = C.region(16384, nic * 512)
    yfm = C.region(8192, 16 * 512)
    wg_ap = C.wio["L3_wg"]
    wu_ap = C.wio["L3_wu"]
    wd_ap = C.wio["L3_wd"]
    E.op("pool", lambda e: e.dma_start(out=C.sb["vecs"][:, 80:124],
                                       in_=C.inputs_ap["L3_sg"][:, :]), sig="dmaw")
    E.op("pool", lambda e: e.dma_start(out=C.sb["vecs"][:, 124:168],
                                       in_=C.inputs_ap["L3_su"][:, :]), sig="dmaw")
    t_v = E.op("pool", lambda e: e.dma_start(out=C.sb["vecs"][:, 168:184],
                                             in_=C.inputs_ap["L3_sd"][:, :]), sig="dmaw")
    C.ffn_slot_act = [None, None]
    for qc in range(2):
        scr_free = [E.tok("dve")] * 4
        t_xhat = emit_ln_to_fm(C, xhat, waits=[E.tok("act"), E.tok("pe")])
        x2_done = None
        for hc in range(nic):
            wg, tg = load_slab(C, wg_ap, 0, D, hc * 128, 128, 0)
            wu, tu = load_slab(C, wu_ap, 0, D, hc * 128, 128, 1)
            t_s0 = E.op("dve", lambda v: v.tensor_scalar(
                C.sb["wslab"][:, 0:2048], C.sb["wslab"][:, 0:2048], -128.0, None, ALU.add),
                sig="dve", waits=[tg])
            t_s1 = E.op("dve", lambda v: v.tensor_scalar(
                C.sb["wslab"][:, 2048:4096], C.sb["wslab"][:, 2048:4096], -128.0, None, ALU.add),
                sig="dve", waits=[tu])
            bankG, gfree, gi = C.take_bank()
            t_g = E.op("pe", lambda t, wg=wg, qc=qc, bankG=bankG: [
                t.matmul(bankG[0:128, 0:512], wg(k, 0, 128),
                         xhat[:, k * TOK + qc * 512:k * TOK + (qc + 1) * 512],
                         start=(k == 0), stop=(k == NCH - 1), skip_group_check=True)
                for k in range(NCH)][-1], sig="pe", waits=[t_xhat, t_s0, gfree])
            ssl = (hc % 2) * 2
            t_sg = E.op("act", lambda a, bankG=bankG, ssl=ssl, hc=hc: a.activation(
                C.sb["ytok2"][0:128, ssl * 512:(ssl + 1) * 512], bankG[0:128, 0:512],
                AF.Silu, scale=C.sb["vecs"][0:128, 80 + hc:81 + hc]),
                sig="act", waits=[t_g, scr_free[ssl], t_v])
            C.release_bank(gi, t_sg)
            bankU, ufree, ui = C.take_bank()
            t_u = E.op("pe", lambda t, wu=wu, qc=qc, bankU=bankU: [
                t.matmul(bankU[0:128, 0:512], wu(k, 0, 128),
                         xhat[:, k * TOK + qc * 512:k * TOK + (qc + 1) * 512],
                         start=(k == 0), stop=(k == NCH - 1), skip_group_check=True)
                for k in range(NCH)][-1], sig="pe", waits=[t_xhat, t_s1, ufree])
            t_su = E.op("act", lambda a, bankU=bankU, ssl=ssl, hc=hc: a.activation(
                C.sb["ytok2"][0:128, (ssl + 1) * 512:(ssl + 2) * 512],
                bankU[0:128, 0:512], AF.Identity,
                scale=C.sb["vecs"][0:128, 124 + hc:125 + hc]),
                sig="act", waits=[t_u, scr_free[ssl + 1], t_v])
            x2_done = E.op("dve", lambda v, hc=hc, ssl=ssl: v.tensor_tensor(
                x2[0:128, hc * 512:(hc + 1) * 512],
                C.sb["ytok2"][0:128, ssl * 512:(ssl + 1) * 512],
                C.sb["ytok2"][0:128, (ssl + 1) * 512:(ssl + 2) * 512], ALU.mult),
                sig="dve", waits=[t_su, t_sg])
            C.release_bank(ui, x2_done)
            scr_free[ssl] = x2_done
            scr_free[ssl + 1] = x2_done
            C.wslab_busy[0] = t_g
            C.wslab_busy[1] = t_u
        yfm_done = [None] * 16
        C.wslab_busy["big"] = E.tok("pe")
        for oc in range(NCH):
            wgb, tgb = load_slab(C, wd_ap, 0, hid, oc * 128, 128, "big")
            t_sb = E.op("dve", lambda v: v.tensor_scalar(
                C.arena[0:128, 0:nic * 128], C.arena[0:128, 0:nic * 128], -128.0, None, ALU.add),
                sig="dve", waits=[tgb])
            yfm_done[oc] = mm_evac(
                C, [(C.arena[0:128, k * 128:(k + 1) * 128], x2[:, k * 512:(k + 1) * 512])
                    for k in range(nic)], 128, 512,
                lambda a, bk, oc=oc: a.activation(
                    yfm[0:128, oc * 512:(oc + 1) * 512], bk, AF.Identity,
                    scale=C.sb["vecs"][0:128, 168 + oc:169 + oc]),
                [x2_done, t_sb, t_v], evac_eng="act")
            C.wslab_busy["big"] = yfm_done[oc]
        last_ev = E.tok("pe")
        for c4 in range(4):
            c = qc * 4 + c4
            for g in range(4):
                t_pe = E.op("pe", lambda t, g=g, c4=c4: [
                    t.transpose(C.ptr[:, j * 128:(j + 1) * 128],
                                yfm[0:128, (g * 4 + j) * 512 + c4 * 128:(g * 4 + j) * 512 + c4 * 128 + 128],
                                C.sb["ident"][:, :]) for j in range(4)][-1],
                    sig="pe", waits=[yfm_done[g * 4 + 3], last_ev, C.ptr_free])
                last_ev = E.op("dve", lambda v, g=g: v.tensor_copy(
                    bass.AP(C.sb["ytok2"], g * 4 * 128, [[D, 128], [128, 4], [1, 128]]),
                    C.ptr[:, 0:512].rearrange("p (a b) -> p a b", a=4)),
                    sig="dve", waits=[t_pe])
                C.ptr_free = last_ev
            t = emit_ln2_tile(C, c, C.sb["ytok2"][0:128, 0:D], last_ev, C.ffn_slot_act)
            last_ev = t
    return E.tok("act")



def emit_l1_attn(C):
    E = C.E
    H = 16
    xhat = C.region(0, 16 * TOK)
    obuf = C.region(16384, 16 * TOK)
    mbuf = C.region(32768, 8192)
    hs = 40960
    qhb = C.region(hs, TOK)
    khb = C.region(hs + TOK, TOK)
    ktil = C.region(hs + 2 * TOK, TOK)
    ktok = C.region(hs + 3 * TOK, TOK)
    vfmb = C.region(hs + 4 * TOK, TOK)
    vtok = C.region(hs + 5 * TOK, TOK)
    sgb = C.region(hs + 6 * TOK, TOK)
    gam = C.region(hs + 7 * TOK, TOK)
    pbuf = C.region(49152, 4 * 512)
    sq2 = C.region(49152 + 2048, 512)
    rstdb = C.region(49152 + 2048 + 512, 512)
    mcomb = C.region(49152 + 2048 + 1024, 128)

    t_ropes = E.op("pool", lambda e: e.dma_start(
        out=C.sb["ropes"][:, :], in_=C.inputs_ap["ropes_in"][:, :]),
        sig="dmaw", waits=[(n, E.cnt[n]) for n in ("dve",)])
    t_cv = E.op("pool", lambda e: e.dma_start(
        out=C.sb["cdec"][:, :], in_=C.inputs_ap["cdec_in"][:, :]), sig="dmaw")
    t_om = E.op("pool", lambda e: e.dma_start(
        out=C.sb["omg"][:, :], in_=C.inputs_ap["omg_in"][:, :]), sig="dmaw")
    t_gn = E.op("pool", lambda e: e.dma_start(
        out=C.sb["gnw"][0:1, 0:128], in_=C.inputs_ap["gnw_in"][:, :]), sig="dmaw")

    t_xhat = emit_ln_to_fm(C, xhat, waits=[E.tok("act"), E.tok("pe")])

    wq = C.wio["L1_Wq"]
    wk = C.wio["L1_Wk"]
    wv = C.wio["L1_Wv"]
    wg = C.wio["L1_Wg"]
    scale = float(1.0 / np.sqrt(128.0))

    kh_free = None
    kh_free_d = None
    t_last = None
    pb_free = [None] * 4
    ob_free = None
    C.wslab_busy["a3"] = E.tok("pe")
    for h in range(H):
        slot = h % 2
        t_gam = E.op("pool", lambda e, h=h: e.dma_start(
            out=gam[0:128, 0:TOK], in_=C.inputs_ap["gam_in"][:, h * TOK:(h + 1) * TOK]),
            sig="dmaw", waits=[kh_free, kh_free_d])
        wgk, tk = load_slab(C, wk, 0, D, h * 128, 128, slot)
        t_k = None
        for q2 in range(2):
            t_k = mm_evac(
                C, [(wgk(k, 0, 128), xhat[:, k * TOK + q2 * 512:k * TOK + (q2 + 1) * 512])
                    for k in range(NCH)], 128, 512,
                lambda a, bk, q2=q2: a.activation(
                    khb[0:128, q2 * 512:(q2 + 1) * 512], bk, AF.Identity),
                [t_xhat, tk, kh_free, kh_free_d])
        C.wslab_busy[slot] = t_k
        t_kr = emit_rope_half(C, khb, 0, "rt64", 64, TOK, [t_k, t_ropes])
        t_kt = E.op("dve", lambda v: (
            v.tensor_tensor(ktil[0:128, 0:512], khb[0:128, 0:512], gam[0:128, 512:1024], ALU.mult),
            v.tensor_tensor(ktil[0:128, 512:1024], khb[0:128, 512:1024], gam[0:128, 512:1024], ALU.mult))[-1],
            sig="dve", waits=[t_kr, t_gam, kh_free, kh_free_d])
        t_tr = None
        for g in range(2):
            t_pe = E.op("pe", lambda t, g=g: [
                t.transpose(C.ptr[:, j * 128:(j + 1) * 128],
                            ktil[0:128, (g * 4 + j) * 128:(g * 4 + j + 1) * 128],
                            C.sb["ident"][:, :]) for j in range(4)][-1],
                sig="pe", waits=[t_kt, t_tr, kh_free, C.ptr_free])
            t_tr = E.op("dve", lambda v, g=g: v.tensor_copy(
                ktok[0:128, g * 512:(g + 1) * 512], C.ptr[:, 0:512]),
                sig="dve", waits=[t_pe])
            C.ptr_free = t_tr
        wgv, tv = load_slab(C, wv, 0, D, h * 128, 128, "a3")
        t_v = None
        for q2 in range(2):
            t_v = mm_evac(
                C, [(wgv(k, 0, 128), xhat[:, k * TOK + q2 * 512:k * TOK + (q2 + 1) * 512])
                    for k in range(NCH)], 128, 512,
                lambda a, bk, q2=q2: a.activation(
                    vfmb[0:128, q2 * 512:(q2 + 1) * 512], bk, AF.Identity),
                [t_xhat, tv, kh_free, kh_free_d])
        C.wslab_busy["a3"] = t_v
        t_trv = None
        for g in range(2):
            t_pe = E.op("pe", lambda t, g=g: [
                t.transpose(C.ptr[:, j * 128:(j + 1) * 128],
                            vfmb[0:128, (g * 4 + j) * 128:(g * 4 + j + 1) * 128],
                            C.sb["ident"][:, :]) for j in range(4)][-1],
                sig="pe", waits=[t_v, t_trv, kh_free, C.ptr_free])
            t_trv = E.op("dve", lambda v, g=g: v.tensor_copy(
                vtok[0:128, g * 512:(g + 1) * 512], C.ptr[:, 0:512]),
                sig="dve", waits=[t_pe])
            C.ptr_free = t_trv
        for i in range(2):
            bank, bfree, bi = C.take_bank()
            t_m = E.op("pe", lambda t, i=i, bank=bank: [
                t.matmul(bank[0:128, 0:128],
                         ktok[0:128, i * 512 + j * 128:i * 512 + (j + 1) * 128],
                         vtok[0:128, i * 512 + j * 128:i * 512 + (j + 1) * 128],
                         start=(j == 0), stop=(j == 3), skip_group_check=True)
                for j in range(4)][-1], sig="pe", waits=[t_tr, t_trv, bfree])
            t_mev = E.op("dve", lambda v, i=i, h=h, bank=bank: v.tensor_copy(
                mbuf[0:128, (h * 2 + i) * 128:(h * 2 + i + 1) * 128], bank[0:128, 0:128]),
                sig="dve", waits=[t_m])
            C.release_bank(bi, t_mev)
        fid = C.n_flags
        C.n_flags += 2
        E.wait("sp", "dve", E.cnt["dve"])

        def mwr(e, h=h):
            sem = C.E.sem_objs["dmas"]
            regs = C.E.regs
            src_ap = mbuf[0:128, h * 256:(h + 1) * 256]
            with e.If_eq(regs["parity"], 0):
                e.dma_start(out=C.shared["lat1"].ap()[0:128, h * 256:(h + 1) * 256],
                            in_=src_ap).then_inc(sem, 16)
            with e.Else():
                e.dma_start(out=C.shared["lat1"].ap()[0:128, 4096 + h * 256:4096 + (h + 1) * 256],
                            in_=src_ap).then_inc(sem, 16)
        E.raw("sp", mwr)
        E.cnt["dmas"] += 1
        E.last_wait[("sp", "dmas")] = 0
        E.wait("sp", "dmas", E.cnt["dmas"])

        def mflag(e, fid=fid):
            sem = C.E.sem_objs["dmas"]
            regs = C.E.regs
            with e.If_eq(regs["parity"], 0):
                e.dma_start(out=C.shf.ap()[0:1, fid:fid + 1], in_=C.sb["meta"][0:1, 1:2]).then_inc(sem, 16)
            with e.Else():
                e.dma_start(out=C.shf.ap()[0:1, fid + 1:fid + 2], in_=C.sb["meta"][0:1, 1:2]).then_inc(sem, 16)
        E.raw("sp", mflag)
        E.cnt["dmas"] += 1
        E.last_wait[("sp", "dmas")] = 0
        E.wait("sp", "dmas", E.cnt["dmas"])

        wgq, tq = load_slab(C, wq, 0, D, h * 128, 128, slot)
        t_q = None
        for q2 in range(2):
            t_q = mm_evac(
                C, [(wgq(k, 0, 128), xhat[:, k * TOK + q2 * 512:k * TOK + (q2 + 1) * 512])
                    for k in range(NCH)], 128, 512,
                lambda a, bk, q2=q2: a.activation(
                    qhb[0:128, q2 * 512:(q2 + 1) * 512], bk, AF.Identity),
                [t_xhat, tq, kh_free, kh_free_d])
        t_qr = emit_rope_half(C, qhb, 0, "rt64", 64, TOK, [t_q, t_ropes])
        t_qg = E.op("dve", lambda v: (
            v.tensor_tensor(qhb[0:128, 0:512], qhb[0:128, 0:512], gam[0:128, 0:512], ALU.mult),
            v.tensor_tensor(qhb[0:128, 512:1024], qhb[0:128, 512:1024], gam[0:128, 0:512], ALU.mult))[-1],
            sig="dve", waits=[t_qr, t_gam])
        wgg, tg = load_slab(C, wg, 0, D, h * 128, 128, "a3")
        t_g = None
        for q2 in range(2):
            t_g = mm_evac(
                C, [(wgg(k, 0, 128), xhat[:, k * TOK + q2 * 512:k * TOK + (q2 + 1) * 512])
                    for k in range(NCH)], 128, 512,
                lambda a, bk, q2=q2: a.activation(
                    sgb[0:128, q2 * 512:(q2 + 1) * 512], bk, AF.Silu),
                [t_xhat, tg, kh_free, kh_free_d])
        C.wslab_busy[slot] = t_q
        C.wslab_busy["a3"] = t_g

        def mpoll(e, fid=fid):
            regs = C.E.regs
            fv, tgt, cc = regs["fv"], regs["tgt"], regs["cc"]
            e.reg_mov(cc, 1)
            with e.While(cc):
                with e.If_eq(regs["parity"], 0):
                    e.dma_start(out=C.sb["flags"][0:1, 0:1],
                                in_=C.shf.ap()[0:1, fid + 1:fid + 2]).then_inc(C.E.sem_objs["poll"], 16)
                with e.Else():
                    e.dma_start(out=C.sb["flags"][0:1, 0:1],
                                in_=C.shf.ap()[0:1, fid:fid + 1]).then_inc(C.E.sem_objs["poll"], 16)
                e.reg_add(tgt, tgt, 16)
                e.wait_ge(C.E.sem_objs["poll"], tgt)
                e.reg_load(fv, C.sb["flags"][0:1, 0:1])
                e.reg_alu(cc, fv, regs["nonce"], ALU.not_equal)
        E.raw("sp", mpoll)

        def mrd(e, h=h):
            sem = C.E.sem_objs["dmas"]
            regs = C.E.regs
            dst = mbuf[0:128, 4096 + h * 256:4096 + (h + 1) * 256]
            with e.If_eq(regs["parity"], 0):
                e.dma_start(out=dst, in_=C.shared["lat1"].ap()[0:128, 4096 + h * 256:4096 + (h + 1) * 256]).then_inc(sem, 16)
            with e.Else():
                e.dma_start(out=dst, in_=C.shared["lat1"].ap()[0:128, h * 256:(h + 1) * 256]).then_inc(sem, 16)
        E.raw("sp", mrd)
        E.cnt["dmas"] += 1
        E.last_wait[("sp", "dmas")] = 0
        E.wait("sp", "dmas", E.cnt["dmas"])
        t_mprt = ("dmas", E.cnt["dmas"])

        for qc in range(2):
            t_mc = E.op("dve", lambda v, h=h, qc=qc: (
                v.tensor_scalar(mcomb[0:128, 0:128],
                                mbuf[0:128, 4096 + h * 256:4096 + h * 256 + 128],
                                C.sb["omg"][0:128, h * 8 + qc * 4 + 2:h * 8 + qc * 4 + 3],
                                None, ALU.mult),
                v.scalar_tensor_tensor(mcomb[0:128, 0:128],
                                       mbuf[0:128, 4096 + h * 256 + 128:4096 + (h + 1) * 256],
                                       C.sb["omg"][0:128, h * 8 + qc * 4 + 3:h * 8 + qc * 4 + 4],
                                       mcomb[0:128, 0:128], ALU.mult, ALU.add))[-1],
                sig="dve", waits=[t_mprt, t_om, ob_free])
            tiles = ([(0, j) for j in range(4)] if qc == 0
                     else [(0, j) for j in range(4)] + [(1, j) for j in range(4)])
            nt = len(tiles)
            pend = None
            for ti, (sl, sub) in enumerate(tiles):
                koff = sl * 512 + sub * 128
                cidx = h * 12 + (qc * 4 if qc == 0 else 4) + ti
                sbank, sfree, sbi = C.take_bank()
                t_s = E.op("pe", lambda t, koff=koff, qc=qc, sbank=sbank: t.matmul(
                    sbank[0:128, 0:512], khb[0:128, koff:koff + 128],
                    qhb[0:128, qc * 512:(qc + 1) * 512],
                    start=True, stop=True, skip_group_check=True),
                    sig="pe", waits=[sfree, t_qg, t_kt])
                ps = ti % 4
                t_exp = E.op("act", lambda a, sbank=sbank, ps=ps, cidx=cidx: a.activation(
                    pbuf[0:128, ps * 512:(ps + 1) * 512], sbank[0:128, 0:512],
                    AF.Identity, scale=C.sb["cdec"][0:128, cidx:cidx + 1]),
                    sig="act", waits=[t_s, pb_free[ps], t_cv])
                C.release_bank(sbi, t_exp)
                t_pready = t_exp
                if (qc == 0 and sl == 0) or (qc == 1 and sl == 1):
                    m = sub if qc == 0 else 8 + sub
                    t_pready = E.op("dve", lambda v, ps=ps, m=m: v.tensor_tensor(
                        pbuf[0:128, ps * 512:(ps + 1) * 512],
                        pbuf[0:128, ps * 512:(ps + 1) * 512],
                        C.sb["masks"][:, m * 512:(m + 1) * 512], ALU.mult),
                        sig="dve", waits=[t_exp])
                if pend is not None:
                    pb_free[pend[0]] = emit_av_l1(C, pbuf, vtok, pend[0], pend[2], pend[3],
                                                  first=(pend[1] == 0), last=False)
                pend = (ps, ti, koff, t_pready)
            pb_free[pend[0]] = emit_av_l1(C, pbuf, vtok, pend[0], pend[2], pend[3],
                                          first=(pend[1] == 0), last=False)
            t_cross = E.op("pe", lambda t, qc=qc: t.matmul(
                C.obank[0:128, 0:512], mcomb[0:128, 0:128],
                qhb[0:128, qc * 512:(qc + 1) * 512],
                start=False, stop=True, skip_group_check=True),
                sig="pe", waits=[t_mc, t_qg])
            t_sq = E.op("act", lambda a: a.activation(
                sq2[0:128, 0:512], C.obank[0:128, 0:512], AF.Square),
                sig="act", waits=[t_cross])
            t_sum = E.op("pe", lambda t: t.matmul(
                C.pz[0:1, 0:512], C.sb["ones"][:, 0:1], sq2[0:128, 0:512],
                start=True, stop=True, skip_group_check=True),
                sig="pe", waits=[t_sq, C.pz_free])
            t_ms = E.op("dve", lambda v: v.tensor_scalar(
                C.sb["zrec"][0:1, 0:512], C.pz[0:1, 0:512], 1.0 / 128.0, EPS, ALU.mult, ALU.add),
                sig="dve", waits=[t_sum])
            C.pz_free = t_ms

            def _rs(v):
                with C.nc.allow_low_precision(reason="rms denom bf16; probs bf16"):
                    return v.reciprocal(C.sb["zrec"][0:1, 0:512], C.sb["zrec"][0:1, 0:512])
            t_rc = E.op("dve", _rs, sig="dve", waits=[t_ms])
            t_ln = E.op("act", lambda a: (
                a.activation(C.sb["zrec"][0:1, 0:512], C.sb["zrec"][0:1, 0:512], AF.Ln),
                a.activation(C.sb["zrec"][0:1, 0:512], C.sb["zrec"][0:1, 0:512], AF.Exp, scale=0.5))[-1],
                sig="act", waits=[t_rc])
            bank, bfree, bi = C.take_bank()
            t_bc = E.op("pe", lambda t, bank=bank: t.matmul(
                bank[0:128, 0:512], C.sb["gnw"][0:1, 0:128], C.sb["zrec"][0:1, 0:512],
                start=True, stop=True, skip_group_check=True),
                sig="pe", waits=[t_ln, bfree, t_gn])
            t_rb = E.op("dve", lambda v, bank=bank: v.tensor_copy(
                rstdb[0:128, 0:512], bank[0:128, 0:512]), sig="dve", waits=[t_bc])
            C.release_bank(bi, t_rb)
            t_t1 = E.op("dve", lambda v: v.tensor_tensor(
                sq2[0:128, 0:512], C.obank[0:128, 0:512], rstdb[0:128, 0:512], ALU.mult),
                sig="dve", waits=[t_rb])
            ob_free = t_t1
            t_fin = E.op("dve", lambda v, h=h, qc=qc: v.tensor_tensor(
                obuf[0:128, h * TOK + qc * 512:h * TOK + (qc + 1) * 512],
                sq2[0:128, 0:512], sgb[0:128, qc * 512:(qc + 1) * 512], ALU.mult),
                sig="dve", waits=[t_t1, t_g])
            C.obank_free = t_t1
            t_last = t_fin
        kh_free = E.tok("pe")
        kh_free_d = E.tok("dve")
    emit_out_ln2(C, obuf, C.wio["L1_Wo"], [t_last], obuf_col0=16384, ytok_col0=32768)


def emit_av_l1(C, pbuf, vtok, ps, koff, t_pready, first, last):
    E = C.E
    waits = [t_pready]
    if first:
        waits.append(C.obank_free)
    kt = koff // 128
    t = E.op("pe", lambda t, ps=ps, first=first, kt=kt: t.matmul(
        C.obank[0:128, 0:512], vtok[0:128, kt * 128:(kt + 1) * 128],
        pbuf[0:128, ps * 512:(ps + 1) * 512],
        start=first, stop=False, skip_group_check=True),
        sig="pe", waits=waits)
    if first:
        C.obank_free = None
    return t


def emit_l1_ffn(C):
    E = C.E
    xhat = C.region(0, 16 * TOK)
    ytok = C.region(16384, 8 * D)
    xab = C.region(24576, 512)
    wtl = C.region(25088, 512)
    wmT = C.region(25600, 512)
    l1b = C.region(26112, D)
    lgt = C.region(28160, 512)

    t_gb = E.op("pool", lambda e: e.dma_start(
        out=C.sb["vecs"][0:8, 184:185], in_=C.inputs_ap["L1_gb"][:, :]), sig="dmaw")
    t_rep = E.op("pool", lambda e: e.dma_start(
        out=C.sb["gnw"][0:8, 128:256], in_=C.inputs_ap["rep_in"][:, :]), sig="dmaw")

    C.ffn_slot_act = [None, None]
    C.part_guards = [E.tok("pe"), E.tok("dve"), E.tok("act"), E.tok("dmas")]
    for qc in range(2):
        t_xhat = emit_ln_to_fm(C, xhat, waits=[E.tok("act"), E.tok("pe")])
        wgt, tgt_ = load_slab(C, C.wio["L1_gw"], 0, D, 0, 8, 0)
        t_lg = mm_evac(
            C, [(wgt(k, 0, 8), xhat[:, k * TOK + qc * 512:k * TOK + (qc + 1) * 512])
                for k in range(NCH)], 8, 512,
            lambda a, bk: a.activation(lgt[0:8, 0:512], bk, AF.Identity,
                                       bias=C.sb["vecs"][0:8, 184:185]),
            [t_xhat, tgt_, t_gb], evac_eng="act")
        C.wslab_busy[0] = t_lg
        wga, ta = load_slab(C, C.wio["L1_A"], 0, D, 0, 128, 1)
        t_xa = mm_evac(
            C, [(wga(k, 0, 128), xhat[:, k * TOK + qc * 512:k * TOK + (qc + 1) * 512])
                for k in range(NCH)], 128, 512,
            lambda a, bk: a.activation(xab[0:128, 0:512], bk, AF.Identity),
            [t_xhat, ta], evac_eng="act")
        C.wslab_busy[1] = t_xa
        st = C.sb["stats"]
        y2 = C.sb["ytok2"]
        t_wm = None
        for c4 in range(4):
            t_tp = E.op("pe", lambda t, c4=c4: t.transpose(
                C.ptr[:, 0:8], lgt[0:8, c4 * 128:(c4 + 1) * 128],
                C.sb["ident"][0:8, 0:8]),
                sig="pe", waits=[t_lg, t_wm, C.ptr_free])
            ta_ = E.op("dve", lambda v, st=st: v.reduce_max(
                st[:, 0:1], C.ptr[0:128, 0:8], axis=AX.X), sig="dve", waits=[t_tp])
            tb_ = E.op("dve", lambda v, st=st: v.tensor_scalar(
                st[:, 8:16], C.ptr[0:128, 0:8], st[:, 0:1], None, ALU.is_equal),
                sig="dve", waits=[ta_])
            tc_ = E.op("dve", lambda v, st=st: v.scalar_tensor_tensor(
                st[:, 16:24], st[:, 8:16], -1e30, C.ptr[0:128, 0:8],
                ALU.mult, ALU.add), sig="dve", waits=[tb_])
            td_ = E.op("dve", lambda v, st=st: v.reduce_max(
                st[:, 1:2], st[:, 16:24], axis=AX.X), sig="dve", waits=[tc_])
            te_ = E.op("dve", lambda v, st=st: v.tensor_tensor(
                st[:, 2:3], st[:, 0:1], st[:, 1:2], ALU.subtract),
                sig="dve", waits=[td_])
            C.ptr_free = tc_
            t_sg2 = E.op("act", lambda a, st=st: a.activation(
                st[:, 3:4], st[:, 2:3], AF.Sigmoid), sig="act", waits=[te_])
            tf_ = E.op("dve", lambda v, st=st: v.tensor_scalar(
                st[:, 4:5], st[:, 3:4], -1.0, 1.0, ALU.mult, ALU.add),
                sig="dve", waits=[t_sg2, te_])
            tg2_ = E.op("dve", lambda v, st=st: v.tensor_scalar(
                st[:, 24:32], st[:, 16:24], st[:, 1:2], None, ALU.is_equal),
                sig="dve", waits=[tf_])
            th_ = E.op("dve", lambda v, st=st: v.tensor_scalar(
                st[:, 8:16], st[:, 8:16], st[:, 3:4], None, ALU.mult),
                sig="dve", waits=[tg2_])
            ti_ = E.op("dve", lambda v, st=st: v.tensor_scalar(
                st[:, 24:32], st[:, 24:32], st[:, 4:5], None, ALU.mult),
                sig="dve", waits=[th_])
            t_d2 = E.op("dve", lambda v, st=st, y2=y2: v.tensor_tensor(
                y2[0:128, 0:8], st[:, 8:16], st[:, 24:32], ALU.add),
                sig="dve", waits=[ti_, t_wm])
            t_tb = E.op("pe", lambda t, y2=y2: t.transpose(
                C.ptr[0:8, 8:136], y2[0:128, 0:8], C.sb["ident"][:, :]),
                sig="pe", waits=[t_d2, C.ptr_free])
            t_wm = E.op("dve", lambda v, c4=c4: v.tensor_copy(
                wmT[0:8, c4 * 128:(c4 + 1) * 128], C.ptr[0:8, 8:136]),
                sig="dve", waits=[t_tb])
            C.ptr_free = t_wm
        bank, bfree, bi = C.take_bank()
        t_wt = E.op("pe", lambda t, bank=bank: t.matmul(
            bank[0:128, 0:512], C.sb["gnw"][0:8, 128:256], wmT[0:8, 0:512],
            start=True, stop=True, skip_group_check=True),
            sig="pe", waits=[t_wm, t_rep, bfree])
        t_wtl = E.op("dve", lambda v, bank=bank: v.tensor_copy(
            wtl[0:128, 0:512], bank[0:128, 0:512]), sig="dve", waits=[t_wt])
        C.release_bank(bi, t_wtl)
        t_u = E.op("dve", lambda v: v.tensor_tensor(
            xab[0:128, 0:512], xab[0:128, 0:512], wtl[0:128, 0:512], ALU.mult),
            sig="dve", waits=[t_wtl, t_xa])
        wgb, tb = load_slab(C, C.wio["L1_B"], 0, 128, 0, 2048, "l1b2",
                            extra=C.part_guards if qc == 0 else ())
        t_bb = E.op("pool", lambda e: e.dma_start(
            out=l1b[0:1, 0:D], in_=C.inputs_ap["L1_b"][:, :]), sig="dmaw")
        C.wslab_busy["l1b2"] = tb
        tile_done = [None] * 4
        for dg in range(4):
            wgw, tw = load_slab(C, C.wio["L1_W"], 0, D, dg * 512, 512, "l1w",
                                extra=C.part_guards if qc == 0 and dg == 0 else ())
            for c4 in range(4):
                pairs = [(xhat[:, k * TOK + qc * 512 + c4 * 128:k * TOK + qc * 512 + c4 * 128 + 128],
                          wgw(k, 0, 512)) for k in range(NCH)]
                pairs.append((xab[0:128, c4 * 128:(c4 + 1) * 128], wgb(0, dg * 512, (dg + 1) * 512)))
                pairs.append((C.sb["ones"][0:1, 0:128], l1b[0:1, dg * 512:(dg + 1) * 512]))
                tile_done[c4] = mm_evac(
                    C, pairs, 128, 512,
                    lambda a, bk, c4=c4, dg=dg: a.activation(
                        ytok[0:128, c4 * D + dg * 512:c4 * D + dg * 512 + 512], bk, AF.Identity),
                    [t_u, tw, tb, t_bb, tile_done[c4]], evac_eng="act")
            C.wslab_busy["l1w"] = tile_done[3]
        for c4 in range(4):
            c = qc * 4 + c4
            t = emit_ln2_tile(C, c, ytok[0:128, c4 * D:(c4 + 1) * D], tile_done[c4], C.ffn_slot_act)
            tile_done[c4] = t
    return E.tok("act")


WSLAB_SMALL = [0, 2048]
WSLAB_W = 4096


def build_program(layers, parts=("attn", "ffn"), dumps=()):
    C = Ctx(layers)
    C.parts = parts
    C.dbg_dumps = list(dumps)
    nc, E = C.nc, C.E

    for s in ("pe", "act", "dve", "pool"):
        E.sem(s, 1)
    E.sem("dmas", 16)
    for i4 in range(4):
        E.sem("dmaw%d" % i4, 16)
    E.sem("poll", 16)

    x_in = C.din("x_in", [TOK, D], F32)
    C.din("meta_in", [1, 4], I32)
    C.din("ident_in", [128, 128], BF16)
    C.din("masks_in", [16 * 128, 512], mybir.dt.uint8)
    C.din("ropes_in", [128, 4 * TOK], BF16)
    y_out = C.dout("y", [TOK, D], F32)
    C.wio = {}
    if 0 in layers:
        C.wio["L0_Wqa"] = C.din("L0_Wqa", [D, 512], F32)
        C.wio["L0_Wqb"] = C.din("L0_Wqb", [512, 3072], F32)
        C.wio["L0_Wkva"] = C.din("L0_Wkva", [D, 576], F32)
        C.wio["L0_Wkvb"] = C.din("L0_Wkvb", [512, 4096], F32)
        C.wio["L0_Wo"] = C.din("L0_Wo", [D, D], F32)
        C.wio["L0_wupq"] = C.din("L0_wupq", [8192, D], I8)
        C.wio["L0_wdnq"] = C.din("L0_wdnq", [D, 8192], I8)
        C.din("L0_wupsc", [8192, 32], F32)
        C.din("L0_wdnsc", [D, 128], F32)
        C.din("L0_bup", [128, 64], F32)
        C.din("L0_bdn", [128, 16], F32)
        C.wio["L0_ua"] = C.din("L0_ua", [D, 16], F32)
        C.wio["L0_ub"] = C.din("L0_ub", [16, 8192], F32)
        C.wio["L0_da"] = C.din("L0_da", [8192, 16], F32)
        C.wio["L0_db"] = C.din("L0_db", [16, D], F32)
    if 1 in layers:
        for nm, shp in (("L1_Wq", [D, D]), ("L1_Wk", [D, D]), ("L1_Wv", [D, D]),
                        ("L1_Wg", [D, D]), ("L1_Wo", [D, D]), ("L1_W", [D, D]),
                        ("L1_A", [D, 128]), ("L1_B", [128, D]), ("L1_gw", [D, 8])):
            C.wio[nm] = C.din(nm, shp, F32)
        C.din("L1_b", [1, D], F32)
        C.din("L1_gb", [8, 1], F32)
        C.din("gnw_in", [1, 128], BF16)
        C.din("rep_in", [8, 128], BF16)
        C.din("gam_in", [128, 16 * TOK], BF16)
        C.din("cdec_in", [128, 192], F32)
        C.din("omg_in", [128, 128], F32)
    if 3 in layers:
        C.wio["L3_Wqd"] = C.din("L3_Wqd", [D, 1024], F32)
        C.wio["L3_Wqu"] = C.din("L3_Wqu", [1024, 1024], F32)
        C.wio["L3_Wqr"] = C.din("L3_Wqr", [1024, 1024], F32)
        C.wio["L3_Wkvd"] = C.din("L3_Wkvd", [D, 512], F32)
        C.wio["L3_Wku"] = C.din("L3_Wku", [512, 1024], F32)
        C.wio["L3_Wkr"] = C.din("L3_Wkr", [D, 1024], F32)
        C.wio["L3_Wvu"] = C.din("L3_Wvu", [512, 2048], F32)
        C.wio["L3_Wo"] = C.din("L3_Wo", [D, D], F32)
        C.wio["L3_wg"] = C.din("L3_wg", [D, 5632], U8)
        C.wio["L3_wu2"] = None
        C.wio["L3_wu"] = C.din("L3_wu", [D, 5632], U8)
        C.wio["L3_wd"] = C.din("L3_wd", [5632, D], U8)
        C.din("L3_sg", [128, 44], F32)
        C.din("L3_su", [128, 44], F32)
        C.din("L3_sd", [128, 16], F32)
        C.din("ropes3_in", [128, 2 * TOK], BF16)
    if 2 in layers:
        C.wio["L2_Wqa"] = C.din("L2_Wqa", [D, 1536], F32)
        C.wio["L2_Wqb"] = C.din("L2_Wqb", [1536, D], F32)
        C.wio["L2_Wkva"] = C.din("L2_Wkva", [D, 576], F32)
        C.wio["L2_Wkvb"] = C.din("L2_Wkvb", [512, 3072], F32)
        C.wio["L2_Wo"] = C.din("L2_Wo", [D, D], F32)
        C.wio["L2_gu"] = C.din("L2_gu", [D, 16384], F32)
        C.wio["L2_dn"] = C.din("L2_dn", [8192, D], F32)
    C.shared = {}
    for lx in layers:
        if lx in (0, 2):
            C.shared[f"lat{lx}"] = nc.dram_tensor(f"sh_lat{lx}", [5 * 128, S], BF16,
                                                  addr_space="Shared")
        elif lx == 1:
            C.shared["lat1"] = nc.dram_tensor("sh_lat1", [128, 8192], BF16,
                                              addr_space="Shared")
        elif lx == 3:
            C.shared["lat3"] = nc.dram_tensor("sh_lat3", [12 * 128, S], BF16,
                                              addr_space="Shared")
    C.shf = nc.dram_tensor("sh_flags", [1, 64], I32, addr_space="Shared")

    C.sbuf("x_tok", [128, 8 * D], F32)
    C.arena = C.sbuf("arena", [128, AW], BF16)
    C.sbuf("wslab", [128, WSLAB_W], BF16)
    C.sbuf("masks", [128, 16 * 512], mybir.dt.uint8)
    C.sbuf("ropes", [128, 4 * TOK], BF16)
    C.scr_tok = C.sbuf("ytok2", [128, 2 * TOK], BF16)
    C.sbuf("stats", [128, 64], F32)
    C.sbuf("zrec", [1, 512], BF16)
    C.sbuf("ones", [128, 128], BF16)
    C.sbuf("ident", [128, 128], BF16)
    C.sbuf("vecs", [128, 192], F32)
    C.sbuf("flags", [1, 2], I32)
    if 1 in layers:
        C.sbuf("gnw", [8, 256], BF16)
        C.sbuf("cdec", [128, 192], F32)
        C.sbuf("omg", [128, 128], F32)

    C.sbuf("meta", [1, 4], I32)

    C.banks = [nc.alloc_psum_tensor(f"pb{i}", [128, 512], F32) for i in range(4)]
    C.bank_free = [None] * 4
    C.bank_rr = 0
    C.obank = nc.alloc_psum_tensor("obank", [128, 512], F32)
    C.pz = nc.alloc_psum_tensor("pz", [128, 512], F32)
    C.ptr = nc.alloc_psum_tensor("ptr", [128, 1024], BF16)
    C.obank_free = None
    C.pz_free = None
    C.ptr_free = None

    ws = C.sb["wslab"]
    C.wslab_base = {0: (ws, WSLAB_SMALL[0], WSLAB_W), 1: (ws, WSLAB_SMALL[1], WSLAB_W),
                    "a2": (C.arena, 14336, AW), "big": (C.arena, 0, AW),
                    "a0": (C.arena, 49152, AW), "a1": (C.arena, 49152 + 2048, AW),
                    "a3": (C.arena, 52352, AW),
                    "l1w": (C.arena, 28672, AW), "l1b2": (C.arena, 36864, AW)}
    C.wslab_busy = {0: None, 1: None, "a2": None, "big": None, "a0": None,
                    "a1": None, "a3": None, "l1w": None, "l1b2": None}

    def alloc_regs(e):
        for r in ("parity", "nonce", "fv", "tgt", "cc"):
            E.regs[r] = e.alloc_register(r)
    E.raw("sp", alloc_regs)
    E.op("sp", lambda e: e.dma_start(out=C.sb["meta"][0:1, 0:4],
                                     in_=C.inputs_ap["meta_in"][0:1, 0:4]), sig="dmas")
    t_meta = E.tok("dmas")

    def load_regs(e):
        e.reg_load(E.regs["parity"], C.sb["meta"][0:1, 0:1])
        e.reg_load(E.regs["nonce"], C.sb["meta"][0:1, 1:2])
        e.reg_mov(E.regs["tgt"], 0)
    E.wait("sp", "dmas", t_meta[1])
    E.raw("sp", load_regs)
    t_x = E.op("sp", lambda e: e.dma_start(
        out=C.sb["x_tok"][:, :], in_=x_in.rearrange("(c p) f -> p c f", p=128)),
        sig="dmas")
    t_id = E.op("pool", lambda e: e.dma_start(
        out=C.sb["ident"][:, :], in_=C.inputs_ap["ident_in"][:, :]), sig="dmaw")
    t_rp = E.op("pool", lambda e: e.dma_start(
        out=C.sb["ropes"][:, :], in_=C.inputs_ap["ropes_in"][:, :]), sig="dmaw")
    t_mk = E.op("pool", lambda e: e.dma_start(
        out=C.sb["masks"][:, :].rearrange("p (t q) -> p t q", t=16),
        in_=C.inputs_ap["masks_in"].rearrange("(t k) q -> k t q", k=128)), sig="dmaw")
    t_ones = E.op("dve", lambda v: v.memset(C.sb["ones"][:, :], 1.0), sig="dve")
    C.boot = [t_meta, t_x, t_id, t_rp, t_mk, t_ones, E.tok("dmas")]

    E.wait("dve", "dmas", t_x[1])
    for tk in (t_id, t_rp, t_mk):
        E.wait("dve", tk[0], tk[1])
        E.wait("pe", tk[0], tk[1])
    E.wait("act", "dmas", t_x[1])

    if "lnonly" in C.parts:
        xhat = C.region(0, 16 * TOK)
        emit_ln_to_fm(C, xhat, waits=[E.tok("act")])
    for lx in layers:
        if lx == 2:
            if "attn" in C.parts:
                emit_mla_attn(C, 2, q_lora=1536, d_nope=64, d_rope=64, d_v=128,
                              wqa_ap=C.wio["L2_Wqa"], wqb_ap=C.wio["L2_Wqb"],
                              wkva_ap=C.wio["L2_Wkva"], wkvb_ap=C.wio["L2_Wkvb"],
                              wo_ap=C.wio["L2_Wo"])
            if "ffn" in C.parts:
                emit_swiglu_ffn(C, C.wio["L2_gu"], C.wio["L2_dn"], inter=8192)
        elif lx == 0:
            if "attn" in C.parts:
                emit_mla_attn(C, 0, q_lora=512, d_nope=128, d_rope=64, d_v=128,
                              wqa_ap=C.wio["L0_Wqa"], wqb_ap=C.wio["L0_Wqb"],
                              wkva_ap=C.wio["L0_Wkva"], wkvb_ap=C.wio["L0_Wkvb"],
                              wo_ap=C.wio["L0_Wo"])
            if "ffn" in C.parts:
                emit_l0_ffn(C)
        elif lx == 1:
            if "attn" in C.parts:
                emit_l1_attn(C)
            if "ffn" in C.parts:
                emit_l1_ffn(C)
        elif lx == 3:
            if "attn" in C.parts:
                emit_l3_attn(C)
            if "ffn" in C.parts:
                emit_l3_ffn(C)
        else:
            raise NotImplementedError(lx)

    E.wait("sp", "act", E.cnt["act"])
    E.wait("sp", "dve", E.cnt["dve"])
    E.wait("sp", "pe", E.cnt["pe"])
    for (dnm, col0, rows, cols) in C.dbg_dumps:
        dap = C.dout("dbg_" + dnm, [rows, cols], BF16)
        E.op("sp", lambda e, dap=dap, col0=col0, rows=rows, cols=cols: e.dma_start(
            out=dap, in_=C.arena[0:rows, col0:col0 + cols]), sig="dmas")
    t_st = E.op("sp", lambda e: e.dma_start(
        out=y_out.rearrange("(c p) f -> p c f", p=128), in_=C.sb["x_tok"][:, :]),
        sig="dmas")
    E.wait("sp", "dmas", t_st[1])

    sem_ctxs = []
    for name in list(E.cnt.keys()):
        h = nc.semaphore(name)
        obj = h.__enter__()
        sem_ctxs.append(h)
        E.sem_objs[name] = obj
    C.sem_ctxs = sem_ctxs
    with nc.Block() as block:
        block.sync(lambda e: [fn(e) for fn in E.code["sp"]])
        block.tensor(lambda e: [fn(e) for fn in E.code["pe"]])
        block.scalar(lambda e: [fn(e) for fn in E.code["act"]])
        block.vector(lambda e: [fn(e) for fn in E.code["dve"]])
        block.gpsimd(lambda e: [fn(e) for fn in E.code["pool"]])
    return C


def _interleave_to_halfsplit_cols(w, col0, d):
    perm = np.concatenate([np.arange(0, d, 2), np.arange(1, d, 2)])
    w = w.copy()
    w[:, col0:col0 + d] = w[:, col0 + perm]
    return w


def _perm_qhead_cols(wqb, H, dh, d_nope, d_rope):
    pe = np.concatenate([d_nope + np.arange(0, d_rope, 2), d_nope + np.arange(1, d_rope, 2)])
    perm1 = np.concatenate([pe, np.arange(d_nope)])
    out = np.empty_like(wqb)
    for h in range(H):
        out[:, h * dh:(h + 1) * dh] = wqb[:, h * dh + perm1]
    return out


def host_prepare(inputs, layers, seed_extra=0):
    rng = np.random.default_rng()
    nonce = int(rng.integers(1 << 20, 1 << 30))
    x0 = np.asarray(inputs["L0_attn_x"], dtype=np.float32)
    per_core_common = {}
    if 0 in layers:
        wqb = _perm_qhead_cols(_cf(inputs["L0_attn_Wqb"]), 16, 192, 128, 64)
        wkva = _cf(inputs["L0_attn_Wkva"])
        pe_perm = np.concatenate([512 + np.arange(0, 64, 2), 512 + np.arange(1, 64, 2)])
        wkva = np.concatenate([wkva[:, :512], wkva[:, pe_perm]], axis=1)
        per_core_common.update({
            "L0_Wqa": _cf(inputs["L0_attn_Wqa"]),
            "L0_Wqb": _cf(wqb),
            "L0_Wkva": _cf(wkva),
            "L0_Wkvb": _cf(inputs["L0_attn_Wkvb"]),
            "L0_Wo": _cf(inputs["L0_attn_Wo"]),
            "L0_wupq": np.ascontiguousarray(np.asarray(inputs["L0_ffn_w_up_q"]).astype(np.int8)),
            "L0_wdnq": np.ascontiguousarray(np.asarray(inputs["L0_ffn_w_down_q"]).astype(np.int8)),
            "L0_wupsc": _cf(inputs["L0_ffn_w_up_scale"]),
            "L0_wdnsc": _cf(inputs["L0_ffn_w_down_scale"]),
            "L0_bup": _cf(np.asarray(inputs["L0_ffn_b_up"]).reshape(64, 128).T),
            "L0_bdn": _cf(np.asarray(inputs["L0_ffn_b_down"]).reshape(16, 128).T),
            "L0_ua": _cf(inputs["L0_ffn_w_up_lora_a"]),
            "L0_ub": _cf(inputs["L0_ffn_w_up_lora_b"]),
            "L0_da": _cf(inputs["L0_ffn_w_down_lora_a"]),
            "L0_db": _cf(inputs["L0_ffn_w_down_lora_b"]),
        })
    if 1 in layers:
        H16 = 16
        la = np.asarray(inputs["L1_ffn_lora_A"], dtype=np.float32)
        lb = np.asarray(inputs["L1_ffn_lora_B"], dtype=np.float32)
        Ahat = np.ascontiguousarray(la.reshape(128, D).T)
        Bhat = np.ascontiguousarray(np.transpose(lb, (0, 2, 1)).reshape(128, D))
        rep = np.zeros((8, 128), np.float32)
        for e8 in range(8):
            rep[e8, e8 * 16:(e8 + 1) * 16] = 1.0
        gammas = 1.0 - 2.0 ** (-5.0 - np.arange(H16, dtype=np.float64))
        gam = np.zeros((128, 16 * TOK), np.float64)
        for h in range(H16):
            gam[:, h * TOK:h * TOK + 512] = gammas[h] ** np.arange(512)[None, :]
            gam[:, h * TOK + 512:(h + 1) * TOK] = gammas[h] ** (511 - np.arange(512))[None, :]
        sc128 = 1.0 / np.sqrt(128.0)
        per_core_common.update({
            "L1_Wq": _cf(inputs["L1_attn_Wq"]), "L1_Wk": _cf(inputs["L1_attn_Wk"]),
            "L1_Wv": _cf(inputs["L1_attn_Wv"]), "L1_Wg": _cf(inputs["L1_attn_Wg"]),
            "L1_Wo": _cf(inputs["L1_attn_Wo"]),
            "L1_W": _cf(np.asarray(inputs["L1_ffn_W"]).T),
            "L1_A": _cf(Ahat), "L1_B": _cf(Bhat),
            "L1_gw": _cf(np.asarray(inputs["L1_ffn_gate_w"]).T),
            "L1_b": _cf(np.asarray(inputs["L1_ffn_b"]).reshape(1, D)),
            "L1_gb": _cf(np.asarray(inputs["L1_ffn_gate_b"]).reshape(8, 1)),
            "gnw_in": _cbf(np.asarray(inputs["L1_attn_g_norm_w"]).reshape(1, 128)),
            "rep_in": _cbf(rep),
            "gam_in": _cbf(gam),
        })
    if 3 in layers:
        pperm = np.concatenate([np.concatenate([
            2 * k * 64 + np.arange(32), (2 * k + 1) * 64 + np.arange(32),
            2 * k * 64 + 32 + np.arange(32), (2 * k + 1) * 64 + 32 + np.arange(32)])
            for k in range(8)])
        per_core_common.update({
            "L3_Wqd": _cf(inputs["L3_attn_Wq_down"]),
            "L3_Wqu": _cf(inputs["L3_attn_Wq_up"]),
            "L3_Wqr": _cf(np.asarray(inputs["L3_attn_Wq_rope"])[:, pperm]),
            "L3_Wkvd": _cf(inputs["L3_attn_Wkv_down"]),
            "L3_Wku": _cf(inputs["L3_attn_Wk_up"]),
            "L3_Wkr": _cf(np.asarray(inputs["L3_attn_Wk_rope"])[:, pperm]),
            "L3_Wvu": _cf(inputs["L3_attn_Wv_up"]),
            "L3_Wo": _cf(inputs["L3_attn_Wo"]),
            "L3_wg": np.ascontiguousarray(np.asarray(inputs["L3_ffn_wq_gate"]).astype(np.uint8).T),
            "L3_wu": np.ascontiguousarray(np.asarray(inputs["L3_ffn_wq_up"]).astype(np.uint8).T),
            "L3_wd": np.ascontiguousarray(np.asarray(inputs["L3_ffn_wq_down"]).astype(np.uint8).T),
            "L3_sg": _cf(np.asarray(inputs["L3_ffn_s_gate"]).reshape(44, 128).T),
            "L3_su": _cf(np.asarray(inputs["L3_ffn_s_up"]).reshape(44, 128).T),
            "L3_sd": _cf(np.asarray(inputs["L3_ffn_s_down"]).reshape(16, 128).T),
        })
    if 2 in layers:
        wqb = _perm_qhead_cols(_cf(inputs["L2_attn_Wqb"]), 16, 128, 64, 64)
        wkva = _cf(inputs["L2_attn_Wkva"])
        pe_perm = np.concatenate([512 + np.arange(0, 64, 2), 512 + np.arange(1, 64, 2)])
        wkva = np.concatenate([wkva[:, :512], wkva[:, pe_perm]], axis=1)
        per_core_common.update({
            "L2_Wqa": _cf(inputs["L2_attn_Wqa"]),
            "L2_Wqb": _cf(wqb),
            "L2_Wkva": _cf(wkva),
            "L2_Wkvb": _cf(inputs["L2_attn_Wkvb"]),
            "L2_Wo": _cf(inputs["L2_attn_Wo"]),
            "L2_gu": _cf(np.ascontiguousarray(np.asarray(inputs["L2_ffn_w_gate_up"]).T)),
            "L2_dn": _cf(np.ascontiguousarray(np.asarray(inputs["L2_ffn_w_down"]).T)),
        })
    ident = np.eye(128, dtype=np.float32).astype(BF)
    in_maps = []
    for core in range(8):
        p, e = divmod(core, 2)
        pos = own_positions(e)
        m = dict(per_core_common)
        m["x_in"] = _cf(x0[p][pos])
        m["meta_in"] = np.array([[e, nonce, 0, 0]], dtype=np.int32)
        m["ident_in"] = ident
        m["masks_in"] = build_masks(e).astype(np.uint8).reshape(16 * 128, 512)
        c32, s32 = rope_tables("interleaved", 32, pos)
        c64, s64 = rope_tables("l1", 64, pos, d_full=128)
        ropes = np.zeros((128, 4 * TOK), np.float32)
        ropes[:, 0:TOK] = np.concatenate([c64, s64], axis=0)
        ropes[:, TOK:2 * TOK] = np.concatenate([s64, c64], axis=0)
        ropes[0:64, 2 * TOK:3 * TOK] = np.concatenate([c32, s32], axis=0)
        ropes[0:64, 3 * TOK:4 * TOK] = np.concatenate([s32, c32], axis=0)
        m["ropes_in"] = ropes.astype(BF)
        if 1 in layers:
            gammas = 1.0 - 2.0 ** (-5.0 - np.arange(16, dtype=np.float64))
            sc = 1.0 / np.sqrt(128.0)
            own = OWN_BLOCKS[e]
            partner = OWN_BLOCKS[1 - e]
            cdec = np.zeros((128, 192), np.float64)
            for h in range(16):
                g = gammas[h]
                pidx = np.arange(128)
                for ti in range(12):
                    if ti < 4:
                        E_ = -(128.0 * ti + pidx)
                    elif ti < 8:
                        E_ = 512.0 * (own[1] - own[0]) - (128.0 * (ti - 4) + pidx)
                    else:
                        E_ = -(128.0 * (ti - 8) + pidx)
                    cdec[:, h * 12 + ti] = (g ** E_) * sc
            m["cdec_in"] = _cf(cdec)
            omg = np.zeros((128, 128), np.float64)
            blocks = [own[0], own[1], partner[0], partner[1]]
            for h in range(16):
                g = gammas[h]
                for qc in range(2):
                    qb = own[qc]
                    for i in (2, 3):
                        ib = blocks[i]
                        if ib < qb:
                            omg[:, h * 8 + qc * 4 + i] = (g ** (512.0 * (qb - ib) - 511.0)) * sc
            m["omg_in"] = _cf(omg)
        if 3 in layers:
            c3, s3 = rope_tables("l3", 32, pos)
            r3 = np.zeros((128, 2 * TOK), np.float32)
            r3[:, 0:TOK] = np.concatenate([c3, c3, s3, s3], axis=0)
            r3[:, TOK:2 * TOK] = np.concatenate([s3, s3, c3, c3], axis=0)
            m["ropes3_in"] = r3.astype(BF)
        in_maps.append(m)
    return in_maps


def run_layers(inputs, layers, parts=("attn", "ffn"), dumps=(), ret_raw=False):
    in_maps = host_prepare(inputs, layers)
    C = build_program(layers, parts=parts, dumps=dumps)
    res = run_bass_kernel_spmd(C.nc, in_maps, core_ids=list(range(8)))
    out = np.zeros((4, S, D), dtype=np.float32)
    for core in range(8):
        p, e = divmod(core, 2)
        pos = own_positions(e)
        out[p][pos] = res.results[core]["y"]
    if ret_raw:
        return out, res
    return out


def kernel(**inputs):
    return run_layers(inputs, [0, 1, 2, 3])
